# version 04
# baseline (speedup 1.0000x reference)
import numpy as np
import ml_dtypes
import concourse.bass as bass
import concourse.mybir as mybir
from concourse.ap import AP
from concourse.bass_utils import run_bass_kernel_spmd

F32 = mybir.dt.float32
BF16 = mybir.dt.bfloat16
I32 = mybir.dt.int32
I8 = mybir.dt.int8
ALU = mybir.AluOpType
ACT = mybir.ActivationFunctionType
AX = mybir.AxisListType.X

B, T, D = 4, 2048, 2048
TOK = 1024
NTOKC = TOK // 128
NDIN = D // 128
HG, DK, DV, CH = 4, 512, 256, 64
NCH = TOK // CH
NDKC = DK // 128
VD = HG * DV
NVDC = VD // 128
SCALE_GLA = float(DK) ** -0.5
S16 = 1.0 / 16.0
EPS = 1e-5
FI = 8192
HB, KVH, DH = 16, 4, 128
SCALE_B = float(1.0 / np.sqrt(np.float64(DH)))
NEG = -1e30
MM = 4096.0

DT_BYTES = {F32: 4, BF16: 2, I32: 4, I8: 1}


class SBuf:
    def __init__(self, nc):
        self.nc = nc
        self.base = (nc.sbuf_base + 31) & ~31
        self.limit = nc.sbuf_top
        self.off = self.base
        self.marks = []
        self.n = 0

    def alloc(self, name, shape, dtype):
        nbytes = int(np.prod(shape[1:])) * DT_BYTES[dtype]
        nbytes = (nbytes + 31) & ~31
        assert self.off + nbytes <= self.limit, (
            f"SBUF overflow at {name}: {self.off}+{nbytes} > {self.limit}")
        t = self.nc.alloc_sbuf_tensor_at(f"{name}_{self.n}", list(shape), dtype, offset=self.off)
        self.n += 1
        self.off += nbytes
        return t

    def push(self):
        self.marks.append(self.off)

    def pop(self):
        self.off = self.marks.pop()


class K:
    def __init__(self, nc):
        self.nc = nc
        self.sb = SBuf(nc)
        self.cnt = {}
        self.sem = {}
        for name in ("h", "g", "cc", "pe", "ve", "se", "ge"):
            self.sem[name] = nc.alloc_semaphore(f"sem_{name}")
            self.cnt[name] = 0
        self.ename = {}

    def _e(self, eng):
        n = {"PE": "pe", "DVE": "ve", "Activation": "se", "Pool": "ge"}
        return n[str(eng.engine.value) if hasattr(eng, "engine") else "?"]

    def edge(self, prod, *cons):
        import concourse.mybir as mb
        emap = {mb.EngineType.PE: "pe", mb.EngineType.DVE: "ve",
                mb.EngineType.Activation: "se", mb.EngineType.Pool: "ge"}
        name = emap[prod.engine]
        self.cnt[name] += 1
        prod.drain().then_inc(self.sem[name], 1)
        for c in cons:
            c.wait_ge(self.sem[name], self.cnt[name])

    def dh(self, out, in_):
        self.nc.sync.dma_start(out=out, in_=in_).then_inc(self.sem["h"], 16)
        self.cnt["h"] += 16

    def dg(self, out, in_):
        self.nc.gpsimd.dma_start(out=out, in_=in_).then_inc(self.sem["g"], 16)
        self.cnt["g"] += 16

    def wait_dma(self, *engines):
        for e in engines:
            e.wait_ge(self.sem["h"], self.cnt["h"])
            e.wait_ge(self.sem["g"], self.cnt["g"])

    def bar(self):
        nc = self.nc
        self.wait_dma(nc.sync, nc.gpsimd)
        nc.all_engine_barrier()


def build(taps=(), upto=99):
    nc = bass.Bass("TRN2", target_bir_lowering=False)
    k = K(nc)
    sb = k.sb
    V, S, P, G = nc.vector, nc.scalar, nc.tensor, nc.gpsimd

    di = lambda n, shp, dt=F32: nc.dram_tensor(n, shp, dt, kind="ExternalInput")
    x_own = di("x_own", [TOK, D])
    out_d = nc.dram_tensor("out", [TOK, D], F32, kind="ExternalOutput")
    Wq_a = di("Wq_a", [HG, 2, D, DK // 2]); Wk_a = di("Wk_a", [HG, 2, D, DK // 2])
    Wv_a = di("Wv_a", [HG, D, DV]); Wg_a = di("Wg_a", [4, D, VD // 4])
    Wgk1 = di("Wgk1", [D, 16]); Wgk2 = di("Wgk2", [16, D]); bgk2 = di("bgk2", [128, HG * NDKC])
    Wnorm_rep = di("Wnorm_rep", [128, DV]); Wo_a = di("Wo_a", [4, VD, D // 4])
    Wq_b = di("Wq_b", [D, D]); Wk_b = di("Wk_b", [D, KVH * DH])
    Wv_b = di("Wv_b", [D, KVH * DH]); Wo_b = di("Wo_b", [D, D])
    wg_sl = di("wg_sl", [FI // 8, D]); wu_sl = di("wu_sl", [FI // 8, D])
    wd_sl = di("wd_sl", [D // 8, FI])
    cos_t = di("cos_t", [TOK, DH]); sin_t = di("sin_t", [TOK, DH])
    gla_gate = di("gla_gate", [128, 1])
    attnb_bias = di("attnb_bias", [128, 1])
    tri_gla = di("tri_gla", [128, CH])
    diag_mask = di("diag_mask", [128, 128])
    tri_r = di("tri_r", [128, 128])
    attnb_cols = di("attnb_cols", [128, 4])
    ident_c = di("ident_c", [128, 128])

    cc_ws_in = nc.dram_tensor("cc_ws_in", [128, 4], F32)
    cc_ws_out = nc.dram_tensor("cc_ws_out", [128, 4], F32)
    WQ_G = D * (FI // 8)
    WQ_SL = 2 * WQ_G + FI * (D // 8)
    cc_wq_in = nc.dram_tensor("cc_wq_in", [WQ_SL], BF16)
    cc_wq_out = nc.dram_tensor("cc_wq_out", [8, WQ_SL], BF16)
    cc_S_in = nc.dram_tensor("cc_S_in", [HG * NDKC * 128, DV], F32)
    cc_S_out = nc.dram_tensor("cc_S_out", [2, HG * NDKC * 128, DV], F32)
    u_spill = nc.dram_tensor("u_spill", [TOK, FI], F32)
    cc_kv_in = nc.dram_tensor("cc_kv_in", [TOK, 2 * KVH * DH], BF16)
    cc_kv_out = nc.dram_tensor("cc_kv_out", [2, TOK, 2 * KVH * DH], BF16)

    dbg = {}

    def tap(name, a):
        if name not in taps:
            return
        d = nc.dram_tensor(f"dbg_{name}", list(a.shape), a.dtype, kind="ExternalOutput")
        k.bar()
        k.dg(d.ap(), a)
        k.bar()
        dbg[name] = f"dbg_{name}"

    resid = sb.alloc("resid", [128, NTOKC * D], F32)
    ident_f = sb.alloc("ident_f", [128, 128], F32)
    tri_s = sb.alloc("tri_s", [128, CH], F32)
    wnorm_s = sb.alloc("wnorm_s", [128, DV], F32)
    gate_s = sb.alloc("gate_s", [128, 1], F32)
    abias_s = sb.alloc("abias_s", [128, 1], F32)
    sc = sb.alloc("smallcols", [128, 64], F32)

    psS_t = nc.alloc_psum_tensor("psS", [128, 2048], F32)
    pb_hi = [nc.alloc_psum_tensor(f"pbk{i}", [128, 512], F32) for i in range(4, 8)]
    PB = [psS_t.ap()[:, i * 512:(i + 1) * 512] for i in range(4)] + [t.ap() for t in pb_hi]
    psS = psS_t.ap()

    k.dh(resid.ap().rearrange("p (c d) -> p c d", d=D),
         x_own.ap().rearrange("(c p) d -> p c d", p=128))
    k.dh(ident_f[:, :], ident_c[:, :])
    k.dh(tri_s[:, :], tri_gla[:, :])
    k.dh(wnorm_s[:, :], Wnorm_rep[:, :])
    k.dh(gate_s[:, :], gla_gate[:, :])
    k.dh(abias_s[:, :], attnb_bias[:, :])
    V.memset(sc[:, 60:61], EPS)
    V.memset(sc[:, 61:62], 1.0)
    V.drain()
    k.bar()

    def ln_to_fm(xln_fm, xf_tile, scratch):
        for tc in range(NTOKC):
            xt = resid[:, tc * D:(tc + 1) * D]
            V.reduce_sum(sc[:, 0:1], xt, axis=AX)
            V.scalar_tensor_tensor(scratch[:, 0:D], xt, 1.0, xt, ALU.bypass, ALU.mult,
                                   accum_out=sc[:, 1:2])
            V.drain()
            V.tensor_scalar(sc[:, 2:3], sc[:, 0:1], 1.0 / D, None, ALU.mult)
            V.drain()
            V.tensor_tensor(sc[:, 3:4], sc[:, 2:3], sc[:, 2:3], ALU.mult)
            V.tensor_scalar(sc[:, 4:5], sc[:, 1:2], 1.0 / D, None, ALU.mult)
            V.drain()
            V.tensor_tensor(sc[:, 5:6], sc[:, 4:5], sc[:, 3:4], ALU.subtract)
            k.edge(V, S)
            S.activation(sc[:, 6:7], sc[:, 5:6], ACT.Ln, bias=sc[:, 60:61], scale=1.0)
            S.drain()
            S.activation(sc[:, 7:8], sc[:, 6:7], ACT.Exp, bias=0.0, scale=-0.5)
            k.edge(S, V)
            V.tensor_scalar(xf_tile[:, 0:D], xt, sc[:, 2:3], sc[:, 7:8],
                            ALU.subtract, ALU.mult)
            k.edge(V, P)
            for dc in range(0, NDIN, 2):
                P.transpose(PB[6][:, 0:128], xf_tile[:, dc * 128:(dc + 1) * 128], ident_f[:, :])
                P.transpose(PB[7][:, 0:128], xf_tile[:, (dc + 1) * 128:(dc + 2) * 128], ident_f[:, :])
                k.edge(P, V)
                V.tensor_copy(xln_fm[:, dc * TOK + tc * 128: dc * TOK + (tc + 1) * 128],
                              PB[6][:, 0:128])
                V.tensor_copy(xln_fm[:, (dc + 1) * TOK + tc * 128: (dc + 1) * TOK + (tc + 1) * 128],
                              PB[7][:, 0:128])
                k.edge(V, P)
        k.bar()

    mats = [(wg_sl, FI // 8, D, 0), (wu_sl, FI // 8, D, WQ_G), (wd_sl, D // 8, FI, 2 * WQ_G)]
    if upto >= 1:
        sb.push()
        ones_f = sb.alloc("ones_f", [128, 128], F32)
        V.memset(ones_f[:, :], 1.0)
        V.drain()
        wbuf = sb.alloc("wsw", [128, 8192], F32)
        wqtstage = sb.alloc("wqtstage", [128, 16384], F32)
        c1buf = sb.alloc("c1buf", [128, 16384], BF16)
        for mi, (wd, rows, cols, _off) in enumerate(mats):
            nrc = rows // 128
            for hf in range(2):
                k.dh(wbuf.ap().rearrange("p (c d) -> p c d", d=cols)[:, 0:nrc // 2, :],
                     wd.ap().rearrange("(c p) d -> p c d", p=128)[:, hf * (nrc // 2):(hf + 1) * (nrc // 2), :])
                k.wait_dma(V)
                V.reduce_sum(sc[:, 12 + hf:13 + hf], wbuf[:, 0:(nrc // 2) * cols], axis=AX,
                             apply_absolute_value=True)
                k.edge(V, nc.sync)
            V.tensor_tensor(sc[:, 8 + mi:9 + mi], sc[:, 12:13], sc[:, 13:14], ALU.add)
            V.drain()
        V.memset(sc[:, 11:12], 0.0)
        k.edge(V, nc.sync)
        k.dh(cc_ws_in.ap(), sc[:, 8:12])
        k.wait_dma(G)
        G.collective_compute("AllReduce", ALU.add, replica_groups=[list(range(8))],
                             ins=[cc_ws_in.ap().opt()], outs=[cc_ws_out.ap().opt()]
                             ).then_inc(k.sem["cc"], 1)
        k.cnt["cc"] += 1
        G.wait_ge(k.sem["cc"], k.cnt["cc"])
        k.dg(sc[:, 16:20], cc_ws_out.ap())
        k.bar()
        P.matmul(PB[0][:, 0:4], ones_f[:, :], sc[:, 16:20], start=True, stop=True)
        k.edge(P, V)
        for mi, (wd, rows, cols, _off) in enumerate(mats):
            n_el = rows * 8 * cols
            V.tensor_scalar(sc[:, 24 + mi:25 + mi], PB[0][:, mi:mi + 1], 1.0 / n_el, None,
                            ALU.mult)
        V.drain()
        k.bar()
        for mi, (wd, rows, cols, _off) in enumerate(mats):
            nrc = rows // 128
            for hf in range(2):
                k.dh(wbuf.ap().rearrange("p (c d) -> p c d", d=cols)[:, 0:nrc // 2, :],
                     wd.ap().rearrange("(c p) d -> p c d", p=128)[:, hf * (nrc // 2):(hf + 1) * (nrc // 2), :])
                k.wait_dma(S, V)
                half_n = (nrc // 2) * cols
                S.activation(wqtstage[:, 0:half_n], wbuf[:, 0:half_n], ACT.Abs,
                             bias=0.0, scale=1.0)
                k.edge(S, V)
                V.tensor_scalar(wqtstage[:, 0:half_n], wqtstage[:, 0:half_n],
                                sc[:, 24 + mi:25 + mi], None, ALU.subtract)
                V.drain()
                V.reduce_sum(sc[:, 12 + hf:13 + hf], wqtstage[:, 0:half_n], axis=AX)
                k.edge(V, nc.sync)
            V.tensor_tensor(sc[:, 8 + mi:9 + mi], sc[:, 12:13], sc[:, 13:14], ALU.add)
            V.drain()
        V.memset(sc[:, 11:12], 0.0)
        k.edge(V, nc.sync)
        k.dh(cc_ws_in.ap(), sc[:, 8:12])
        k.wait_dma(G)
        G.collective_compute("AllReduce", ALU.add, replica_groups=[list(range(8))],
                             ins=[cc_ws_in.ap().opt()], outs=[cc_ws_out.ap().opt()]
                             ).then_inc(k.sem["cc"], 1)
        k.cnt["cc"] += 1
        G.wait_ge(k.sem["cc"], k.cnt["cc"])
        k.dg(sc[:, 16:20], cc_ws_out.ap())
        k.bar()
        P.matmul(PB[0][:, 0:4], ones_f[:, :], sc[:, 16:20], start=True, stop=True)
        k.edge(P, V)
        for mi, (wd, rows, cols, _off) in enumerate(mats):
            n_el = rows * 8 * cols
            V.tensor_scalar(sc[:, 28 + mi:29 + mi], PB[0][:, mi:mi + 1], 1.0 / n_el, None,
                            ALU.mult)
        V.drain()
        V.tensor_tensor(sc[:, 28:31], sc[:, 28:31], sc[:, 24:27], ALU.add)
        V.drain()
        V.tensor_scalar(sc[:, 28:31], sc[:, 28:31], 1.0, EPS, ALU.mult, ALU.add)
        V.drain()
        V.tensor_scalar(sc[:, 20:23], sc[:, 28:31], 0.5, None, ALU.mult)
        V.tensor_scalar(sc[:, 24:27], sc[:, 28:31], -0.5, None, ALU.mult)
        V.drain()
        k.bar()
        for mi, (wd, rows, cols, off) in enumerate(mats):
            nrc = rows // 128
            ncolc = cols // 128
            half_n = (nrc // 2) * cols
            for hf in range(2):
                k.dh(wbuf.ap().rearrange("p (c d) -> p c d", d=cols)[:, 0:nrc // 2, :],
                     wd.ap().rearrange("(c p) d -> p c d", p=128)[:, hf * (nrc // 2):(hf + 1) * (nrc // 2), :])
                k.wait_dma(V)
                V.tensor_scalar(wqtstage[:, hf * half_n:(hf + 1) * half_n], wbuf[:, 0:half_n],
                                sc[:, 20 + mi:21 + mi], None, ALU.is_ge)
                V.drain()
                V.scalar_tensor_tensor(wqtstage[:, hf * half_n:(hf + 1) * half_n], wbuf[:, 0:half_n],
                                       sc[:, 24 + mi:25 + mi], wqtstage[:, hf * half_n:(hf + 1) * half_n],
                                       ALU.is_le, ALU.subtract)
                k.edge(V, nc.sync)
            k.edge(V, P)
            tiles = [(cc_i, rc) for cc_i in range(ncolc) for rc in range(nrc)]
            for i0 in range(0, len(tiles), 4):
                grp = tiles[i0:i0 + 4]
                for j, (cc_i, rc) in enumerate(grp):
                    P.transpose(PB[j][:, 0:128],
                                wqtstage[:, rc * cols + cc_i * 128: rc * cols + cc_i * 128 + 128],
                                ident_f[:, :])
                k.edge(P, V)
                for j, (cc_i, rc) in enumerate(grp):
                    V.tensor_copy(c1buf[:, cc_i * rows + rc * 128: cc_i * rows + (rc + 1) * 128],
                                  PB[j][:, 0:128])
                k.edge(V, P)
            k.edge(V, nc.sync)
            k.dh(AP(cc_wq_in, off, [[rows, 128], [128 * rows, ncolc], [1, rows]]),
                 c1buf.ap().rearrange("p (a r) -> p a r", r=rows)[:, 0:ncolc, :])
            k.bar()
        k.wait_dma(G)
        G.collective_compute("AllGather", ALU.bypass, replica_groups=[list(range(8))],
                             ins=[cc_wq_in.ap().opt()], outs=[cc_wq_out.ap().opt()]
                             ).then_inc(k.sem["cc"], 1)
        k.cnt["cc"] += 1
        sb.pop()
        tap("wstat", sc[:, 0:32])

    if upto >= 2:
        sb.push()
        xln_fm = sb.alloc("xln_fm", [128, NDIN * TOK], BF16)
        scratch = sb.alloc("lnscr", [128, 2048], F32)
        qdec = sb.alloc("qdec", [128, HG * NDKC * TOK], BF16)
        o_all = sb.alloc("o_all", [128, NTOKC * VD], BF16)
        g_all = sb.alloc("g_all", [128, NTOKC * VD], BF16)
        kbuf = sb.alloc("kbuf", [128, NDKC * TOK], BF16)
        vbuf = sb.alloc("vbuf", [128, NTOKC * DV], BF16)
        v_odd = sb.alloc("v_odd", [64, NTOKC * DV], BF16)
        Sf = sb.alloc("Sf", [128, NDKC * DV], F32)
        Sb = sb.alloc("Sb", [128, NDKC * DV], BF16)
        t_fm = sb.alloc("t_fm", [16, TOK], BF16)
        wgk1_s = sb.alloc("wgk1_s", [128, NDIN * 16], BF16)
        wgk2_s = sb.alloc("wgk2_s", [16, D], BF16)
        wbuf0 = sb.alloc("whead0", [128, NDIN * 256], BF16)

        kkT = sb.alloc("kkT", [128, DK], BF16)
        abuf = sb.alloc("abuf", [128, CH], BF16)
        nbg = sb.alloc("nbg", [128, HG * NDKC], F32)
        bgf = sb.alloc("bgf", [128, HG * NDKC], F32)

        ln_to_fm(xln_fm, scratch, scratch)
        tap("xln_fm", xln_fm[:, :])

        k.dh(bgf[:, :], bgk2[:, :])
        k.wait_dma(V)
        V.tensor_scalar(nbg[:, :], bgf[:, :], -1.0, None, ALU.mult)
        V.drain()
        k.dg(wgk1_s.ap().rearrange("p (c o) -> p c o", o=16),
             Wgk1.ap().rearrange("(c p) o -> p c o", p=128))
        k.dg(wgk2_s[:, :], Wgk2[:, :])
        k.bar()
        for half in range(2):
            for dc in range(NDIN):
                P.matmul(PB[4][0:16, 0:512], wgk1_s[:, dc * 16:(dc + 1) * 16],
                         xln_fm[:, dc * TOK + half * 512: dc * TOK + (half + 1) * 512],
                         start=(dc == 0), stop=(dc == NDIN - 1))
            k.edge(P, V)
            V.tensor_copy(t_fm[0:16, half * 512:(half + 1) * 512], PB[4][0:16, 0:512])
            k.edge(V, P)
        k.bar()
        tap("t_fm", t_fm[:, :])

        for h in range(HG):
            qh = qdec[:, h * NDKC * TOK:(h + 1) * NDKC * TOK]
            for dst, wsrc, scale_, buf in ((qh, Wq_a, SCALE_GLA, "q"), (None, Wk_a, None, "k")):
                for chf in range(2):
                    k.dg(wbuf0.ap().rearrange("p (c o) -> p c o", o=256),
                         wsrc.ap()[h, chf, :, :].rearrange("(c p) o -> p c o", p=128))
                    k.bar()
                    for dkc in (chf * 2, chf * 2 + 1):
                        dkl = dkc - chf * 2
                        for tb in range(2):
                            pbi = PB[tb]
                            for dc in range(NDIN):
                                P.matmul(pbi[:, :],
                                         wbuf0[:, dc * 256 + dkl * 128: dc * 256 + (dkl + 1) * 128],
                                         xln_fm[:, dc * TOK + tb * 512: dc * TOK + (tb + 1) * 512],
                                         start=(dc == 0), stop=(dc == NDIN - 1))
                        k.edge(P, V)
                        for tb in range(2):
                            if buf == "q":
                                V.tensor_scalar(dst[:, dkc * TOK + tb * 512: dkc * TOK + (tb + 1) * 512],
                                                PB[tb][:, :], scale_, None, ALU.mult)
                            else:
                                V.tensor_copy(kbuf[:, dkc * TOK + tb * 512: dkc * TOK + (tb + 1) * 512],
                                              PB[tb][:, :])
                        k.edge(V, P)
                    k.bar()
            k.dg(wbuf0.ap().rearrange("p (c o) -> p c o", o=DV),
                 Wv_a.ap()[h, :, :].rearrange("(c p) o -> p c o", p=128))
            k.bar()
            for tc in range(NTOKC):
                pbi = PB[tc % 2]
                for dc in range(NDIN):
                    P.matmul(pbi[:, 0:DV],
                             xln_fm[:, dc * TOK + tc * 128: dc * TOK + (tc + 1) * 128],
                             wbuf0[:, dc * DV:(dc + 1) * DV],
                             start=(dc == 0), stop=(dc == NDIN - 1))
                k.edge(P, V)
                V.tensor_copy(vbuf[:, tc * DV:(tc + 1) * DV], pbi[:, 0:DV])
                k.edge(V, P)
            k.bar()
            k.dh(v_odd[0:64, :], vbuf[64:128, :])
            k.bar()
            if h == 0:
                tap("q0", qdec[:, 0:NDKC * TOK])
                tap("k0", kbuf[:, :])
                tap("v0", vbuf[:, :])
            lsq = scratch[:, 1536:1536 + NDKC * CH]
            bfm = scratch[:, 1792:1792 + NDKC * CH]
            scr_bf = scratch.ap().bitcast(BF16)
            qet = scr_bf[:, 512:512 + NDKC * CH]
            kdt = scr_bf[:, 768:768 + NDKC * CH]
            ebt = scratch[:, 768:768 + NDKC * CH]
            enbt = scratch[:, 1024:1024 + NDKC * CH]
            kkf = scratch[:, 1280:1280 + NDKC * CH]
            V.memset(Sf[:, :], 0.0)
            V.memset(Sb[:, :], 0.0)
            V.memset(sc[:, 32:36], 0.0)
            V.drain()
            k.bar()
            for c in range(NCH):
                od = (c % 2) * 64
                tc = c // 2
                for dkc in range(NDKC):
                    P.matmul(PB[2][:, dkc * CH:(dkc + 1) * CH],
                             wgk2_s[0:16, (h * DK + dkc * 128):(h * DK + dkc * 128) + 128],
                             t_fm[0:16, c * CH:(c + 1) * CH], start=True, stop=True)
                k.edge(P, S)
                zabs = scratch[:, 0:NDKC * CH]
                rmin = scratch[:, 512:512 + NDKC * CH]
                for dkc in range(NDKC):
                    S.activation(zabs[:, dkc * CH:(dkc + 1) * CH],
                                 PB[2][:, dkc * CH:(dkc + 1) * CH],
                                 ACT.Abs, bias=bgf[:, h * NDKC + dkc: h * NDKC + dkc + 1], scale=1.0)
                    S.activation(rmin[:, dkc * CH:(dkc + 1) * CH],
                                 PB[2][:, dkc * CH:(dkc + 1) * CH],
                                 ACT.Relu, bias=nbg[:, h * NDKC + dkc: h * NDKC + dkc + 1], scale=-1.0)
                S.drain()
                S.activation(zabs, zabs, ACT.Exp, bias=0.0, scale=-1.0)
                S.drain()
                S.activation(zabs, zabs, ACT.Ln, bias=sc[:, 61:62], scale=1.0)
                k.edge(S, V, P)
                V.scalar_tensor_tensor(lsq, rmin, -1.0, zabs, ALU.mult, ALU.subtract)
                V.drain()
                for dkc in range(NDKC):
                    V.tensor_tensor_scan(bfm[:, dkc * CH:(dkc + 1) * CH],
                                         lsq[:, dkc * CH:(dkc + 1) * CH],
                                         lsq[:, dkc * CH:(dkc + 1) * CH], 0.0, ALU.add, ALU.bypass)
                k.edge(V, S)
                S.activation(ebt, bfm, ACT.Exp, bias=0.0, scale=S16)
                S.activation(enbt, bfm, ACT.Exp, bias=0.0, scale=-S16)
                blast_ap = bfm.rearrange("p (a s) -> p a s", s=CH)[:, :, CH - 1:CH].opt()
                S.activation(sc[:, 36:40], blast_ap, ACT.Exp, bias=0.0, scale=S16)
                S.activation(sc[:, 40:44], sc[:, 32:36], ACT.Exp, bias=0.0, scale=S16)
                k.edge(S, V)
                V.tensor_tensor(sc[:, 32:36], sc[:, 32:36], blast_ap, ALU.add)
                V.drain()
                q_ap = AP(qdec, h * NDKC * TOK + c * CH, [[HG * NDKC * TOK, 128], [TOK, NDKC], [1, CH]])
                k_ap = AP(kbuf, c * CH, [[NDKC * TOK, 128], [TOK, NDKC], [1, CH]])
                qe3 = qet.rearrange("p (a s) -> p a s", s=CH)
                V.tensor_tensor(qe3, q_ap, ebt.rearrange("p (a s) -> p a s", s=CH), ALU.mult)
                V.tensor_tensor(kdt.rearrange("p (a s) -> p a s", s=CH), k_ap,
                                enbt.rearrange("p (a s) -> p a s", s=CH), ALU.mult)
                V.drain()
                for dkc in range(NDKC):
                    V.scalar_tensor_tensor(kkf[:, dkc * CH:(dkc + 1) * CH],
                                           kdt[:, dkc * CH:(dkc + 1) * CH],
                                           sc[:, 36 + dkc:37 + dkc],
                                           kdt[:, dkc * CH:(dkc + 1) * CH],
                                           ALU.mult, ALU.bypass)
                V.drain()
                for dkc in range(NDKC):
                    V.tensor_scalar(AP(qdec, h * NDKC * TOK + dkc * TOK + c * CH,
                                       [[HG * NDKC * TOK, 128], [1, CH]]),
                                    qet[:, dkc * CH:(dkc + 1) * CH],
                                    sc[:, 40 + dkc:41 + dkc], None, ALU.mult)
                k.edge(V, P)
                for dkc in range(NDKC):
                    P.matmul(PB[3][od:od + 64, 0:64], kdt[:, dkc * CH:(dkc + 1) * CH],
                             qet[:, dkc * CH:(dkc + 1) * CH],
                             start=(dkc == 0), stop=(dkc == NDKC - 1), tile_position=(0, od))
                for dkc in range(NDKC):
                    P.transpose(PB[4][0:64, dkc * 128:(dkc + 1) * 128],
                                kkf[:, dkc * CH:(dkc + 1) * CH], ident_f[:, :])
                k.edge(P, V)
                V.tensor_tensor(abuf[od:od + 64, :], PB[3][od:od + 64, 0:64],
                                tri_s[od:od + 64, :], ALU.mult)
                V.tensor_copy(kkT[0:64, :], PB[4][0:64, :])
                k.edge(V, P)
                P.matmul(PB[5][od:od + 64, 0:DV], abuf[od:od + 64, :],
                         vbuf[od:od + 64, tc * DV:(tc + 1) * DV], start=True, stop=False,
                         tile_position=(od, od))
                for dkc in range(NDKC):
                    P.matmul(PB[5][od:od + 64, 0:DV], qet[:, dkc * CH:(dkc + 1) * CH],
                             Sb[:, dkc * DV:(dkc + 1) * DV], start=False,
                             stop=(dkc == NDKC - 1), tile_position=(0, od))
                v_u = vbuf[0:64, tc * DV:(tc + 1) * DV] if od == 0 else \
                    v_odd[0:64, tc * DV:(tc + 1) * DV]
                for dkc in range(NDKC):
                    P.matmul(PB[6 + dkc // 2][:, (dkc % 2) * DV:(dkc % 2 + 1) * DV],
                             kkT[0:64, dkc * 128:(dkc + 1) * 128],
                             v_u, start=True, stop=True, tile_position=(0, 0))
                k.edge(P, V)
                V.tensor_copy(o_all[od:od + 64, tc * VD + h * DV: tc * VD + (h + 1) * DV],
                              PB[5][od:od + 64, 0:DV])
                for dkc in range(NDKC):
                    V.scalar_tensor_tensor(Sf[:, dkc * DV:(dkc + 1) * DV],
                                           Sf[:, dkc * DV:(dkc + 1) * DV],
                                           sc[:, 36 + dkc:37 + dkc],
                                           PB[6 + dkc // 2][:, (dkc % 2) * DV:(dkc % 2 + 1) * DV],
                                           ALU.mult, ALU.add)
                V.drain()
                V.tensor_copy(Sb[:, :], Sf[:, :])
                k.edge(V, P)
            k.bar()
            k.dh(cc_S_in.ap().rearrange("(a p) v -> p a v", p=128)[:, h * NDKC:(h + 1) * NDKC, :],
                 Sf.ap().rearrange("p (a v) -> p a v", v=DV))
            k.bar()
            if h == 0:
                tap("S0f", Sf[:, :])
                tap("o_h0", o_all[:, 0:VD])
        k.wait_dma(G)
        G.collective_compute("AllGather", ALU.bypass,
                             replica_groups=[[0, 1], [2, 3], [4, 5], [6, 7]],
                             ins=[cc_S_in.ap().opt()], outs=[cc_S_out.ap().opt()]
                             ).then_inc(k.sem["cc"], 1)
        k.cnt["cc"] += 1
        for qtr in range(4):
            k.dg(wbuf0.ap().rearrange("p (c o) -> p c o", o=256)[:, 0:NDIN, :],
                 Wg_a.ap()[qtr, :, :].rearrange("(c p) o -> p c o", p=128))
            k.bar()
            for tc in range(NTOKC):
                pbi = PB[tc % 2]
                for dc in range(NDIN):
                    P.matmul(pbi[:, 0:256], xln_fm[:, dc * TOK + tc * 128: dc * TOK + (tc + 1) * 128],
                             wbuf0[:, dc * 256:(dc + 1) * 256],
                             start=(dc == 0), stop=(dc == NDIN - 1))
                k.edge(P, V)
                V.tensor_copy(g_all[:, tc * VD + qtr * 256: tc * VD + (qtr + 1) * 256],
                              pbi[:, 0:256])
                k.edge(V, P)
            k.bar()
        tap("g_all", g_all[:, :])
        G.wait_ge(k.sem["cc"], k.cnt["cc"])
        s0buf = kbuf
        for h in range(HG):
            k.dg(s0buf.ap().rearrange("p (a v) -> p a v", v=DV)[:, 0:NDKC, :],
                 cc_S_out.ap()[0, :, :].rearrange("(a p) v -> p a v", p=128)
                 [:, h * NDKC:(h + 1) * NDKC, :])
            k.bar()
            qh = qdec[:, h * NDKC * TOK:(h + 1) * NDKC * TOK]
            for tc in range(NTOKC):
                pbi = PB[2 + tc % 2]
                for dkc in range(NDKC):
                    P.matmul(pbi[:, 0:DV], qh[:, dkc * TOK + tc * 128: dkc * TOK + (tc + 1) * 128],
                             s0buf[:, dkc * DV:(dkc + 1) * DV],
                             start=(dkc == 0), stop=(dkc == NDKC - 1))
                k.edge(P, V)
                dst = o_all[:, tc * VD + h * DV: tc * VD + (h + 1) * DV]
                V.scalar_tensor_tensor(dst, pbi[:, 0:DV], gate_s[:, 0:1], dst, ALU.mult, ALU.add)
                k.edge(V, P)
            k.bar()
        k.bar()
        tap("o_corr", o_all[:, :])
        z_fm = qdec
        for tc in range(NTOKC):
            for h in range(HG):
                o_sl = o_all[:, tc * VD + h * DV: tc * VD + (h + 1) * DV]
                g_sl = g_all[:, tc * VD + h * DV: tc * VD + (h + 1) * DV]
                V.scalar_tensor_tensor(scratch[:, 0:DV], o_sl, 1.0, o_sl, ALU.bypass, ALU.mult,
                                       accum_out=sc[:, 44:45])
                k.edge(V, S)
                S.activation(sc[:, 45:46], sc[:, 44:45], ACT.Ln, bias=sc[:, 60:61], scale=1.0 / DV)
                S.drain()
                S.activation(sc[:, 46:47], sc[:, 45:46], ACT.Exp, bias=0.0, scale=-0.5)
                S.activation(scratch[:, 1024:1024 + DV], g_sl, ACT.Silu, bias=0.0, scale=1.0)
                k.edge(S, V)
                V.tensor_scalar(scratch[:, 0:DV], o_sl, sc[:, 46:47], None, ALU.mult)
                V.drain()
                V.tensor_tensor(scratch[:, 0:DV], scratch[:, 0:DV], wnorm_s[:, :], ALU.mult)
                V.drain()
                V.tensor_tensor(scratch[:, 256:256 + DV], scratch[:, 0:DV],
                                scratch[:, 1024:1024 + DV], ALU.mult)
                k.edge(V, P)
                for j in range(2):
                    vc = 2 * h + j
                    pbi = PB[j]
                    P.transpose(pbi[:, 0:128], scratch[:, 256 + j * 128:256 + (j + 1) * 128],
                                ident_f[:, :])
                k.edge(P, V)
                for j in range(2):
                    vc = 2 * h + j
                    V.tensor_copy(z_fm[:, vc * TOK + tc * 128: vc * TOK + (tc + 1) * 128],
                                  PB[j][:, 0:128])
                k.edge(V, P)
        k.bar()
        tap("z_fm", z_fm[:, 0:NVDC * TOK])
        for qtr in range(4):
            k.dg(wbuf0.ap().rearrange("p (c o) -> p c o", o=512)[:, 0:NVDC, :],
                 Wo_a.ap()[qtr, :, :].rearrange("(c p) o -> p c o", p=128))
            k.bar()
            for tc in range(NTOKC):
                pbi = PB[tc % 2]
                for vc in range(NVDC):
                    P.matmul(pbi[:, :], z_fm[:, vc * TOK + tc * 128: vc * TOK + (tc + 1) * 128],
                             wbuf0[:, vc * 512:(vc + 1) * 512],
                             start=(vc == 0), stop=(vc == NVDC - 1))
                k.edge(P, V)
                V.tensor_tensor(resid[:, tc * D + qtr * 512: tc * D + qtr * 512 + 512],
                                resid[:, tc * D + qtr * 512: tc * D + qtr * 512 + 512],
                                pbi[:, :], ALU.add)
                k.edge(V, P)
            k.bar()
        k.bar()
        sb.pop()
        tap("x1", resid[:, :])




    def emit_ffn(tag):
        sb.push()
        xq_fm = sb.alloc(f"xqfm{tag}", [128, NDIN * TOK], BF16)
        scratch = sb.alloc(f"fscr{tag}", [128, D], F32)
        qscr = sb.alloc(f"qscr{tag}", [128, D], I32)
        slab_g = sb.alloc(f"slabg{tag}", [128, NDIN * 512], BF16)
        slab_u = sb.alloc(f"slabu{tag}", [128, NDIN * 512], BF16)
        sgt = sb.alloc(f"sgt{tag}", [128, 2 * 512], BF16)
        ut = sb.alloc(f"ut{tag}", [128, 2 * 512], F32)
        for tc in range(NTOKC):
            xt = resid[:, tc * D:(tc + 1) * D]
            V.reduce_sum(sc[:, 0:1], xt, axis=AX)
            V.scalar_tensor_tensor(scratch[:, 0:D], xt, 1.0, xt, ALU.bypass, ALU.mult,
                                   accum_out=sc[:, 1:2])
            V.drain()
            V.tensor_scalar(sc[:, 2:3], sc[:, 0:1], 1.0 / D, None, ALU.mult)
            V.drain()
            V.tensor_tensor(sc[:, 3:4], sc[:, 2:3], sc[:, 2:3], ALU.mult)
            V.tensor_scalar(sc[:, 4:5], sc[:, 1:2], 1.0 / D, None, ALU.mult)
            V.drain()
            V.tensor_tensor(sc[:, 5:6], sc[:, 4:5], sc[:, 3:4], ALU.subtract)
            k.edge(V, S)
            S.activation(sc[:, 6:7], sc[:, 5:6], ACT.Ln, bias=sc[:, 60:61], scale=1.0)
            S.drain()
            S.activation(sc[:, 7:8], sc[:, 6:7], ACT.Exp, bias=0.0, scale=-0.5)
            k.edge(S, V)
            V.tensor_scalar(scratch[:, 0:D], xt, sc[:, 2:3], sc[:, 7:8],
                            ALU.subtract, ALU.mult)
            V.drain()
            V.reduce_max(sc[:, 8:9], scratch[:, 0:D], axis=AX, apply_absolute_value=True)
            V.drain()
            V.tensor_scalar(sc[:, 16 + tc:17 + tc], sc[:, 8:9], 1.0 / 127.0, EPS,
                            ALU.mult, ALU.add)
            V.drain()
            V.reciprocal(sc[:, 9:10], sc[:, 16 + tc:17 + tc])
            V.drain()
            V.tensor_scalar(scratch[:, 0:D], scratch[:, 0:D], sc[:, 9:10], None, ALU.mult)
            V.drain()
            V.tensor_copy(qscr[:, :], scratch[:, 0:D])
            V.drain()
            V.tensor_copy(scratch[:, 0:D], qscr[:, :])
            k.edge(V, P)
            for dc in range(0, NDIN, 2):
                P.transpose(PB[6][:, 0:128], scratch[:, dc * 128:(dc + 1) * 128], ident_f[:, :])
                P.transpose(PB[7][:, 0:128], scratch[:, (dc + 1) * 128:(dc + 2) * 128], ident_f[:, :])
                k.edge(P, V)
                V.tensor_copy(xq_fm[:, dc * TOK + tc * 128: dc * TOK + (tc + 1) * 128],
                              PB[6][:, 0:128])
                V.tensor_copy(xq_fm[:, (dc + 1) * TOK + tc * 128: (dc + 1) * TOK + (tc + 1) * 128],
                              PB[7][:, 0:128])
                k.edge(V, P)
        V.tensor_scalar(sc[:, 40:48], sc[:, 16:24], sc[:, 28:29], -1.0, ALU.mult, ALU.mult)
        V.tensor_scalar(sc[:, 48:56], sc[:, 16:24], sc[:, 29:30], -1.0, ALU.mult, ALU.mult)
        V.memset(sc[:, 32:40], 0.0)
        V.drain()
        k.bar()
        tap(f"xqfm_{tag}", xq_fm[:, :])
        G.wait_ge(k.sem["cc"], k.cnt["cc"])
        for IB in range(16):
            r = IB // 2
            w0 = (IB % 2) * 512
            k.dg(slab_g.ap().rearrange("p (c o) -> p c o", o=512),
                 AP(cc_wq_out, r * WQ_SL + w0, [[1024, 128], [128 * 1024, NDIN], [1, 512]]))
            k.dg(slab_u.ap().rearrange("p (c o) -> p c o", o=512),
                 AP(cc_wq_out, r * WQ_SL + WQ_G + w0,
                    [[1024, 128], [128 * 1024, NDIN], [1, 512]]))
            k.bar()
            for tc in range(NTOKC):
                par = tc % 2
                pg, pu = PB[par * 2], PB[par * 2 + 1]
                sg_t = sgt[:, par * 512:(par + 1) * 512]
                u_t = ut[:, par * 512:(par + 1) * 512]
                for dc in range(NDIN):
                    P.matmul(pg[:, :], xq_fm[:, dc * TOK + tc * 128: dc * TOK + (tc + 1) * 128],
                             slab_g[:, dc * 512:(dc + 1) * 512],
                             start=(dc == 0), stop=(dc == NDIN - 1))
                for dc in range(NDIN):
                    P.matmul(pu[:, :], xq_fm[:, dc * TOK + tc * 128: dc * TOK + (tc + 1) * 128],
                             slab_u[:, dc * 512:(dc + 1) * 512],
                             start=(dc == 0), stop=(dc == NDIN - 1))
                k.edge(P, V)
                k.wait_dma(V)
                V.tensor_scalar(scratch[:, 0:512], pg[:, :], sc[:, 40 + tc:41 + tc],
                                None, ALU.mult)
                k.edge(V, S)
                S.activation(sg_t, scratch[:, 0:512], ACT.Silu, bias=0.0, scale=1.0)
                k.edge(S, V)
                V.scalar_tensor_tensor(u_t, pu[:, :], sc[:, 48 + tc:49 + tc], sg_t,
                                       ALU.mult, ALU.mult)
                V.drain()
                V.reduce_max(sc[:, 10:11], u_t, axis=AX, apply_absolute_value=True)
                V.drain()
                V.tensor_tensor(sc[:, 32 + tc:33 + tc], sc[:, 32 + tc:33 + tc],
                                sc[:, 10:11], ALU.max)
                k.edge(V, nc.sync, P)
                k.dh(AP(u_spill, (tc * 128) * FI + IB * 512, [[FI, 128], [1, 512]]), u_t)
            k.bar()
        V.tensor_scalar(sc[:, 32:40], sc[:, 32:40], 1.0 / 127.0, EPS, ALU.mult, ALU.add)
        V.drain()
        V.tensor_scalar(sc[:, 48:56], sc[:, 32:40], sc[:, 30:31], -1.0, ALU.mult, ALU.mult)
        for tc in range(NTOKC):
            V.reciprocal(sc[:, 0 + tc:1 + tc], sc[:, 32 + tc:33 + tc])
        V.drain()
        k.bar()
        sb.pop()
        sb.push()
        xqu_tm = sb.alloc(f"xqutm{tag}", [128, NTOKC * FI], I8)
        urow = sb.alloc(f"urow{tag}", [128, FI // 8], F32)
        scratch = sb.alloc(f"fscr2{tag}", [128, 512], F32)
        qscr = sb.alloc(f"qscr2{tag}", [128, 512], I32)
        for tc in range(NTOKC):
            for hf in range(8):
                k.edge(V, nc.sync)
                k.dh(urow[:, :], AP(u_spill, (tc * 128) * FI + hf * (FI // 8),
                                    [[FI, 128], [1, FI // 8]]))
                k.bar()
                for ib in range(2):
                    V.tensor_scalar(scratch[:, 0:512], urow[:, ib * 512:(ib + 1) * 512],
                                    sc[:, 0 + tc:1 + tc], None, ALU.mult)
                    V.drain()
                    V.tensor_copy(qscr[:, :], scratch[:, 0:512])
                    V.drain()
                    off = tc * FI + hf * (FI // 8) + ib * 512
                    V.tensor_copy(xqu_tm[:, off:off + 512], qscr[:, :])
                    V.drain()
        k.bar()
        if f"usp_{tag}" in taps:
            dtmp = nc.dram_tensor(f"dbg_usp_{tag}", [TOK, FI], BF16, kind="ExternalOutput")
            k.dg(dtmp.ap(), u_spill.ap())
            k.bar()
            dbg[f"usp_{tag}"] = 1
        tap(f"xqu_{tag}", xqu_tm[:, :])
        tap(f"scs_{tag}", sc[:, 0:64])
        xqu_fm = sb.alloc(f"xqufm{tag}", [128, 64 * TOK], I8)
        for ic in range(64):
            for tc in range(0, NTOKC, 2):
                V.tensor_copy(scratch[:, 0:128], xqu_tm[:, tc * FI + ic * 128: tc * FI + (ic + 1) * 128])
                V.tensor_copy(scratch[:, 128:256],
                              xqu_tm[:, (tc + 1) * FI + ic * 128: (tc + 1) * FI + (ic + 1) * 128])
                k.edge(V, P)
                P.transpose(PB[6][:, 0:128], scratch[:, 0:128], ident_f[:, :])
                P.transpose(PB[7][:, 0:128], scratch[:, 128:256], ident_f[:, :])
                k.edge(P, V)
                V.tensor_copy(xqu_fm[:, ic * TOK + tc * 128: ic * TOK + (tc + 1) * 128],
                              PB[6][:, 0:128])
                V.tensor_copy(xqu_fm[:, ic * TOK + (tc + 1) * 128: ic * TOK + (tc + 2) * 128],
                              PB[7][:, 0:128])
                k.edge(V, P)
        k.bar()
        wdb = sb.alloc(f"wdb{tag}", [128, 1024], BF16)
        xqb = sb.alloc(f"xqb{tag}", [128, 512], BF16)
        for oh in range(2):
            for th in range(2):
                for ic in range(64):
                    k.dg(wdb[:, :], AP(cc_wq_out, (4 * oh) * WQ_SL + 2 * WQ_G + ic * 128 * 256,
                                       [[256, 128], [WQ_SL, 4], [1, 256]]))
                    k.wait_dma(V)
                    V.tensor_copy(xqb[:, :],
                                  xqu_fm[:, ic * TOK + th * 512: ic * TOK + (th + 1) * 512])
                    k.edge(V, P)
                    for tcl in range(4):
                        for ob in range(2):
                            P.matmul(PB[tcl * 2 + ob][:, :], xqb[:, tcl * 128:(tcl + 1) * 128],
                                     wdb[:, ob * 512:(ob + 1) * 512],
                                     start=(ic == 0), stop=(ic == 63))
                    k.edge(P, G, V)
                for tcl in range(4):
                    tc = th * 4 + tcl
                    for ob in range(2):
                        col = oh * 1024 + ob * 512
                        V.scalar_tensor_tensor(resid[:, tc * D + col: tc * D + col + 512],
                                               PB[tcl * 2 + ob][:, :], sc[:, 48 + tc:49 + tc],
                                               resid[:, tc * D + col: tc * D + col + 512],
                                               ALU.mult, ALU.add)
                k.edge(V, P)
                k.bar()
        k.bar()
        sb.pop()

    if upto >= 3:
        emit_ffn("f1")
        tap("x2", resid[:, :])

    if upto >= 4:
        sb.push()
        q_fm = sb.alloc("bqfm", [128, HB * TOK], BF16)
        k_fm = sb.alloc("bkfm", [128, KVH * 2 * TOK], BF16)
        v_tm = sb.alloc("bvtm", [128, 16 * KVH * DH], BF16)
        sb.push()
        xln_fm = sb.alloc("bxln", [128, NDIN * TOK], BF16)
        sb.push()
        lnscr = sb.alloc("blnscr", [128, D], F32)
        ln_to_fm(xln_fm, lnscr, lnscr)
        sb.pop()
        slab = sb.alloc("bslab", [128, NDIN * 512], BF16)
        scratch = sb.alloc("bscr", [128, 1536], F32)
        cos_s = sb.alloc("bcos", [128, NTOKC * DH], BF16)
        sin_s = sb.alloc("bsin", [128, NTOKC * DH], BF16)

        k.dg(cos_s.ap().rearrange("p (c d) -> p c d", d=DH),
             cos_t.ap().rearrange("(c p) d -> p c d", p=128))
        k.dg(sin_s.ap().rearrange("p (c d) -> p c d", d=DH),
             sin_t.ap().rearrange("(c p) d -> p c d", p=128))
        k.bar()

        def rope_tile(tc, nh):
            cbc = AP(cos_s, tc * DH, [[NTOKC * DH, 128], [0, nh], [1, DH]])
            sA1 = AP(sin_s, tc * DH, [[NTOKC * DH, 128], [0, nh], [1, 64]])
            sA2 = AP(sin_s, tc * DH + 64, [[NTOKC * DH, 128], [0, nh], [1, 64]])
            src3 = AP(scratch, 0, [[1536, 128], [DH, nh], [1, DH]])
            src_lo = AP(scratch, 0, [[1536, 128], [DH, nh], [1, 64]])
            src_hi = AP(scratch, 64, [[1536, 128], [DH, nh], [1, 64]])
            dst3 = AP(scratch, 512, [[1536, 128], [DH, nh], [1, DH]])
            dst_lo = AP(scratch, 512, [[1536, 128], [DH, nh], [1, 64]])
            dst_hi = AP(scratch, 512 + 64, [[1536, 128], [DH, nh], [1, 64]])
            tmp = AP(scratch, 1024, [[1536, 128], [64, nh], [1, 64]])
            V.tensor_tensor(dst3, src3, cbc, ALU.mult)
            V.drain()
            V.tensor_tensor(tmp, src_hi, sA1, ALU.mult)
            V.drain()
            V.tensor_tensor(dst_lo, dst_lo, tmp, ALU.add)
            V.drain()
            V.tensor_tensor(tmp, src_lo, sA2, ALU.mult)
            V.drain()
            V.tensor_tensor(dst_hi, dst_hi, tmp, ALU.add)
            V.drain()

        for wsrc, kind in ((Wk_b, "k"), (Wv_b, "v")):
            k.dg(slab.ap().rearrange("p (c o) -> p c o", o=512)[:, 0:NDIN, :],
                 wsrc.ap().rearrange("(c p) o -> p c o", p=128))
            k.bar()
            for tc in range(NTOKC):
                pbi = PB[tc % 2]
                for dc in range(NDIN):
                    P.matmul(pbi, xln_fm[:, dc * TOK + tc * 128: dc * TOK + (tc + 1) * 128],
                             slab[:, dc * 512:(dc + 1) * 512],
                             start=(dc == 0), stop=(dc == NDIN - 1))
                k.edge(P, V)
                k.wait_dma(V)
                if kind == "k":
                    V.tensor_copy(scratch[:, 0:512], pbi)
                    V.drain()
                    rope_tile(tc, KVH)
                    V.tensor_copy(scratch.ap().bitcast(BF16)[:, 2048:2048 + 512],
                                  scratch[:, 512:1024])
                    V.drain()
                    col0 = 0
                else:
                    V.tensor_copy(scratch.ap().bitcast(BF16)[:, 2048:2048 + 512], pbi)
                    V.drain()
                    col0 = KVH * DH
                k.edge(V, nc.sync, P)
                k.dh(AP(cc_kv_in, (tc * 128) * (2 * KVH * DH) + col0,
                        [[2 * KVH * DH, 128], [1, 512]]),
                     scratch.ap().bitcast(BF16)[:, 2048:2048 + 512])
            k.bar()
        k.wait_dma(G)
        G.collective_compute("AllGather", ALU.bypass,
                             replica_groups=[[0, 1], [2, 3], [4, 5], [6, 7]],
                             ins=[cc_kv_in.ap().opt()], outs=[cc_kv_out.ap().opt()]
                             ).then_inc(k.sem["cc"], 1)
        k.cnt["cc"] += 1
        for qtr in range(4):
            k.dg(slab.ap().rearrange("p (c o) -> p c o", o=512)[:, 0:NDIN, :],
                 Wq_b.ap().rearrange("(c p) (b o) -> p c b o", p=128, o=512)[:, :, qtr, :])
            k.bar()
            for tc in range(NTOKC):
                pbi = PB[tc % 2]
                for dc in range(NDIN):
                    P.matmul(pbi, xln_fm[:, dc * TOK + tc * 128: dc * TOK + (tc + 1) * 128],
                             slab[:, dc * 512:(dc + 1) * 512],
                             start=(dc == 0), stop=(dc == NDIN - 1))
                k.edge(P, V)
                V.tensor_copy(scratch[:, 0:512], pbi)
                V.drain()
                rope_tile(tc, 4)
                k.edge(V, P)
                for j in range(4):
                    hh_ = qtr * 4 + j
                    P.transpose(PB[4 + j % 2][:, 0:128], scratch[:, 512 + j * 128:512 + (j + 1) * 128],
                                ident_f[:, :])
                    k.edge(P, V)
                    V.tensor_copy(q_fm[:, hh_ * TOK + tc * 128: hh_ * TOK + (tc + 1) * 128],
                                  PB[4 + j % 2][:, 0:128])
                    k.edge(V, P)
            k.bar()
        sb.pop()
        sb.push()
        o_fm = sb.alloc("bofm", [128, HB * TOK], BF16)
        scratch = sb.alloc("bscr2", [128, 1536], F32)
        pbuf = sb.alloc("bp", [128, 2048], F32)
        pT = sb.alloc("bpT", [128, 2048], BF16)
        trim_s = sb.alloc("trims", [128, 128], F32)
        trir_s = sb.alloc("trirs", [128, 128], F32)
        bcols = sb.alloc("bcolsb", [128, 8], F32)
        wob_slab = sb.alloc("bwo", [128, NDIN * 512], BF16)
        k.dh(trim_s[:, :], diag_mask[:, :])
        k.dh(trir_s[:, :], tri_r[:, :])
        k.dh(bcols[:, 0:4], attnb_cols[:, :])
        V.memset(bcols[:, 6:7], NEG)
        V.drain()
        k.bar()
        for kt in range(16):
            sl_ = kt // 8
            r0 = (kt % 8) * 128
            k.dg(scratch[:, 0:512],
                 AP(cc_kv_out, (sl_ * TOK + r0) * (2 * KVH * DH), [[2 * KVH * DH, 128], [1, 512]]))
            k.bar()
            for j in range(KVH):
                P.transpose(PB[4][:, 0:128], scratch[:, j * 128:(j + 1) * 128], ident_f[:, :])
                k.edge(P, V)
                V.tensor_copy(k_fm[:, j * 2048 + kt * 128: j * 2048 + (kt + 1) * 128],
                              PB[4][:, 0:128])
                k.edge(V, P)
            k.bar()
        k.dg(v_tm.ap().rearrange("p (c o) -> p c o", o=512),
             AP(cc_kv_out, KVH * DH, [[2 * KVH * DH, 128], [128 * 2 * KVH * DH, 16], [1, 512]]))
        k.bar()
        for hh_ in range(HB):
            kv = hh_ // 4
            for qt in range(NTOKC):
                for kb in range(4):
                    P.matmul(PB[kb], q_fm[:, hh_ * TOK + qt * 128: hh_ * TOK + (qt + 1) * 128],
                             k_fm[:, kv * 2048 + kb * 512: kv * 2048 + (kb + 1) * 512],
                             start=True, stop=True)
                k.edge(P, V)
                V.tensor_tensor(psS[:, qt * 128:(qt + 1) * 128],
                                psS[:, qt * 128:(qt + 1) * 128], trim_s[:, :], ALU.add)
                V.tensor_tensor(psS[:, 1024 + qt * 128: 1024 + (qt + 1) * 128],
                                psS[:, 1024 + qt * 128: 1024 + (qt + 1) * 128],
                                trir_s[:, :], ALU.add)
                V.drain()
                if qt < 7:
                    nsfx = (7 - qt) * 128
                    V.tensor_scalar(psS[:, (qt + 1) * 128:(qt + 1) * 128 + nsfx],
                                    psS[:, (qt + 1) * 128:(qt + 1) * 128 + nsfx],
                                    bcols[:, 0:1], None, ALU.add)
                    V.tensor_scalar(psS[:, 1024 + (qt + 1) * 128: 1024 + (qt + 1) * 128 + nsfx],
                                    psS[:, 1024 + (qt + 1) * 128: 1024 + (qt + 1) * 128 + nsfx],
                                    bcols[:, 6:7], None, ALU.add)
                if qt > 0:
                    V.tensor_scalar(psS[:, 1024:1024 + qt * 128],
                                    psS[:, 1024:1024 + qt * 128],
                                    bcols[:, 1:2], None, ALU.add)
                V.drain()
                V.reduce_max(sc[:, 0:1], psS[:, 0:1024], axis=AX)
                V.reduce_max(sc[:, 1:2], psS[:, 1024:2048], axis=AX)
                V.drain()
                V.tensor_tensor(sc[:, 4:5], sc[:, 0:1], sc[:, 1:2], ALU.max)
                V.drain()
                V.tensor_scalar(sc[:, 5:6], sc[:, 4:5], -SCALE_B, None, ALU.mult)
                V.drain()
                k.edge(V, S)
                S.activation(pbuf[:, 0:1024], psS[:, 0:1024], ACT.Exp,
                             bias=sc[:, 5:6], scale=SCALE_B, accum_out=sc[:, 10:11])
                S.activation(pbuf[:, 1024:2048], psS[:, 1024:2048], ACT.Exp,
                             bias=sc[:, 5:6], scale=SCALE_B, accum_out=sc[:, 11:12])
                k.edge(S, V, P)
                V.tensor_tensor(sc[:, 12:13], sc[:, 10:11], sc[:, 11:12], ALU.add)
                V.drain()
                V.reciprocal(sc[:, 13:14], sc[:, 12:13])
                V.drain()
                V.tensor_scalar(pbuf[:, :], pbuf[:, :], sc[:, 13:14], None, ALU.mult)
                k.edge(V, P)
                for g2 in range(0, 16, 2):
                    P.transpose(PB[4][:, 0:128], pbuf[:, g2 * 128:(g2 + 1) * 128], ident_f[:, :])
                    P.transpose(PB[5][:, 0:128], pbuf[:, (g2 + 1) * 128:(g2 + 2) * 128], ident_f[:, :])
                    k.edge(P, V)
                    V.tensor_copy(pT[:, g2 * 128:(g2 + 1) * 128], PB[4][:, 0:128])
                    V.tensor_copy(pT[:, (g2 + 1) * 128:(g2 + 2) * 128], PB[5][:, 0:128])
                    k.edge(V, P)
                for kc in range(16):
                    P.matmul(PB[6][:, 0:128],
                             v_tm[:, kc * 512 + kv * 128: kc * 512 + (kv + 1) * 128],
                             pT[:, kc * 128:(kc + 1) * 128],
                             start=(kc == 0), stop=(kc == 15))
                k.edge(P, V)
                V.tensor_copy(o_fm[:, hh_ * TOK + qt * 128: hh_ * TOK + (qt + 1) * 128],
                              PB[6][:, 0:128])
                k.edge(V, P, S)
        k.bar()
        tap("o_attnb", o_fm[:, :])
        for qtr in range(4):
            k.dg(wob_slab.ap().rearrange("p (c o) -> p c o", o=512)[:, 0:NDIN, :],
                 Wo_b.ap().rearrange("(c p) (b o) -> p c b o", p=128, o=512)[:, :, qtr, :])
            k.bar()
            for tc in range(NTOKC):
                pbi = PB[tc % 2]
                for hc in range(HB):
                    P.matmul(pbi, o_fm[:, hc * TOK + tc * 128: hc * TOK + (tc + 1) * 128],
                             wob_slab[:, hc * 512:(hc + 1) * 512],
                             start=(hc == 0), stop=(hc == HB - 1))
                k.edge(P, V)
                V.tensor_tensor(resid[:, tc * D + qtr * 512: tc * D + qtr * 512 + 512],
                                resid[:, tc * D + qtr * 512: tc * D + qtr * 512 + 512],
                                pbi, ALU.add)
                k.edge(V, P)
            k.bar()
        sb.pop()
        sb.pop()
        tap("x3", resid[:, :])

    if upto >= 5:
        emit_ffn("f2")

    k.bar()
    k.dh(out_d.ap().rearrange("(c p) d -> p c d", p=128),
         resid.ap().rearrange("p (c d) -> p c d", d=D))
    k.wait_dma(nc.sync)
    return nc, dbg


def stage_inputs(inputs):
    f = lambda a: np.ascontiguousarray(a, dtype=np.float32)
    x = np.asarray(inputs["L0_attn_a_x"])
    Wq_a = f(np.asarray(inputs["L0_attn_a_Wq"]).reshape(D, HG, 2, DK // 2).transpose(1, 2, 0, 3))
    Wk_a = f(np.asarray(inputs["L0_attn_a_Wk"]).reshape(D, HG, 2, DK // 2).transpose(1, 2, 0, 3))
    Wv_a = f(np.asarray(inputs["L0_attn_a_Wv"]).reshape(D, HG, DV).transpose(1, 0, 2))
    Wnorm_rep = f(np.broadcast_to(np.asarray(inputs["L0_attn_a_Wnorm"])[None, :], (128, DV)))
    tri = (np.arange(CH)[:, None] <= np.arange(CH)[None, :]).astype(np.float32)
    tri2 = f(np.concatenate([tri, tri], axis=0))
    tri_causal = np.where(np.arange(128)[None, :] <= np.arange(128)[:, None], 0.0, NEG).astype(np.float32)
    zeros128 = np.zeros((128, 128), np.float32)
    negs128 = np.full((128, 128), NEG, np.float32)
    ident = np.eye(128, dtype=np.float32)
    inv = 1.0 / 10000.0 ** (np.arange(0, DH, 2, dtype=np.float64) / DH)
    posf = np.arange(T, dtype=np.float64)[:, None] * inv[None, :]
    cos_full = np.concatenate([np.cos(posf)] * 2, axis=-1).astype(np.float32)
    sin_full = np.concatenate([-np.sin(posf), np.sin(posf)], axis=-1).astype(np.float32)
    wg = np.asarray(inputs["L0_ffn_w_gate"]); wu = np.asarray(inputs["L0_ffn_w_up"])
    wdn = np.asarray(inputs["L0_ffn_w_down"])

    in_maps = []
    for core in range(8):
        b, hh = core // 2, core % 2
        m = {
            "x_own": f(x[b, hh * TOK:(hh + 1) * TOK, :]),
            "Wq_a": Wq_a, "Wk_a": Wk_a, "Wv_a": Wv_a,
            "Wg_a": f(np.asarray(inputs["L0_attn_a_Wg"]).reshape(D, 4, VD // 4).transpose(1, 0, 2)),
            "Wgk1": f(inputs["L0_attn_a_Wgk1"]), "Wgk2": f(inputs["L0_attn_a_Wgk2"]),
            "bgk2": f(np.asarray(inputs["L0_attn_a_bgk2"]).reshape(HG * NDKC, 128).T),
            "Wnorm_rep": Wnorm_rep,
            "Wo_a": f(np.asarray(inputs["L0_attn_a_Wo"]).reshape(VD, 4, D // 4).transpose(1, 0, 2)),
            "Wq_b": f(inputs["L0_attn_b_Wq"]), "Wk_b": f(inputs["L0_attn_b_Wk"]),
            "Wv_b": f(inputs["L0_attn_b_Wv"]), "Wo_b": f(inputs["L0_attn_b_Wo"]),
            "wg_sl": f(wg[core * (FI // 8):(core + 1) * (FI // 8), :]),
            "wu_sl": f(wu[core * (FI // 8):(core + 1) * (FI // 8), :]),
            "wd_sl": f(wdn[core * (D // 8):(core + 1) * (D // 8), :]),
            "cos_t": f(cos_full[hh * TOK:(hh + 1) * TOK]),
            "sin_t": f(sin_full[hh * TOK:(hh + 1) * TOK]),
            "gla_gate": np.full((128, 1), float(hh), np.float32),
            "attnb_bias": np.full((128, 1), 0.0 if hh else NEG, np.float32),
            "tri_gla": tri2, "ident_c": ident,
            "diag_mask": tri_causal if hh == 0 else zeros128,
            "tri_r": negs128 if hh == 0 else tri_causal,
            "attnb_cols": (np.array([[NEG, NEG, 0.0, 0.0]], np.float32) if hh == 0
                           else np.array([[0.0, 0.0, 0.0, 0.0]], np.float32)
                           ).repeat(128, axis=0),
        }
        in_maps.append(m)
    return in_maps


def kernel(**inputs):
    nc, _ = build()
    in_maps = stage_inputs(inputs)
    res = run_bass_kernel_spmd(nc, in_maps, core_ids=list(range(8)))
    outs = np.empty((B, T, D), np.float32)
    for core in range(8):
        b, hh = core // 2, core % 2
        outs[b, hh * TOK:(hh + 1) * TOK, :] = res.results[core]["out"]
    return outs


if __name__ == "__main__":
    nc, dbg = build()
    print("built", nc.n_instructions())


# version 05
# speedup vs baseline: 1.3227x; 1.3227x over previous
import numpy as np
import ml_dtypes
import concourse.bass as bass
import concourse.mybir as mybir
from concourse.ap import AP
from concourse.bass_utils import run_bass_kernel_spmd

F32 = mybir.dt.float32
BF16 = mybir.dt.bfloat16
I32 = mybir.dt.int32
I8 = mybir.dt.int8
ALU = mybir.AluOpType
ACT = mybir.ActivationFunctionType
AX = mybir.AxisListType.X

B, T, D = 4, 2048, 2048
TOK = 1024
NTOKC = TOK // 128
NDIN = D // 128
HG, DK, DV, CH = 4, 512, 256, 64
NCH = TOK // CH
NDKC = DK // 128
VD = HG * DV
NVDC = VD // 128
SCALE_GLA = float(DK) ** -0.5
S16 = 1.0 / 16.0
EPS = 1e-5
FI = 8192
HB, KVH, DH = 16, 4, 128
SCALE_B = float(1.0 / np.sqrt(np.float64(DH)))
NEG = -1e30
MM = 4096.0

DT_BYTES = {F32: 4, BF16: 2, I32: 4, I8: 1}


class SBuf:
    def __init__(self, nc):
        self.nc = nc
        self.base = (nc.sbuf_base + 31) & ~31
        self.limit = nc.sbuf_top
        self.off = self.base
        self.marks = []
        self.n = 0

    def alloc(self, name, shape, dtype):
        nbytes = int(np.prod(shape[1:])) * DT_BYTES[dtype]
        nbytes = (nbytes + 31) & ~31
        assert self.off + nbytes <= self.limit, (
            f"SBUF overflow at {name}: {self.off}+{nbytes} > {self.limit}")
        t = self.nc.alloc_sbuf_tensor_at(f"{name}_{self.n}", list(shape), dtype, offset=self.off)
        self.n += 1
        self.off += nbytes
        return t

    def push(self):
        self.marks.append(self.off)

    def pop(self):
        self.off = self.marks.pop()


class K:
    def __init__(self, nc):
        self.nc = nc
        self.sb = SBuf(nc)
        self.cnt = {}
        self.sem = {}
        for name in ("h", "g", "cc", "pe", "ve", "se", "ge"):
            self.sem[name] = nc.alloc_semaphore(f"sem_{name}")
            self.cnt[name] = 0
        self.ename = {}

    def _e(self, eng):
        n = {"PE": "pe", "DVE": "ve", "Activation": "se", "Pool": "ge"}
        return n[str(eng.engine.value) if hasattr(eng, "engine") else "?"]

    def edge(self, prod, *cons):
        import concourse.mybir as mb
        emap = {mb.EngineType.PE: "pe", mb.EngineType.DVE: "ve",
                mb.EngineType.Activation: "se", mb.EngineType.Pool: "ge"}
        name = emap[prod.engine]
        self.cnt[name] += 1
        prod.drain().then_inc(self.sem[name], 1)
        for c in cons:
            c.wait_ge(self.sem[name], self.cnt[name])

    def dh(self, out, in_):
        self.nc.sync.dma_start(out=out, in_=in_).then_inc(self.sem["h"], 16)
        self.cnt["h"] += 16

    def dg(self, out, in_):
        self.nc.gpsimd.dma_start(out=out, in_=in_).then_inc(self.sem["g"], 16)
        self.cnt["g"] += 16

    def wait_dma(self, *engines):
        for e in engines:
            e.wait_ge(self.sem["h"], self.cnt["h"])
            e.wait_ge(self.sem["g"], self.cnt["g"])

    def bar(self):
        nc = self.nc
        self.wait_dma(nc.sync, nc.gpsimd)
        nc.all_engine_barrier()


def build(taps=(), upto=99):
    nc = bass.Bass("TRN2", target_bir_lowering=False)
    k = K(nc)
    sb = k.sb
    V, S, P, G = nc.vector, nc.scalar, nc.tensor, nc.gpsimd

    di = lambda n, shp, dt=F32: nc.dram_tensor(n, shp, dt, kind="ExternalInput")
    x_own = di("x_own", [TOK, D])
    out_d = nc.dram_tensor("out", [TOK, D], F32, kind="ExternalOutput")
    Wq_a = di("Wq_a", [HG, 2, D, DK // 2]); Wk_a = di("Wk_a", [HG, 2, D, DK // 2])
    Wv_a = di("Wv_a", [HG, D, DV]); Wg_a = di("Wg_a", [4, D, VD // 4])
    Wgk1 = di("Wgk1", [D, 16]); Wgk2 = di("Wgk2", [16, D]); bgk2 = di("bgk2", [128, HG * NDKC])
    Wnorm_rep = di("Wnorm_rep", [128, DV]); Wo_a = di("Wo_a", [4, VD, D // 4])
    Wq_b = di("Wq_b", [D, D]); Wk_b = di("Wk_b", [D, KVH * DH])
    Wv_b = di("Wv_b", [D, KVH * DH]); Wo_b = di("Wo_b", [D, D])
    wg_sl = di("wg_sl", [FI // 8, D]); wu_sl = di("wu_sl", [FI // 8, D])
    wd_sl = di("wd_sl", [D // 8, FI])
    cos_t = di("cos_t", [TOK, DH]); sin_t = di("sin_t", [TOK, DH])
    gla_gate = di("gla_gate", [128, 1])
    attnb_bias = di("attnb_bias", [128, 1])
    tri_gla = di("tri_gla", [128, CH])
    diag_mask = di("diag_mask", [128, 128])
    tri_r = di("tri_r", [128, 128])
    attnb_cols = di("attnb_cols", [128, 4])
    ident_c = di("ident_c", [128, 128])

    cc_ws_in = nc.dram_tensor("cc_ws_in", [128, 4], F32)
    cc_ws_out = nc.dram_tensor("cc_ws_out", [128, 4], F32)
    WQ_G = D * (FI // 8)
    WQ_SL = 2 * WQ_G + FI * (D // 8)
    cc_wq_in = nc.dram_tensor("cc_wq_in", [WQ_SL], BF16)
    cc_wq_out = nc.dram_tensor("cc_wq_out", [8, WQ_SL], BF16)
    cc_S_in = nc.dram_tensor("cc_S_in", [HG * NDKC * 128, DV], F32)
    cc_S_out = nc.dram_tensor("cc_S_out", [2, HG * NDKC * 128, DV], F32)
    u_spill = nc.dram_tensor("u_spill", [TOK, FI], F32)
    cc_kv_in = nc.dram_tensor("cc_kv_in", [TOK, 2 * KVH * DH], BF16)
    cc_kv_out = nc.dram_tensor("cc_kv_out", [2, TOK, 2 * KVH * DH], BF16)

    dbg = {}

    def tap(name, a):
        if name not in taps:
            return
        d = nc.dram_tensor(f"dbg_{name}", list(a.shape), a.dtype, kind="ExternalOutput")
        k.bar()
        k.dg(d.ap(), a)
        k.bar()
        dbg[name] = f"dbg_{name}"

    resid = sb.alloc("resid", [128, NTOKC * D], F32)
    ident_f = sb.alloc("ident_f", [128, 128], F32)
    sc = sb.alloc("smallcols", [128, 64], F32)

    psS_t = nc.alloc_psum_tensor("psS", [128, 2048], F32)
    pb_hi = [nc.alloc_psum_tensor(f"pbk{i}", [128, 512], F32) for i in range(4, 8)]
    PB = [psS_t.ap()[:, i * 512:(i + 1) * 512] for i in range(4)] + [t.ap() for t in pb_hi]
    psS = psS_t.ap()

    k.dh(resid.ap().rearrange("p (c d) -> p c d", d=D),
         x_own.ap().rearrange("(c p) d -> p c d", p=128))
    k.dh(ident_f[:, :], ident_c[:, :])
    V.memset(sc[:, 60:61], EPS)
    V.memset(sc[:, 61:62], 1.0)
    V.drain()
    k.bar()

    def ln_to_fm(xln_fm, xf_tile, scratch):
        for tc in range(NTOKC):
            xt = resid[:, tc * D:(tc + 1) * D]
            V.reduce_sum(sc[:, 0:1], xt, axis=AX)
            V.scalar_tensor_tensor(scratch[:, 0:D], xt, 1.0, xt, ALU.bypass, ALU.mult,
                                   accum_out=sc[:, 1:2])
            V.drain()
            V.tensor_scalar(sc[:, 2:3], sc[:, 0:1], 1.0 / D, None, ALU.mult)
            V.drain()
            V.tensor_tensor(sc[:, 3:4], sc[:, 2:3], sc[:, 2:3], ALU.mult)
            V.tensor_scalar(sc[:, 4:5], sc[:, 1:2], 1.0 / D, None, ALU.mult)
            V.drain()
            V.tensor_tensor(sc[:, 5:6], sc[:, 4:5], sc[:, 3:4], ALU.subtract)
            k.edge(V, S)
            S.activation(sc[:, 6:7], sc[:, 5:6], ACT.Ln, bias=sc[:, 60:61], scale=1.0)
            S.drain()
            S.activation(sc[:, 7:8], sc[:, 6:7], ACT.Exp, bias=0.0, scale=-0.5)
            k.edge(S, V)
            V.tensor_scalar(xf_tile[:, 0:D], xt, sc[:, 2:3], sc[:, 7:8],
                            ALU.subtract, ALU.mult)
            k.edge(V, P)
            for dc in range(0, NDIN, 2):
                P.transpose(PB[6][:, 0:128], xf_tile[:, dc * 128:(dc + 1) * 128], ident_f[:, :])
                P.transpose(PB[7][:, 0:128], xf_tile[:, (dc + 1) * 128:(dc + 2) * 128], ident_f[:, :])
                k.edge(P, V)
                V.tensor_copy(xln_fm[:, dc * TOK + tc * 128: dc * TOK + (tc + 1) * 128],
                              PB[6][:, 0:128])
                V.tensor_copy(xln_fm[:, (dc + 1) * TOK + tc * 128: (dc + 1) * TOK + (tc + 1) * 128],
                              PB[7][:, 0:128])
                k.edge(V, P)
        k.bar()

    mats = [(wg_sl, FI // 8, D, 0), (wu_sl, FI // 8, D, WQ_G), (wd_sl, D // 8, FI, 2 * WQ_G)]
    if upto >= 1:
        sb.push()
        ones_f = sb.alloc("ones_f", [128, 128], F32)
        V.memset(ones_f[:, :], 1.0)
        V.drain()
        wbuf = sb.alloc("wsw", [128, 8192], F32)
        wqtstage = sb.alloc("wqtstage", [128, 16384], F32)
        c1buf = sb.alloc("c1buf", [128, 16384], BF16)
        for mi, (wd, rows, cols, _off) in enumerate(mats):
            nrc = rows // 128
            for hf in range(2):
                k.dh(wbuf.ap().rearrange("p (c d) -> p c d", d=cols)[:, 0:nrc // 2, :],
                     wd.ap().rearrange("(c p) d -> p c d", p=128)[:, hf * (nrc // 2):(hf + 1) * (nrc // 2), :])
                k.wait_dma(V)
                V.reduce_sum(sc[:, 12 + hf:13 + hf], wbuf[:, 0:(nrc // 2) * cols], axis=AX,
                             apply_absolute_value=True)
                k.edge(V, nc.sync)
            V.tensor_tensor(sc[:, 8 + mi:9 + mi], sc[:, 12:13], sc[:, 13:14], ALU.add)
            V.drain()
        V.memset(sc[:, 11:12], 0.0)
        k.edge(V, nc.sync)
        k.dh(cc_ws_in.ap(), sc[:, 8:12])
        k.wait_dma(G)
        G.collective_compute("AllReduce", ALU.add, replica_groups=[list(range(8))],
                             ins=[cc_ws_in.ap().opt()], outs=[cc_ws_out.ap().opt()]
                             ).then_inc(k.sem["cc"], 1)
        k.cnt["cc"] += 1
        G.wait_ge(k.sem["cc"], k.cnt["cc"])
        k.dg(sc[:, 16:20], cc_ws_out.ap())
        k.bar()
        P.matmul(PB[0][:, 0:4], ones_f[:, :], sc[:, 16:20], start=True, stop=True)
        k.edge(P, V)
        for mi, (wd, rows, cols, _off) in enumerate(mats):
            n_el = rows * 8 * cols
            V.tensor_scalar(sc[:, 24 + mi:25 + mi], PB[0][:, mi:mi + 1], 1.0 / n_el, None,
                            ALU.mult)
        V.drain()
        k.bar()
        for mi, (wd, rows, cols, _off) in enumerate(mats):
            nrc = rows // 128
            for hf in range(2):
                k.dh(wbuf.ap().rearrange("p (c d) -> p c d", d=cols)[:, 0:nrc // 2, :],
                     wd.ap().rearrange("(c p) d -> p c d", p=128)[:, hf * (nrc // 2):(hf + 1) * (nrc // 2), :])
                k.wait_dma(S, V)
                half_n = (nrc // 2) * cols
                S.activation(wqtstage[:, 0:half_n], wbuf[:, 0:half_n], ACT.Abs,
                             bias=0.0, scale=1.0)
                k.edge(S, V)
                V.tensor_scalar(wqtstage[:, 0:half_n], wqtstage[:, 0:half_n],
                                sc[:, 24 + mi:25 + mi], None, ALU.subtract)
                V.drain()
                V.reduce_sum(sc[:, 12 + hf:13 + hf], wqtstage[:, 0:half_n], axis=AX)
                k.edge(V, nc.sync)
            V.tensor_tensor(sc[:, 8 + mi:9 + mi], sc[:, 12:13], sc[:, 13:14], ALU.add)
            V.drain()
        V.memset(sc[:, 11:12], 0.0)
        k.edge(V, nc.sync)
        k.dh(cc_ws_in.ap(), sc[:, 8:12])
        k.wait_dma(G)
        G.collective_compute("AllReduce", ALU.add, replica_groups=[list(range(8))],
                             ins=[cc_ws_in.ap().opt()], outs=[cc_ws_out.ap().opt()]
                             ).then_inc(k.sem["cc"], 1)
        k.cnt["cc"] += 1
        G.wait_ge(k.sem["cc"], k.cnt["cc"])
        k.dg(sc[:, 16:20], cc_ws_out.ap())
        k.bar()
        P.matmul(PB[0][:, 0:4], ones_f[:, :], sc[:, 16:20], start=True, stop=True)
        k.edge(P, V)
        for mi, (wd, rows, cols, _off) in enumerate(mats):
            n_el = rows * 8 * cols
            V.tensor_scalar(sc[:, 28 + mi:29 + mi], PB[0][:, mi:mi + 1], 1.0 / n_el, None,
                            ALU.mult)
        V.drain()
        V.tensor_tensor(sc[:, 28:31], sc[:, 28:31], sc[:, 24:27], ALU.add)
        V.drain()
        V.tensor_scalar(sc[:, 28:31], sc[:, 28:31], 1.0, EPS, ALU.mult, ALU.add)
        V.drain()
        V.tensor_scalar(sc[:, 20:23], sc[:, 28:31], 0.5, None, ALU.mult)
        V.tensor_scalar(sc[:, 24:27], sc[:, 28:31], -0.5, None, ALU.mult)
        V.drain()
        k.bar()
        for mi, (wd, rows, cols, off) in enumerate(mats):
            nrc = rows // 128
            ncolc = cols // 128
            half_n = (nrc // 2) * cols
            for hf in range(2):
                k.dh(wbuf.ap().rearrange("p (c d) -> p c d", d=cols)[:, 0:nrc // 2, :],
                     wd.ap().rearrange("(c p) d -> p c d", p=128)[:, hf * (nrc // 2):(hf + 1) * (nrc // 2), :])
                k.wait_dma(V)
                V.tensor_scalar(wqtstage[:, hf * half_n:(hf + 1) * half_n], wbuf[:, 0:half_n],
                                sc[:, 20 + mi:21 + mi], None, ALU.is_ge)
                V.drain()
                V.scalar_tensor_tensor(wqtstage[:, hf * half_n:(hf + 1) * half_n], wbuf[:, 0:half_n],
                                       sc[:, 24 + mi:25 + mi], wqtstage[:, hf * half_n:(hf + 1) * half_n],
                                       ALU.is_le, ALU.subtract)
                k.edge(V, nc.sync)
            k.edge(V, P)
            tiles = [(cc_i, rc) for cc_i in range(ncolc) for rc in range(nrc)]
            for i0 in range(0, len(tiles), 4):
                grp = tiles[i0:i0 + 4]
                for j, (cc_i, rc) in enumerate(grp):
                    P.transpose(PB[j][:, 0:128],
                                wqtstage[:, rc * cols + cc_i * 128: rc * cols + cc_i * 128 + 128],
                                ident_f[:, :])
                k.edge(P, V)
                for j, (cc_i, rc) in enumerate(grp):
                    V.tensor_copy(c1buf[:, cc_i * rows + rc * 128: cc_i * rows + (rc + 1) * 128],
                                  PB[j][:, 0:128])
                k.edge(V, P)
            k.edge(V, nc.sync)
            k.dh(AP(cc_wq_in, off, [[rows, 128], [128 * rows, ncolc], [1, rows]]),
                 c1buf.ap().rearrange("p (a r) -> p a r", r=rows)[:, 0:ncolc, :])
            k.bar()
        k.wait_dma(G)
        G.collective_compute("AllGather", ALU.bypass, replica_groups=[list(range(8))],
                             ins=[cc_wq_in.ap().opt()], outs=[cc_wq_out.ap().opt()]
                             ).then_inc(k.sem["cc"], 1)
        k.cnt["cc"] += 1
        sb.pop()
        tap("wstat", sc[:, 0:32])

    if upto >= 2:
        sb.push()
        tri_s = sb.alloc("tri_s", [128, CH], F32)
        wnorm_s = sb.alloc("wnorm_s", [128, DV], F32)
        gate_s = sb.alloc("gate_s", [128, 1], F32)
        k.dh(tri_s[:, :], tri_gla[:, :])
        k.dh(wnorm_s[:, :], Wnorm_rep[:, :])
        k.dh(gate_s[:, :], gla_gate[:, :])
        xln_fm = sb.alloc("xln_fm", [128, NDIN * TOK], BF16)
        scratch = sb.alloc("lnscr", [128, 2048], F32)
        qdec = sb.alloc("qdec", [128, HG * NDKC * TOK], BF16)
        o_all = sb.alloc("o_all", [128, NTOKC * VD], BF16)
        g_all = sb.alloc("g_all", [128, NTOKC * VD], BF16)
        kbuf = sb.alloc("kbuf", [128, NDKC * TOK], BF16)
        vbuf = sb.alloc("vbuf", [128, NTOKC * DV], BF16)
        v_odd = sb.alloc("v_odd", [64, NTOKC * DV], BF16)
        Sf = sb.alloc("Sf", [128, NDKC * DV], F32)
        Sb = sb.alloc("Sb", [128, NDKC * DV], BF16)
        t_fm = sb.alloc("t_fm", [16, TOK], BF16)
        wgk1_s = sb.alloc("wgk1_s", [128, NDIN * 16], BF16)
        wgk2_s = sb.alloc("wgk2_s", [16, D], BF16)
        wbuf0 = sb.alloc("whead0", [128, NDIN * 256], BF16)

        kkT = sb.alloc("kkT", [128, DK], BF16)
        abuf = sb.alloc("abuf", [128, CH], BF16)
        nbg = sb.alloc("nbg", [128, HG * NDKC], F32)
        bgf = sb.alloc("bgf", [128, HG * NDKC], F32)

        ln_to_fm(xln_fm, scratch, scratch)
        tap("xln_fm", xln_fm[:, :])

        k.dh(bgf[:, :], bgk2[:, :])
        k.wait_dma(V)
        V.tensor_scalar(nbg[:, :], bgf[:, :], -1.0, None, ALU.mult)
        V.drain()
        k.dg(wgk1_s.ap().rearrange("p (c o) -> p c o", o=16),
             Wgk1.ap().rearrange("(c p) o -> p c o", p=128))
        k.dg(wgk2_s[:, :], Wgk2[:, :])
        k.bar()
        for half in range(2):
            for dc in range(NDIN):
                P.matmul(PB[4][0:16, 0:512], wgk1_s[:, dc * 16:(dc + 1) * 16],
                         xln_fm[:, dc * TOK + half * 512: dc * TOK + (half + 1) * 512],
                         start=(dc == 0), stop=(dc == NDIN - 1))
            k.edge(P, V)
            V.tensor_copy(t_fm[0:16, half * 512:(half + 1) * 512], PB[4][0:16, 0:512])
            k.edge(V, P)
        k.bar()
        tap("t_fm", t_fm[:, :])

        for h in range(HG):
            qh = qdec[:, h * NDKC * TOK:(h + 1) * NDKC * TOK]
            for dst, wsrc, scale_, buf in ((qh, Wq_a, SCALE_GLA, "q"), (None, Wk_a, None, "k")):
                for chf in range(2):
                    k.dg(wbuf0.ap().rearrange("p (c o) -> p c o", o=256),
                         wsrc.ap()[h, chf, :, :].rearrange("(c p) o -> p c o", p=128))
                    k.bar()
                    for dkc in (chf * 2, chf * 2 + 1):
                        dkl = dkc - chf * 2
                        for tb in range(2):
                            pbi = PB[tb]
                            for dc in range(NDIN):
                                P.matmul(pbi[:, :],
                                         wbuf0[:, dc * 256 + dkl * 128: dc * 256 + (dkl + 1) * 128],
                                         xln_fm[:, dc * TOK + tb * 512: dc * TOK + (tb + 1) * 512],
                                         start=(dc == 0), stop=(dc == NDIN - 1))
                        k.edge(P, V)
                        for tb in range(2):
                            if buf == "q":
                                V.tensor_scalar(dst[:, dkc * TOK + tb * 512: dkc * TOK + (tb + 1) * 512],
                                                PB[tb][:, :], scale_, None, ALU.mult)
                            else:
                                V.tensor_copy(kbuf[:, dkc * TOK + tb * 512: dkc * TOK + (tb + 1) * 512],
                                              PB[tb][:, :])
                        k.edge(V, P)
                    k.bar()
            k.dg(wbuf0.ap().rearrange("p (c o) -> p c o", o=DV),
                 Wv_a.ap()[h, :, :].rearrange("(c p) o -> p c o", p=128))
            k.bar()
            for tc in range(NTOKC):
                pbi = PB[tc % 2]
                for dc in range(NDIN):
                    P.matmul(pbi[:, 0:DV],
                             xln_fm[:, dc * TOK + tc * 128: dc * TOK + (tc + 1) * 128],
                             wbuf0[:, dc * DV:(dc + 1) * DV],
                             start=(dc == 0), stop=(dc == NDIN - 1))
                k.edge(P, V)
                V.tensor_copy(vbuf[:, tc * DV:(tc + 1) * DV], pbi[:, 0:DV])
                k.edge(V, P)
            k.bar()
            k.dh(v_odd[0:64, :], vbuf[64:128, :])
            k.bar()
            if h == 0:
                tap("q0", qdec[:, 0:NDKC * TOK])
                tap("k0", kbuf[:, :])
                tap("v0", vbuf[:, :])
            lsq = scratch[:, 1536:1536 + NDKC * CH]
            bfm = scratch[:, 1792:1792 + NDKC * CH]
            scr_bf = scratch.ap().bitcast(BF16)
            qet = scr_bf[:, 512:512 + NDKC * CH]
            kdt = scr_bf[:, 768:768 + NDKC * CH]
            ebt = scratch[:, 768:768 + NDKC * CH]
            enbt = scratch[:, 1024:1024 + NDKC * CH]
            kkf = scratch[:, 1280:1280 + NDKC * CH]
            V.memset(Sf[:, :], 0.0)
            V.memset(Sb[:, :], 0.0)
            V.memset(sc[:, 32:36], 0.0)
            V.drain()
            k.bar()
            for c in range(NCH):
                od = (c % 2) * 64
                tc = c // 2
                for dkc in range(NDKC):
                    P.matmul(PB[2][:, dkc * CH:(dkc + 1) * CH],
                             wgk2_s[0:16, (h * DK + dkc * 128):(h * DK + dkc * 128) + 128],
                             t_fm[0:16, c * CH:(c + 1) * CH], start=True, stop=True)
                k.edge(P, S)
                zabs = scratch[:, 0:NDKC * CH]
                rmin = scratch[:, 512:512 + NDKC * CH]
                for dkc in range(NDKC):
                    S.activation(zabs[:, dkc * CH:(dkc + 1) * CH],
                                 PB[2][:, dkc * CH:(dkc + 1) * CH],
                                 ACT.Abs, bias=bgf[:, h * NDKC + dkc: h * NDKC + dkc + 1], scale=1.0)
                    S.activation(rmin[:, dkc * CH:(dkc + 1) * CH],
                                 PB[2][:, dkc * CH:(dkc + 1) * CH],
                                 ACT.Relu, bias=nbg[:, h * NDKC + dkc: h * NDKC + dkc + 1], scale=-1.0)
                S.drain()
                S.activation(zabs, zabs, ACT.Exp, bias=0.0, scale=-1.0)
                S.drain()
                S.activation(zabs, zabs, ACT.Ln, bias=sc[:, 61:62], scale=1.0)
                k.edge(S, V, P)
                V.scalar_tensor_tensor(lsq, rmin, -1.0, zabs, ALU.mult, ALU.subtract)
                V.drain()
                for dkc in range(NDKC):
                    V.tensor_tensor_scan(bfm[:, dkc * CH:(dkc + 1) * CH],
                                         lsq[:, dkc * CH:(dkc + 1) * CH],
                                         lsq[:, dkc * CH:(dkc + 1) * CH], 0.0, ALU.add, ALU.bypass)
                k.edge(V, S)
                S.activation(ebt, bfm, ACT.Exp, bias=0.0, scale=S16)
                S.activation(enbt, bfm, ACT.Exp, bias=0.0, scale=-S16)
                blast_ap = bfm.rearrange("p (a s) -> p a s", s=CH)[:, :, CH - 1:CH].opt()
                S.activation(sc[:, 36:40], blast_ap, ACT.Exp, bias=0.0, scale=S16)
                S.activation(sc[:, 40:44], sc[:, 32:36], ACT.Exp, bias=0.0, scale=S16)
                k.edge(S, V)
                V.tensor_tensor(sc[:, 32:36], sc[:, 32:36], blast_ap, ALU.add)
                V.drain()
                q_ap = AP(qdec, h * NDKC * TOK + c * CH, [[HG * NDKC * TOK, 128], [TOK, NDKC], [1, CH]])
                k_ap = AP(kbuf, c * CH, [[NDKC * TOK, 128], [TOK, NDKC], [1, CH]])
                qe3 = qet.rearrange("p (a s) -> p a s", s=CH)
                V.tensor_tensor(qe3, q_ap, ebt.rearrange("p (a s) -> p a s", s=CH), ALU.mult)
                V.tensor_tensor(kdt.rearrange("p (a s) -> p a s", s=CH), k_ap,
                                enbt.rearrange("p (a s) -> p a s", s=CH), ALU.mult)
                V.drain()
                for dkc in range(NDKC):
                    V.scalar_tensor_tensor(kkf[:, dkc * CH:(dkc + 1) * CH],
                                           kdt[:, dkc * CH:(dkc + 1) * CH],
                                           sc[:, 36 + dkc:37 + dkc],
                                           kdt[:, dkc * CH:(dkc + 1) * CH],
                                           ALU.mult, ALU.bypass)
                V.drain()
                for dkc in range(NDKC):
                    V.tensor_scalar(AP(qdec, h * NDKC * TOK + dkc * TOK + c * CH,
                                       [[HG * NDKC * TOK, 128], [1, CH]]),
                                    qet[:, dkc * CH:(dkc + 1) * CH],
                                    sc[:, 40 + dkc:41 + dkc], None, ALU.mult)
                k.edge(V, P)
                for dkc in range(NDKC):
                    P.matmul(PB[3][od:od + 64, 0:64], kdt[:, dkc * CH:(dkc + 1) * CH],
                             qet[:, dkc * CH:(dkc + 1) * CH],
                             start=(dkc == 0), stop=(dkc == NDKC - 1), tile_position=(0, od))
                for dkc in range(NDKC):
                    P.transpose(PB[4][0:64, dkc * 128:(dkc + 1) * 128],
                                kkf[:, dkc * CH:(dkc + 1) * CH], ident_f[:, :])
                k.edge(P, V)
                V.tensor_tensor(abuf[od:od + 64, :], PB[3][od:od + 64, 0:64],
                                tri_s[od:od + 64, :], ALU.mult)
                V.tensor_copy(kkT[0:64, :], PB[4][0:64, :])
                k.edge(V, P)
                P.matmul(PB[5][od:od + 64, 0:DV], abuf[od:od + 64, :],
                         vbuf[od:od + 64, tc * DV:(tc + 1) * DV], start=True, stop=False,
                         tile_position=(od, od))
                for dkc in range(NDKC):
                    P.matmul(PB[5][od:od + 64, 0:DV], qet[:, dkc * CH:(dkc + 1) * CH],
                             Sb[:, dkc * DV:(dkc + 1) * DV], start=False,
                             stop=(dkc == NDKC - 1), tile_position=(0, od))
                v_u = vbuf[0:64, tc * DV:(tc + 1) * DV] if od == 0 else \
                    v_odd[0:64, tc * DV:(tc + 1) * DV]
                for dkc in range(NDKC):
                    P.matmul(PB[6 + dkc // 2][:, (dkc % 2) * DV:(dkc % 2 + 1) * DV],
                             kkT[0:64, dkc * 128:(dkc + 1) * 128],
                             v_u, start=True, stop=True, tile_position=(0, 0))
                k.edge(P, V)
                V.tensor_copy(o_all[od:od + 64, tc * VD + h * DV: tc * VD + (h + 1) * DV],
                              PB[5][od:od + 64, 0:DV])
                for dkc in range(NDKC):
                    V.scalar_tensor_tensor(Sf[:, dkc * DV:(dkc + 1) * DV],
                                           Sf[:, dkc * DV:(dkc + 1) * DV],
                                           sc[:, 36 + dkc:37 + dkc],
                                           PB[6 + dkc // 2][:, (dkc % 2) * DV:(dkc % 2 + 1) * DV],
                                           ALU.mult, ALU.add)
                V.drain()
                V.tensor_copy(Sb[:, :], Sf[:, :])
                k.edge(V, P)
            k.bar()
            k.dh(cc_S_in.ap().rearrange("(a p) v -> p a v", p=128)[:, h * NDKC:(h + 1) * NDKC, :],
                 Sf.ap().rearrange("p (a v) -> p a v", v=DV))
            k.bar()
            if h == 0:
                tap("S0f", Sf[:, :])
                tap("o_h0", o_all[:, 0:VD])
        k.wait_dma(G)
        G.collective_compute("AllGather", ALU.bypass,
                             replica_groups=[[0, 1], [2, 3], [4, 5], [6, 7]],
                             ins=[cc_S_in.ap().opt()], outs=[cc_S_out.ap().opt()]
                             ).then_inc(k.sem["cc"], 1)
        k.cnt["cc"] += 1
        for qtr in range(4):
            k.dg(wbuf0.ap().rearrange("p (c o) -> p c o", o=256)[:, 0:NDIN, :],
                 Wg_a.ap()[qtr, :, :].rearrange("(c p) o -> p c o", p=128))
            k.bar()
            for tc in range(NTOKC):
                pbi = PB[tc % 2]
                for dc in range(NDIN):
                    P.matmul(pbi[:, 0:256], xln_fm[:, dc * TOK + tc * 128: dc * TOK + (tc + 1) * 128],
                             wbuf0[:, dc * 256:(dc + 1) * 256],
                             start=(dc == 0), stop=(dc == NDIN - 1))
                k.edge(P, V)
                V.tensor_copy(g_all[:, tc * VD + qtr * 256: tc * VD + (qtr + 1) * 256],
                              pbi[:, 0:256])
                k.edge(V, P)
            k.bar()
        tap("g_all", g_all[:, :])
        G.wait_ge(k.sem["cc"], k.cnt["cc"])
        s0buf = kbuf
        for h in range(HG):
            k.dg(s0buf.ap().rearrange("p (a v) -> p a v", v=DV)[:, 0:NDKC, :],
                 cc_S_out.ap()[0, :, :].rearrange("(a p) v -> p a v", p=128)
                 [:, h * NDKC:(h + 1) * NDKC, :])
            k.bar()
            qh = qdec[:, h * NDKC * TOK:(h + 1) * NDKC * TOK]
            for tc in range(NTOKC):
                pbi = PB[2 + tc % 2]
                for dkc in range(NDKC):
                    P.matmul(pbi[:, 0:DV], qh[:, dkc * TOK + tc * 128: dkc * TOK + (tc + 1) * 128],
                             s0buf[:, dkc * DV:(dkc + 1) * DV],
                             start=(dkc == 0), stop=(dkc == NDKC - 1))
                k.edge(P, V)
                dst = o_all[:, tc * VD + h * DV: tc * VD + (h + 1) * DV]
                V.scalar_tensor_tensor(dst, pbi[:, 0:DV], gate_s[:, 0:1], dst, ALU.mult, ALU.add)
                k.edge(V, P)
            k.bar()
        k.bar()
        tap("o_corr", o_all[:, :])
        z_fm = qdec
        for tc in range(NTOKC):
            for h in range(HG):
                o_sl = o_all[:, tc * VD + h * DV: tc * VD + (h + 1) * DV]
                g_sl = g_all[:, tc * VD + h * DV: tc * VD + (h + 1) * DV]
                V.scalar_tensor_tensor(scratch[:, 0:DV], o_sl, 1.0, o_sl, ALU.bypass, ALU.mult,
                                       accum_out=sc[:, 44:45])
                k.edge(V, S)
                S.activation(sc[:, 45:46], sc[:, 44:45], ACT.Ln, bias=sc[:, 60:61], scale=1.0 / DV)
                S.drain()
                S.activation(sc[:, 46:47], sc[:, 45:46], ACT.Exp, bias=0.0, scale=-0.5)
                S.activation(scratch[:, 1024:1024 + DV], g_sl, ACT.Silu, bias=0.0, scale=1.0)
                k.edge(S, V)
                V.tensor_scalar(scratch[:, 0:DV], o_sl, sc[:, 46:47], None, ALU.mult)
                V.drain()
                V.tensor_tensor(scratch[:, 0:DV], scratch[:, 0:DV], wnorm_s[:, :], ALU.mult)
                V.drain()
                V.tensor_tensor(scratch[:, 256:256 + DV], scratch[:, 0:DV],
                                scratch[:, 1024:1024 + DV], ALU.mult)
                k.edge(V, P)
                for j in range(2):
                    vc = 2 * h + j
                    pbi = PB[j]
                    P.transpose(pbi[:, 0:128], scratch[:, 256 + j * 128:256 + (j + 1) * 128],
                                ident_f[:, :])
                k.edge(P, V)
                for j in range(2):
                    vc = 2 * h + j
                    V.tensor_copy(z_fm[:, vc * TOK + tc * 128: vc * TOK + (tc + 1) * 128],
                                  PB[j][:, 0:128])
                k.edge(V, P)
        k.bar()
        tap("z_fm", z_fm[:, 0:NVDC * TOK])
        for qtr in range(4):
            k.dg(wbuf0.ap().rearrange("p (c o) -> p c o", o=512)[:, 0:NVDC, :],
                 Wo_a.ap()[qtr, :, :].rearrange("(c p) o -> p c o", p=128))
            k.bar()
            for tc in range(NTOKC):
                pbi = PB[tc % 2]
                for vc in range(NVDC):
                    P.matmul(pbi[:, :], z_fm[:, vc * TOK + tc * 128: vc * TOK + (tc + 1) * 128],
                             wbuf0[:, vc * 512:(vc + 1) * 512],
                             start=(vc == 0), stop=(vc == NVDC - 1))
                k.edge(P, V)
                V.tensor_tensor(resid[:, tc * D + qtr * 512: tc * D + qtr * 512 + 512],
                                resid[:, tc * D + qtr * 512: tc * D + qtr * 512 + 512],
                                pbi[:, :], ALU.add)
                k.edge(V, P)
            k.bar()
        k.bar()
        sb.pop()
        tap("x1", resid[:, :])




    def emit_ffn(tag):
        sb.push()
        xq_fm = sb.alloc(f"xqfm{tag}", [128, NDIN * TOK], BF16)
        scratch = sb.alloc(f"fscr{tag}", [128, D], F32)
        qscr = sb.alloc(f"qscr{tag}", [128, D], I32)
        slab_g = sb.alloc(f"slabg{tag}", [128, 2 * NDIN * 512], BF16)
        slab_u = sb.alloc(f"slabu{tag}", [128, 2 * NDIN * 512], BF16)
        sgt = sb.alloc(f"sgt{tag}", [128, 2 * 512], BF16)
        ut = sb.alloc(f"ut{tag}", [128, 2 * 512], F32)
        for tc in range(NTOKC):
            xt = resid[:, tc * D:(tc + 1) * D]
            V.reduce_sum(sc[:, 0:1], xt, axis=AX)
            V.scalar_tensor_tensor(scratch[:, 0:D], xt, 1.0, xt, ALU.bypass, ALU.mult,
                                   accum_out=sc[:, 1:2])
            V.drain()
            V.tensor_scalar(sc[:, 2:3], sc[:, 0:1], 1.0 / D, None, ALU.mult)
            V.drain()
            V.tensor_tensor(sc[:, 3:4], sc[:, 2:3], sc[:, 2:3], ALU.mult)
            V.tensor_scalar(sc[:, 4:5], sc[:, 1:2], 1.0 / D, None, ALU.mult)
            V.drain()
            V.tensor_tensor(sc[:, 5:6], sc[:, 4:5], sc[:, 3:4], ALU.subtract)
            k.edge(V, S)
            S.activation(sc[:, 6:7], sc[:, 5:6], ACT.Ln, bias=sc[:, 60:61], scale=1.0)
            S.drain()
            S.activation(sc[:, 7:8], sc[:, 6:7], ACT.Exp, bias=0.0, scale=-0.5)
            k.edge(S, V)
            V.tensor_scalar(scratch[:, 0:D], xt, sc[:, 2:3], sc[:, 7:8],
                            ALU.subtract, ALU.mult)
            V.drain()
            V.reduce_max(sc[:, 8:9], scratch[:, 0:D], axis=AX, apply_absolute_value=True)
            V.drain()
            V.tensor_scalar(sc[:, 16 + tc:17 + tc], sc[:, 8:9], 1.0 / 127.0, EPS,
                            ALU.mult, ALU.add)
            V.drain()
            V.reciprocal(sc[:, 9:10], sc[:, 16 + tc:17 + tc])
            V.drain()
            V.tensor_scalar(scratch[:, 0:D], scratch[:, 0:D], sc[:, 9:10], None, ALU.mult)
            V.drain()
            V.tensor_copy(qscr[:, :], scratch[:, 0:D])
            V.drain()
            V.tensor_copy(scratch[:, 0:D], qscr[:, :])
            k.edge(V, P)
            for dc in range(0, NDIN, 2):
                P.transpose(PB[6][:, 0:128], scratch[:, dc * 128:(dc + 1) * 128], ident_f[:, :])
                P.transpose(PB[7][:, 0:128], scratch[:, (dc + 1) * 128:(dc + 2) * 128], ident_f[:, :])
                k.edge(P, V)
                V.tensor_copy(xq_fm[:, dc * TOK + tc * 128: dc * TOK + (tc + 1) * 128],
                              PB[6][:, 0:128])
                V.tensor_copy(xq_fm[:, (dc + 1) * TOK + tc * 128: (dc + 1) * TOK + (tc + 1) * 128],
                              PB[7][:, 0:128])
                k.edge(V, P)
        V.tensor_scalar(sc[:, 40:48], sc[:, 16:24], sc[:, 28:29], -1.0, ALU.mult, ALU.mult)
        V.tensor_scalar(sc[:, 48:56], sc[:, 16:24], sc[:, 29:30], -1.0, ALU.mult, ALU.mult)
        V.memset(sc[:, 32:40], 0.0)
        V.drain()
        k.bar()
        tap(f"xqfm_{tag}", xq_fm[:, :])
        G.wait_ge(k.sem["cc"], k.cnt["cc"])
        def slab_ap(mat_off, IB, par):
            r = IB // 2
            w0 = (IB % 2) * 512
            return AP(cc_wq_out, r * WQ_SL + mat_off + w0,
                      [[1024, 128], [128 * 1024, NDIN], [1, 512]])
        pe0 = k.cnt["pe"]; ve0 = k.cnt["ve"]; h0 = k.cnt["h"]; gcA = []
        k.dg(slab_g.ap().rearrange("p (s c o) -> p s c o", s=2, o=512)[:, 0, :, :], slab_ap(0, 0, 0))
        k.dg(slab_u.ap().rearrange("p (s c o) -> p s c o", s=2, o=512)[:, 0, :, :], slab_ap(WQ_G, 0, 0))
        gcA.append(k.cnt["g"])
        for IB in range(16):
            if IB + 1 < 16:
                G.wait_ge(k.sem["pe"], pe0 + IB * 8)
                par2 = (IB + 1) % 2
                k.dg(slab_g.ap().rearrange("p (s c o) -> p s c o", s=2, o=512)[:, par2, :, :],
                     slab_ap(0, IB + 1, par2))
                k.dg(slab_u.ap().rearrange("p (s c o) -> p s c o", s=2, o=512)[:, par2, :, :],
                     slab_ap(WQ_G, IB + 1, par2))
                gcA.append(k.cnt["g"])
            sp = IB % 2
            for tc in range(NTOKC):
                u = IB * 8 + tc
                par = u % 2
                pg, pu = PB[par * 2], PB[par * 2 + 1]
                sg_t = sgt[:, par * 512:(par + 1) * 512]
                u_t = ut[:, par * 512:(par + 1) * 512]
                if u >= 2:
                    P.wait_ge(k.sem["ve"], ve0 + (u - 1) * 2)
                if tc == 0:
                    P.wait_ge(k.sem["g"], gcA[IB])
                for dc in range(NDIN):
                    P.matmul(pg, xq_fm[:, dc * TOK + tc * 128: dc * TOK + (tc + 1) * 128],
                             slab_g[:, sp * NDIN * 512 + dc * 512: sp * NDIN * 512 + (dc + 1) * 512],
                             start=(dc == 0), stop=(dc == NDIN - 1))
                for dc in range(NDIN):
                    P.matmul(pu, xq_fm[:, dc * TOK + tc * 128: dc * TOK + (tc + 1) * 128],
                             slab_u[:, sp * NDIN * 512 + dc * 512: sp * NDIN * 512 + (dc + 1) * 512],
                             start=(dc == 0), stop=(dc == NDIN - 1))
                P.drain().then_inc(k.sem["pe"], 1); k.cnt["pe"] += 1
                V.wait_ge(k.sem["pe"], pe0 + u + 1)
                if u >= 2:
                    V.wait_ge(k.sem["h"], h0 + (u - 1) * 16)
                scr_t = scratch[:, par * 512:(par + 1) * 512]
                V.tensor_scalar(scr_t, pg, sc[:, 40 + tc:41 + tc], None, ALU.mult)
                k.edge(V, S)
                S.activation(sg_t, scr_t, ACT.Silu, bias=0.0, scale=1.0)
                k.edge(S, V)
                V.scalar_tensor_tensor(u_t, pu, sc[:, 48 + tc:49 + tc], sg_t,
                                       ALU.mult, ALU.mult)
                V.drain()
                V.reduce_max(sc[:, 10 + par:11 + par], u_t, axis=AX, apply_absolute_value=True)
                V.drain()
                V.tensor_tensor(sc[:, 32 + tc:33 + tc], sc[:, 32 + tc:33 + tc],
                                sc[:, 10 + par:11 + par], ALU.max)
                V.drain().then_inc(k.sem["ve"], 1); k.cnt["ve"] += 1
                nc.sync.wait_ge(k.sem["ve"], k.cnt["ve"])
                k.dh(AP(u_spill, (tc * 128) * FI + IB * 512, [[FI, 128], [1, 512]]), u_t)
        k.bar()
        V.tensor_scalar(sc[:, 32:40], sc[:, 32:40], 1.0 / 127.0, EPS, ALU.mult, ALU.add)
        V.drain()
        V.tensor_scalar(sc[:, 48:56], sc[:, 32:40], sc[:, 30:31], -1.0, ALU.mult, ALU.mult)
        for tc in range(NTOKC):
            V.reciprocal(sc[:, 0 + tc:1 + tc], sc[:, 32 + tc:33 + tc])
        V.drain()
        k.bar()
        sb.pop()
        sb.push()
        xqu_tm = sb.alloc(f"xqutm{tag}", [128, NTOKC * FI], I8)
        urow = sb.alloc(f"urow{tag}", [128, 2 * (FI // 16)], F32)
        scratch = sb.alloc(f"fscr2{tag}", [128, 512], F32)
        qscr = sb.alloc(f"qscr2{tag}", [128, 512], I32)
        units_c = [(tc, hf) for tc in range(NTOKC) for hf in range(16)]
        ve0 = k.cnt["ve"]; h0 = k.cnt["h"]
        def urow_ap(tc, hf):
            return AP(u_spill, (tc * 128) * FI + hf * 512, [[FI, 128], [1, 512]])
        k.dh(urow[:, 0:512], urow_ap(*units_c[0]))
        for u, (tc, hf) in enumerate(units_c):
            if u + 1 < len(units_c):
                nc.sync.wait_ge(k.sem["ve"], ve0 + u)
                k.dh(urow.ap().rearrange("p (s n) -> p s n", s=2)[:, (u + 1) % 2, :],
                     urow_ap(*units_c[u + 1]))
            V.wait_ge(k.sem["h"], h0 + (u + 1) * 16)
            ur = urow[:, (u % 2) * 512:(u % 2 + 1) * 512]
            V.tensor_scalar(scratch[:, 0:512], ur, sc[:, 0 + tc:1 + tc], None, ALU.mult)
            V.drain()
            V.tensor_copy(qscr[:, :], scratch[:, 0:512])
            V.drain()
            off = tc * FI + hf * 512
            V.tensor_copy(xqu_tm[:, off:off + 512], qscr[:, :])
            V.drain().then_inc(k.sem["ve"], 1); k.cnt["ve"] += 1
        k.bar()
        if f"usp_{tag}" in taps:
            dtmp = nc.dram_tensor(f"dbg_usp_{tag}", [TOK, FI], BF16, kind="ExternalOutput")
            k.dg(dtmp.ap(), u_spill.ap())
            k.bar()
            dbg[f"usp_{tag}"] = 1
        tap(f"xqu_{tag}", xqu_tm[:, :])
        tap(f"scs_{tag}", sc[:, 0:64])
        xqu_fm = sb.alloc(f"xqufm{tag}", [128, 64 * TOK], I8)
        units_d = [(ic, tc) for ic in range(64) for tc in range(0, NTOKC, 2)]
        pe0 = k.cnt["pe"]; ve0 = k.cnt["ve"]
        for u, (ic, tc) in enumerate(units_d):
            par = u % 2
            so = par * 256
            if u >= 2:
                V.wait_ge(k.sem["pe"], pe0 + u - 1)
            V.tensor_copy(scratch[:, so:so + 128],
                          xqu_tm[:, tc * FI + ic * 128: tc * FI + (ic + 1) * 128])
            V.tensor_copy(scratch[:, so + 128:so + 256],
                          xqu_tm[:, (tc + 1) * FI + ic * 128: (tc + 1) * FI + (ic + 1) * 128])
            V.drain().then_inc(k.sem["ve"], 1); k.cnt["ve"] += 1
            P.wait_ge(k.sem["ve"], ve0 + 2 * u + 1)
            if u >= 2:
                P.wait_ge(k.sem["ve"], ve0 + 2 * (u - 2) + 2)
            P.transpose(PB[4 + par * 2][:, 0:128], scratch[:, so:so + 128], ident_f[:, :])
            P.transpose(PB[5 + par * 2][:, 0:128], scratch[:, so + 128:so + 256], ident_f[:, :])
            P.drain().then_inc(k.sem["pe"], 1); k.cnt["pe"] += 1
            V.wait_ge(k.sem["pe"], pe0 + u + 1)
            V.tensor_copy(xqu_fm[:, ic * TOK + tc * 128: ic * TOK + (tc + 1) * 128],
                          PB[4 + par * 2][:, 0:128])
            V.tensor_copy(xqu_fm[:, ic * TOK + (tc + 1) * 128: ic * TOK + (tc + 2) * 128],
                          PB[5 + par * 2][:, 0:128])
            V.drain().then_inc(k.sem["ve"], 1); k.cnt["ve"] += 1
        P.wait_ge(k.sem["ve"], k.cnt["ve"])
        k.bar()
        wdb = sb.alloc(f"wdb{tag}", [128, 2 * 1024], BF16)
        xqb = sb.alloc(f"xqb{tag}", [128, 2 * 512], BF16)
        def wdb_ap(oh, ic):
            return AP(cc_wq_out, (4 * oh) * WQ_SL + 2 * WQ_G + ic * 128 * 256,
                      [[256, 128], [WQ_SL, 4], [1, 256]])
        for oh in range(2):
            for th in range(2):
                pe0 = k.cnt["pe"]; ve0 = k.cnt["ve"]; g0 = k.cnt["g"]
                k.dg(wdb.ap().rearrange("p (s n) -> p s n", s=2)[:, 0, :], wdb_ap(oh, 0))
                for ic in range(64):
                    par = ic % 2
                    if ic + 1 < 64:
                        G.wait_ge(k.sem["pe"], pe0 + ic)
                        k.dg(wdb.ap().rearrange("p (s n) -> p s n", s=2)[:, (ic + 1) % 2, :],
                             wdb_ap(oh, ic + 1))
                    if ic >= 2:
                        V.wait_ge(k.sem["pe"], pe0 + ic - 1)
                    V.wait_ge(k.sem["g"], g0 + (ic + 1) * 16)
                    V.tensor_copy(xqb[:, par * 512:(par + 1) * 512],
                                  xqu_fm[:, ic * TOK + th * 512: ic * TOK + (th + 1) * 512])
                    V.drain().then_inc(k.sem["ve"], 1); k.cnt["ve"] += 1
                    P.wait_ge(k.sem["ve"], ve0 + ic + 1)
                    for tcl in range(4):
                        for ob in range(2):
                            P.matmul(PB[tcl * 2 + ob], xqb[:, par * 512 + tcl * 128: par * 512 + (tcl + 1) * 128],
                                     wdb[:, par * 1024 + ob * 512: par * 1024 + (ob + 1) * 512],
                                     start=(ic == 0), stop=(ic == 63))
                    P.drain().then_inc(k.sem["pe"], 1); k.cnt["pe"] += 1
                k.edge(P, V)
                for tcl in range(4):
                    tc = th * 4 + tcl
                    for ob in range(2):
                        col = oh * 1024 + ob * 512
                        V.scalar_tensor_tensor(resid[:, tc * D + col: tc * D + col + 512],
                                               PB[tcl * 2 + ob], sc[:, 48 + tc:49 + tc],
                                               resid[:, tc * D + col: tc * D + col + 512],
                                               ALU.mult, ALU.add)
                k.edge(V, P)
                k.bar()
        k.bar()
        sb.pop()

    if upto >= 3:
        emit_ffn("f1")
        tap("x2", resid[:, :])

    if upto >= 4:
        sb.push()
        q_fm = sb.alloc("bqfm", [128, HB * TOK], BF16)
        k_fm = sb.alloc("bkfm", [128, KVH * 2 * TOK], BF16)
        v_tm = sb.alloc("bvtm", [128, 16 * KVH * DH], BF16)
        sb.push()
        xln_fm = sb.alloc("bxln", [128, NDIN * TOK], BF16)
        sb.push()
        lnscr = sb.alloc("blnscr", [128, D], F32)
        ln_to_fm(xln_fm, lnscr, lnscr)
        sb.pop()
        slab = sb.alloc("bslab", [128, NDIN * 512], BF16)
        scratch = sb.alloc("bscr", [128, 1536], F32)
        cos_s = sb.alloc("bcos", [128, NTOKC * DH], BF16)
        sin_s = sb.alloc("bsin", [128, NTOKC * DH], BF16)

        k.dg(cos_s.ap().rearrange("p (c d) -> p c d", d=DH),
             cos_t.ap().rearrange("(c p) d -> p c d", p=128))
        k.dg(sin_s.ap().rearrange("p (c d) -> p c d", d=DH),
             sin_t.ap().rearrange("(c p) d -> p c d", p=128))
        k.bar()

        def rope_tile(tc, nh):
            cbc = AP(cos_s, tc * DH, [[NTOKC * DH, 128], [0, nh], [1, DH]])
            sA1 = AP(sin_s, tc * DH, [[NTOKC * DH, 128], [0, nh], [1, 64]])
            sA2 = AP(sin_s, tc * DH + 64, [[NTOKC * DH, 128], [0, nh], [1, 64]])
            src3 = AP(scratch, 0, [[1536, 128], [DH, nh], [1, DH]])
            src_lo = AP(scratch, 0, [[1536, 128], [DH, nh], [1, 64]])
            src_hi = AP(scratch, 64, [[1536, 128], [DH, nh], [1, 64]])
            dst3 = AP(scratch, 512, [[1536, 128], [DH, nh], [1, DH]])
            dst_lo = AP(scratch, 512, [[1536, 128], [DH, nh], [1, 64]])
            dst_hi = AP(scratch, 512 + 64, [[1536, 128], [DH, nh], [1, 64]])
            tmp = AP(scratch, 1024, [[1536, 128], [64, nh], [1, 64]])
            V.tensor_tensor(dst3, src3, cbc, ALU.mult)
            V.drain()
            V.tensor_tensor(tmp, src_hi, sA1, ALU.mult)
            V.drain()
            V.tensor_tensor(dst_lo, dst_lo, tmp, ALU.add)
            V.drain()
            V.tensor_tensor(tmp, src_lo, sA2, ALU.mult)
            V.drain()
            V.tensor_tensor(dst_hi, dst_hi, tmp, ALU.add)
            V.drain()

        for wsrc, kind in ((Wk_b, "k"), (Wv_b, "v")):
            k.dg(slab.ap().rearrange("p (c o) -> p c o", o=512)[:, 0:NDIN, :],
                 wsrc.ap().rearrange("(c p) o -> p c o", p=128))
            k.bar()
            for tc in range(NTOKC):
                pbi = PB[tc % 2]
                for dc in range(NDIN):
                    P.matmul(pbi, xln_fm[:, dc * TOK + tc * 128: dc * TOK + (tc + 1) * 128],
                             slab[:, dc * 512:(dc + 1) * 512],
                             start=(dc == 0), stop=(dc == NDIN - 1))
                k.edge(P, V)
                k.wait_dma(V)
                if kind == "k":
                    V.tensor_copy(scratch[:, 0:512], pbi)
                    V.drain()
                    rope_tile(tc, KVH)
                    V.tensor_copy(scratch.ap().bitcast(BF16)[:, 2048:2048 + 512],
                                  scratch[:, 512:1024])
                    V.drain()
                    col0 = 0
                else:
                    V.tensor_copy(scratch.ap().bitcast(BF16)[:, 2048:2048 + 512], pbi)
                    V.drain()
                    col0 = KVH * DH
                k.edge(V, nc.sync, P)
                k.dh(AP(cc_kv_in, (tc * 128) * (2 * KVH * DH) + col0,
                        [[2 * KVH * DH, 128], [1, 512]]),
                     scratch.ap().bitcast(BF16)[:, 2048:2048 + 512])
            k.bar()
        k.wait_dma(G)
        G.collective_compute("AllGather", ALU.bypass,
                             replica_groups=[[0, 1], [2, 3], [4, 5], [6, 7]],
                             ins=[cc_kv_in.ap().opt()], outs=[cc_kv_out.ap().opt()]
                             ).then_inc(k.sem["cc"], 1)
        k.cnt["cc"] += 1
        for qtr in range(4):
            k.dg(slab.ap().rearrange("p (c o) -> p c o", o=512)[:, 0:NDIN, :],
                 Wq_b.ap().rearrange("(c p) (b o) -> p c b o", p=128, o=512)[:, :, qtr, :])
            k.bar()
            for tc in range(NTOKC):
                pbi = PB[tc % 2]
                for dc in range(NDIN):
                    P.matmul(pbi, xln_fm[:, dc * TOK + tc * 128: dc * TOK + (tc + 1) * 128],
                             slab[:, dc * 512:(dc + 1) * 512],
                             start=(dc == 0), stop=(dc == NDIN - 1))
                k.edge(P, V)
                V.tensor_copy(scratch[:, 0:512], pbi)
                V.drain()
                rope_tile(tc, 4)
                k.edge(V, P)
                for j in range(4):
                    hh_ = qtr * 4 + j
                    P.transpose(PB[4 + j % 2][:, 0:128], scratch[:, 512 + j * 128:512 + (j + 1) * 128],
                                ident_f[:, :])
                    k.edge(P, V)
                    V.tensor_copy(q_fm[:, hh_ * TOK + tc * 128: hh_ * TOK + (tc + 1) * 128],
                                  PB[4 + j % 2][:, 0:128])
                    k.edge(V, P)
            k.bar()
        sb.pop()
        sb.push()
        o_fm = sb.alloc("bofm", [128, HB * TOK], BF16)
        scratch = sb.alloc("bscr2", [128, 1536], F32)
        pbuf = sb.alloc("bp", [128, 2048], F32)
        pT = sb.alloc("bpT", [128, 2048], BF16)
        trim_s = sb.alloc("trims", [128, 128], F32)
        trir_s = sb.alloc("trirs", [128, 128], F32)
        bcols = sb.alloc("bcolsb", [128, 8], F32)
        wob_slab = sb.alloc("bwo", [128, NDIN * 512], BF16)
        k.dh(trim_s[:, :], diag_mask[:, :])
        k.dh(trir_s[:, :], tri_r[:, :])
        k.dh(bcols[:, 0:4], attnb_cols[:, :])
        V.memset(bcols[:, 6:7], NEG)
        V.drain()
        k.bar()
        for kt in range(16):
            sl_ = kt // 8
            r0 = (kt % 8) * 128
            k.dg(scratch[:, 0:512],
                 AP(cc_kv_out, (sl_ * TOK + r0) * (2 * KVH * DH), [[2 * KVH * DH, 128], [1, 512]]))
            k.bar()
            for j in range(KVH):
                P.transpose(PB[4][:, 0:128], scratch[:, j * 128:(j + 1) * 128], ident_f[:, :])
                k.edge(P, V)
                V.tensor_copy(k_fm[:, j * 2048 + kt * 128: j * 2048 + (kt + 1) * 128],
                              PB[4][:, 0:128])
                k.edge(V, P)
            k.bar()
        k.dg(v_tm.ap().rearrange("p (c o) -> p c o", o=512),
             AP(cc_kv_out, KVH * DH, [[2 * KVH * DH, 128], [128 * 2 * KVH * DH, 16], [1, 512]]))
        k.bar()
        for hh_ in range(HB):
            kv = hh_ // 4
            for qt in range(NTOKC):
                for kb in range(4):
                    P.matmul(PB[kb], q_fm[:, hh_ * TOK + qt * 128: hh_ * TOK + (qt + 1) * 128],
                             k_fm[:, kv * 2048 + kb * 512: kv * 2048 + (kb + 1) * 512],
                             start=True, stop=True)
                k.edge(P, V)
                V.tensor_tensor(psS[:, qt * 128:(qt + 1) * 128],
                                psS[:, qt * 128:(qt + 1) * 128], trim_s[:, :], ALU.add)
                V.tensor_tensor(psS[:, 1024 + qt * 128: 1024 + (qt + 1) * 128],
                                psS[:, 1024 + qt * 128: 1024 + (qt + 1) * 128],
                                trir_s[:, :], ALU.add)
                V.drain()
                if qt < 7:
                    nsfx = (7 - qt) * 128
                    V.tensor_scalar(psS[:, (qt + 1) * 128:(qt + 1) * 128 + nsfx],
                                    psS[:, (qt + 1) * 128:(qt + 1) * 128 + nsfx],
                                    bcols[:, 0:1], None, ALU.add)
                    V.tensor_scalar(psS[:, 1024 + (qt + 1) * 128: 1024 + (qt + 1) * 128 + nsfx],
                                    psS[:, 1024 + (qt + 1) * 128: 1024 + (qt + 1) * 128 + nsfx],
                                    bcols[:, 6:7], None, ALU.add)
                if qt > 0:
                    V.tensor_scalar(psS[:, 1024:1024 + qt * 128],
                                    psS[:, 1024:1024 + qt * 128],
                                    bcols[:, 1:2], None, ALU.add)
                V.drain()
                V.reduce_max(sc[:, 0:1], psS[:, 0:1024], axis=AX)
                V.reduce_max(sc[:, 1:2], psS[:, 1024:2048], axis=AX)
                V.drain()
                V.tensor_tensor(sc[:, 4:5], sc[:, 0:1], sc[:, 1:2], ALU.max)
                V.drain()
                V.tensor_scalar(sc[:, 5:6], sc[:, 4:5], -SCALE_B, None, ALU.mult)
                V.drain()
                k.edge(V, S)
                S.activation(pbuf[:, 0:1024], psS[:, 0:1024], ACT.Exp,
                             bias=sc[:, 5:6], scale=SCALE_B, accum_out=sc[:, 10:11])
                S.activation(pbuf[:, 1024:2048], psS[:, 1024:2048], ACT.Exp,
                             bias=sc[:, 5:6], scale=SCALE_B, accum_out=sc[:, 11:12])
                k.edge(S, V, P)
                V.tensor_tensor(sc[:, 12:13], sc[:, 10:11], sc[:, 11:12], ALU.add)
                V.drain()
                V.reciprocal(sc[:, 13:14], sc[:, 12:13])
                V.drain()
                V.tensor_scalar(pbuf[:, :], pbuf[:, :], sc[:, 13:14], None, ALU.mult)
                k.edge(V, P)
                for g2 in range(0, 16, 2):
                    P.transpose(PB[4][:, 0:128], pbuf[:, g2 * 128:(g2 + 1) * 128], ident_f[:, :])
                    P.transpose(PB[5][:, 0:128], pbuf[:, (g2 + 1) * 128:(g2 + 2) * 128], ident_f[:, :])
                    k.edge(P, V)
                    V.tensor_copy(pT[:, g2 * 128:(g2 + 1) * 128], PB[4][:, 0:128])
                    V.tensor_copy(pT[:, (g2 + 1) * 128:(g2 + 2) * 128], PB[5][:, 0:128])
                    k.edge(V, P)
                for kc in range(16):
                    P.matmul(PB[6][:, 0:128],
                             v_tm[:, kc * 512 + kv * 128: kc * 512 + (kv + 1) * 128],
                             pT[:, kc * 128:(kc + 1) * 128],
                             start=(kc == 0), stop=(kc == 15))
                k.edge(P, V)
                V.tensor_copy(o_fm[:, hh_ * TOK + qt * 128: hh_ * TOK + (qt + 1) * 128],
                              PB[6][:, 0:128])
                k.edge(V, P, S)
        k.bar()
        tap("o_attnb", o_fm[:, :])
        for qtr in range(4):
            k.dg(wob_slab.ap().rearrange("p (c o) -> p c o", o=512)[:, 0:NDIN, :],
                 Wo_b.ap().rearrange("(c p) (b o) -> p c b o", p=128, o=512)[:, :, qtr, :])
            k.bar()
            for tc in range(NTOKC):
                pbi = PB[tc % 2]
                for hc in range(HB):
                    P.matmul(pbi, o_fm[:, hc * TOK + tc * 128: hc * TOK + (tc + 1) * 128],
                             wob_slab[:, hc * 512:(hc + 1) * 512],
                             start=(hc == 0), stop=(hc == HB - 1))
                k.edge(P, V)
                V.tensor_tensor(resid[:, tc * D + qtr * 512: tc * D + qtr * 512 + 512],
                                resid[:, tc * D + qtr * 512: tc * D + qtr * 512 + 512],
                                pbi, ALU.add)
                k.edge(V, P)
            k.bar()
        sb.pop()
        sb.pop()
        tap("x3", resid[:, :])

    if upto >= 5:
        emit_ffn("f2")

    k.bar()
    k.dh(out_d.ap().rearrange("(c p) d -> p c d", p=128),
         resid.ap().rearrange("p (c d) -> p c d", d=D))
    k.wait_dma(nc.sync)
    return nc, dbg


def stage_inputs(inputs):
    f = lambda a: np.ascontiguousarray(a, dtype=np.float32)
    x = np.asarray(inputs["L0_attn_a_x"])
    Wq_a = f(np.asarray(inputs["L0_attn_a_Wq"]).reshape(D, HG, 2, DK // 2).transpose(1, 2, 0, 3))
    Wk_a = f(np.asarray(inputs["L0_attn_a_Wk"]).reshape(D, HG, 2, DK // 2).transpose(1, 2, 0, 3))
    Wv_a = f(np.asarray(inputs["L0_attn_a_Wv"]).reshape(D, HG, DV).transpose(1, 0, 2))
    Wnorm_rep = f(np.broadcast_to(np.asarray(inputs["L0_attn_a_Wnorm"])[None, :], (128, DV)))
    tri = (np.arange(CH)[:, None] <= np.arange(CH)[None, :]).astype(np.float32)
    tri2 = f(np.concatenate([tri, tri], axis=0))
    tri_causal = np.where(np.arange(128)[None, :] <= np.arange(128)[:, None], 0.0, NEG).astype(np.float32)
    zeros128 = np.zeros((128, 128), np.float32)
    negs128 = np.full((128, 128), NEG, np.float32)
    ident = np.eye(128, dtype=np.float32)
    inv = 1.0 / 10000.0 ** (np.arange(0, DH, 2, dtype=np.float64) / DH)
    posf = np.arange(T, dtype=np.float64)[:, None] * inv[None, :]
    cos_full = np.concatenate([np.cos(posf)] * 2, axis=-1).astype(np.float32)
    sin_full = np.concatenate([-np.sin(posf), np.sin(posf)], axis=-1).astype(np.float32)
    wg = np.asarray(inputs["L0_ffn_w_gate"]); wu = np.asarray(inputs["L0_ffn_w_up"])
    wdn = np.asarray(inputs["L0_ffn_w_down"])

    in_maps = []
    for core in range(8):
        b, hh = core // 2, core % 2
        m = {
            "x_own": f(x[b, hh * TOK:(hh + 1) * TOK, :]),
            "Wq_a": Wq_a, "Wk_a": Wk_a, "Wv_a": Wv_a,
            "Wg_a": f(np.asarray(inputs["L0_attn_a_Wg"]).reshape(D, 4, VD // 4).transpose(1, 0, 2)),
            "Wgk1": f(inputs["L0_attn_a_Wgk1"]), "Wgk2": f(inputs["L0_attn_a_Wgk2"]),
            "bgk2": f(np.asarray(inputs["L0_attn_a_bgk2"]).reshape(HG * NDKC, 128).T),
            "Wnorm_rep": Wnorm_rep,
            "Wo_a": f(np.asarray(inputs["L0_attn_a_Wo"]).reshape(VD, 4, D // 4).transpose(1, 0, 2)),
            "Wq_b": f(inputs["L0_attn_b_Wq"]), "Wk_b": f(inputs["L0_attn_b_Wk"]),
            "Wv_b": f(inputs["L0_attn_b_Wv"]), "Wo_b": f(inputs["L0_attn_b_Wo"]),
            "wg_sl": f(wg[core * (FI // 8):(core + 1) * (FI // 8), :]),
            "wu_sl": f(wu[core * (FI // 8):(core + 1) * (FI // 8), :]),
            "wd_sl": f(wdn[core * (D // 8):(core + 1) * (D // 8), :]),
            "cos_t": f(cos_full[hh * TOK:(hh + 1) * TOK]),
            "sin_t": f(sin_full[hh * TOK:(hh + 1) * TOK]),
            "gla_gate": np.full((128, 1), float(hh), np.float32),
            "attnb_bias": np.full((128, 1), 0.0 if hh else NEG, np.float32),
            "tri_gla": tri2, "ident_c": ident,
            "diag_mask": tri_causal if hh == 0 else zeros128,
            "tri_r": negs128 if hh == 0 else tri_causal,
            "attnb_cols": (np.array([[NEG, NEG, 0.0, 0.0]], np.float32) if hh == 0
                           else np.array([[0.0, 0.0, 0.0, 0.0]], np.float32)
                           ).repeat(128, axis=0),
        }
        in_maps.append(m)
    return in_maps


def kernel(**inputs):
    nc, _ = build()
    in_maps = stage_inputs(inputs)
    res = run_bass_kernel_spmd(nc, in_maps, core_ids=list(range(8)))
    outs = np.empty((B, T, D), np.float32)
    for core in range(8):
        b, hh = core // 2, core % 2
        outs[b, hh * TOK:(hh + 1) * TOK, :] = res.results[core]["out"]
    return outs


if __name__ == "__main__":
    nc, dbg = build()
    print("built", nc.n_instructions())


# version 06
# speedup vs baseline: 1.4609x; 1.1045x over previous
import numpy as np
import ml_dtypes
import concourse.bass as bass
import concourse.mybir as mybir
from concourse.ap import AP
from concourse.bass_utils import run_bass_kernel_spmd

F32 = mybir.dt.float32
BF16 = mybir.dt.bfloat16
I32 = mybir.dt.int32
I8 = mybir.dt.int8
ALU = mybir.AluOpType
ACT = mybir.ActivationFunctionType
AX = mybir.AxisListType.X

B, T, D = 4, 2048, 2048
TOK = 1024
NTOKC = TOK // 128
NDIN = D // 128
HG, DK, DV, CH = 4, 512, 256, 64
NCH = TOK // CH
NDKC = DK // 128
VD = HG * DV
NVDC = VD // 128
SCALE_GLA = float(DK) ** -0.5
S16 = 1.0 / 16.0
EPS = 1e-5
FI = 8192
HB, KVH, DH = 16, 4, 128
SCALE_B = float(1.0 / np.sqrt(np.float64(DH)))
NEG = -1e30
MM = 4096.0

DT_BYTES = {F32: 4, BF16: 2, I32: 4, I8: 1}


class SBuf:
    def __init__(self, nc):
        self.nc = nc
        self.base = (nc.sbuf_base + 31) & ~31
        self.limit = nc.sbuf_top
        self.off = self.base
        self.marks = []
        self.n = 0

    def alloc(self, name, shape, dtype):
        nbytes = int(np.prod(shape[1:])) * DT_BYTES[dtype]
        nbytes = (nbytes + 31) & ~31
        assert self.off + nbytes <= self.limit, (
            f"SBUF overflow at {name}: {self.off}+{nbytes} > {self.limit}")
        t = self.nc.alloc_sbuf_tensor_at(f"{name}_{self.n}", list(shape), dtype, offset=self.off)
        self.n += 1
        self.off += nbytes
        return t

    def push(self):
        self.marks.append(self.off)

    def pop(self):
        self.off = self.marks.pop()


class K:
    def __init__(self, nc):
        self.nc = nc
        self.sb = SBuf(nc)
        self.cnt = {}
        self.sem = {}
        for name in ("h", "g", "cc", "pe", "ve", "se", "ge"):
            self.sem[name] = nc.alloc_semaphore(f"sem_{name}")
            self.cnt[name] = 0
        self.ename = {}

    def _e(self, eng):
        n = {"PE": "pe", "DVE": "ve", "Activation": "se", "Pool": "ge"}
        return n[str(eng.engine.value) if hasattr(eng, "engine") else "?"]

    def edge(self, prod, *cons):
        import concourse.mybir as mb
        emap = {mb.EngineType.PE: "pe", mb.EngineType.DVE: "ve",
                mb.EngineType.Activation: "se", mb.EngineType.Pool: "ge"}
        name = emap[prod.engine]
        self.cnt[name] += 1
        prod.drain().then_inc(self.sem[name], 1)
        for c in cons:
            c.wait_ge(self.sem[name], self.cnt[name])

    def dh(self, out, in_):
        self.nc.sync.dma_start(out=out, in_=in_).then_inc(self.sem["h"], 16)
        self.cnt["h"] += 16

    def dg(self, out, in_):
        self.nc.gpsimd.dma_start(out=out, in_=in_).then_inc(self.sem["g"], 16)
        self.cnt["g"] += 16

    def wait_dma(self, *engines):
        for e in engines:
            e.wait_ge(self.sem["h"], self.cnt["h"])
            e.wait_ge(self.sem["g"], self.cnt["g"])

    def bar(self):
        nc = self.nc
        self.wait_dma(nc.sync, nc.gpsimd)
        nc.all_engine_barrier()


def build(taps=(), upto=99):
    nc = bass.Bass("TRN2", target_bir_lowering=False)
    k = K(nc)
    sb = k.sb
    V, S, P, G = nc.vector, nc.scalar, nc.tensor, nc.gpsimd

    di = lambda n, shp, dt=F32: nc.dram_tensor(n, shp, dt, kind="ExternalInput")
    x_own = di("x_own", [TOK, D])
    out_d = nc.dram_tensor("out", [TOK, D], F32, kind="ExternalOutput")
    Wq_a = di("Wq_a", [HG, 2, D, DK // 2]); Wk_a = di("Wk_a", [HG, 2, D, DK // 2])
    Wv_a = di("Wv_a", [HG, D, DV]); Wg_a = di("Wg_a", [4, D, VD // 4])
    Wgk1 = di("Wgk1", [D, 16]); Wgk2 = di("Wgk2", [16, D]); bgk2 = di("bgk2", [128, HG * NDKC])
    Wnorm_rep = di("Wnorm_rep", [128, DV]); Wo_a = di("Wo_a", [4, VD, D // 4])
    Wq_b = di("Wq_b", [D, D]); Wk_b = di("Wk_b", [D, KVH * DH])
    Wv_b = di("Wv_b", [D, KVH * DH]); Wo_b = di("Wo_b", [D, D])
    wg_sl = di("wg_sl", [FI // 8, D]); wu_sl = di("wu_sl", [FI // 8, D])
    wd_sl = di("wd_sl", [D // 8, FI])
    cos_t = di("cos_t", [TOK, DH]); sin_t = di("sin_t", [TOK, DH])
    gla_gate = di("gla_gate", [128, 1])
    attnb_bias = di("attnb_bias", [128, 1])
    tri_gla = di("tri_gla", [128, CH])
    diag_mask = di("diag_mask", [128, 128])
    tri_r = di("tri_r", [128, 128])
    attnb_cols = di("attnb_cols", [128, 4])
    ident_c = di("ident_c", [128, 128])

    cc_ws_in = nc.dram_tensor("cc_ws_in", [128, 4], F32)
    cc_ws_out = nc.dram_tensor("cc_ws_out", [128, 4], F32)
    WQ_G = D * (FI // 8)
    WQ_SL = 2 * WQ_G + FI * (D // 8)
    cc_wq_in = nc.dram_tensor("cc_wq_in", [WQ_SL], BF16)
    cc_wq_out = nc.dram_tensor("cc_wq_out", [8, WQ_SL], BF16)
    cc_S_in = nc.dram_tensor("cc_S_in", [HG * NDKC * 128, DV], F32)
    cc_S_out = nc.dram_tensor("cc_S_out", [2, HG * NDKC * 128, DV], F32)
    u_spill = nc.dram_tensor("u_spill", [TOK, FI], F32)
    cc_kv_in = nc.dram_tensor("cc_kv_in", [TOK, 2 * KVH * DH], BF16)
    cc_kv_out = nc.dram_tensor("cc_kv_out", [2, TOK, 2 * KVH * DH], BF16)

    dbg = {}

    def tap(name, a):
        if name not in taps:
            return
        d = nc.dram_tensor(f"dbg_{name}", list(a.shape), a.dtype, kind="ExternalOutput")
        k.bar()
        k.dg(d.ap(), a)
        k.bar()
        dbg[name] = f"dbg_{name}"

    resid = sb.alloc("resid", [128, NTOKC * D], F32)
    ident_f = sb.alloc("ident_f", [128, 128], F32)
    sc = sb.alloc("smallcols", [128, 64], F32)

    psS_t = nc.alloc_psum_tensor("psS", [128, 2048], F32)
    pb_hi = [nc.alloc_psum_tensor(f"pbk{i}", [128, 512], F32) for i in range(4, 8)]
    PB = [psS_t.ap()[:, i * 512:(i + 1) * 512] for i in range(4)] + [t.ap() for t in pb_hi]
    psS = psS_t.ap()

    k.dh(resid.ap().rearrange("p (c d) -> p c d", d=D),
         x_own.ap().rearrange("(c p) d -> p c d", p=128))
    k.dh(ident_f[:, :], ident_c[:, :])
    V.memset(sc[:, 60:61], EPS)
    V.memset(sc[:, 61:62], 1.0)
    V.drain()
    k.bar()

    def ln_to_fm(xln_fm, xf_tile, scratch):
        for tc in range(NTOKC):
            xt = resid[:, tc * D:(tc + 1) * D]
            V.reduce_sum(sc[:, 0:1], xt, axis=AX)
            V.scalar_tensor_tensor(scratch[:, 0:D], xt, 1.0, xt, ALU.bypass, ALU.mult,
                                   accum_out=sc[:, 1:2])
            V.drain()
            V.tensor_scalar(sc[:, 2:3], sc[:, 0:1], 1.0 / D, None, ALU.mult)
            V.drain()
            V.tensor_tensor(sc[:, 3:4], sc[:, 2:3], sc[:, 2:3], ALU.mult)
            V.tensor_scalar(sc[:, 4:5], sc[:, 1:2], 1.0 / D, None, ALU.mult)
            V.drain()
            V.tensor_tensor(sc[:, 5:6], sc[:, 4:5], sc[:, 3:4], ALU.subtract)
            k.edge(V, S)
            S.activation(sc[:, 6:7], sc[:, 5:6], ACT.Ln, bias=sc[:, 60:61], scale=1.0)
            S.drain()
            S.activation(sc[:, 7:8], sc[:, 6:7], ACT.Exp, bias=0.0, scale=-0.5)
            k.edge(S, V)
            V.tensor_scalar(xf_tile[:, 0:D], xt, sc[:, 2:3], sc[:, 7:8],
                            ALU.subtract, ALU.mult)
            k.edge(V, P)
            for dc in range(0, NDIN, 2):
                P.transpose(PB[6][:, 0:128], xf_tile[:, dc * 128:(dc + 1) * 128], ident_f[:, :])
                P.transpose(PB[7][:, 0:128], xf_tile[:, (dc + 1) * 128:(dc + 2) * 128], ident_f[:, :])
                k.edge(P, V)
                V.tensor_copy(xln_fm[:, dc * TOK + tc * 128: dc * TOK + (tc + 1) * 128],
                              PB[6][:, 0:128])
                V.tensor_copy(xln_fm[:, (dc + 1) * TOK + tc * 128: (dc + 1) * TOK + (tc + 1) * 128],
                              PB[7][:, 0:128])
                k.edge(V, P)
        k.bar()

    mats = [(wg_sl, FI // 8, D, 0), (wu_sl, FI // 8, D, WQ_G), (wd_sl, D // 8, FI, 2 * WQ_G)]
    if upto >= 1:
        sb.push()
        ones_f = sb.alloc("ones_f", [128, 128], F32)
        V.memset(ones_f[:, :], 1.0)
        V.drain()
        wbuf = sb.alloc("wsw", [128, 8192], F32)
        wqtstage = sb.alloc("wqtstage", [128, 16384], F32)
        c1buf = sb.alloc("c1buf", [128, 16384], BF16)
        for mi, (wd, rows, cols, _off) in enumerate(mats):
            nrc = rows // 128
            for hf in range(2):
                k.dh(wbuf.ap().rearrange("p (c d) -> p c d", d=cols)[:, 0:nrc // 2, :],
                     wd.ap().rearrange("(c p) d -> p c d", p=128)[:, hf * (nrc // 2):(hf + 1) * (nrc // 2), :])
                k.wait_dma(V)
                V.reduce_sum(sc[:, 12 + hf:13 + hf], wbuf[:, 0:(nrc // 2) * cols], axis=AX,
                             apply_absolute_value=True)
                k.edge(V, nc.sync)
            V.tensor_tensor(sc[:, 8 + mi:9 + mi], sc[:, 12:13], sc[:, 13:14], ALU.add)
            V.drain()
        V.memset(sc[:, 11:12], 0.0)
        k.edge(V, nc.sync)
        k.dh(cc_ws_in.ap(), sc[:, 8:12])
        k.wait_dma(G)
        G.collective_compute("AllReduce", ALU.add, replica_groups=[list(range(8))],
                             ins=[cc_ws_in.ap().opt()], outs=[cc_ws_out.ap().opt()]
                             ).then_inc(k.sem["cc"], 1)
        k.cnt["cc"] += 1
        G.wait_ge(k.sem["cc"], k.cnt["cc"])
        k.dg(sc[:, 16:20], cc_ws_out.ap())
        k.bar()
        P.matmul(PB[0][:, 0:4], ones_f[:, :], sc[:, 16:20], start=True, stop=True)
        k.edge(P, V)
        for mi, (wd, rows, cols, _off) in enumerate(mats):
            n_el = rows * 8 * cols
            V.tensor_scalar(sc[:, 24 + mi:25 + mi], PB[0][:, mi:mi + 1], 1.0 / n_el, None,
                            ALU.mult)
        V.drain()
        k.bar()
        for mi, (wd, rows, cols, _off) in enumerate(mats):
            nrc = rows // 128
            for hf in range(2):
                k.dh(wbuf.ap().rearrange("p (c d) -> p c d", d=cols)[:, 0:nrc // 2, :],
                     wd.ap().rearrange("(c p) d -> p c d", p=128)[:, hf * (nrc // 2):(hf + 1) * (nrc // 2), :])
                k.wait_dma(S, V)
                half_n = (nrc // 2) * cols
                S.activation(wqtstage[:, 0:half_n], wbuf[:, 0:half_n], ACT.Abs,
                             bias=0.0, scale=1.0)
                k.edge(S, V)
                V.tensor_scalar(wqtstage[:, 0:half_n], wqtstage[:, 0:half_n],
                                sc[:, 24 + mi:25 + mi], None, ALU.subtract)
                V.drain()
                V.reduce_sum(sc[:, 12 + hf:13 + hf], wqtstage[:, 0:half_n], axis=AX)
                k.edge(V, nc.sync)
            V.tensor_tensor(sc[:, 8 + mi:9 + mi], sc[:, 12:13], sc[:, 13:14], ALU.add)
            V.drain()
        V.memset(sc[:, 11:12], 0.0)
        k.edge(V, nc.sync)
        k.dh(cc_ws_in.ap(), sc[:, 8:12])
        k.wait_dma(G)
        G.collective_compute("AllReduce", ALU.add, replica_groups=[list(range(8))],
                             ins=[cc_ws_in.ap().opt()], outs=[cc_ws_out.ap().opt()]
                             ).then_inc(k.sem["cc"], 1)
        k.cnt["cc"] += 1
        G.wait_ge(k.sem["cc"], k.cnt["cc"])
        k.dg(sc[:, 16:20], cc_ws_out.ap())
        k.bar()
        P.matmul(PB[0][:, 0:4], ones_f[:, :], sc[:, 16:20], start=True, stop=True)
        k.edge(P, V)
        for mi, (wd, rows, cols, _off) in enumerate(mats):
            n_el = rows * 8 * cols
            V.tensor_scalar(sc[:, 28 + mi:29 + mi], PB[0][:, mi:mi + 1], 1.0 / n_el, None,
                            ALU.mult)
        V.drain()
        V.tensor_tensor(sc[:, 28:31], sc[:, 28:31], sc[:, 24:27], ALU.add)
        V.drain()
        V.tensor_scalar(sc[:, 28:31], sc[:, 28:31], 1.0, EPS, ALU.mult, ALU.add)
        V.drain()
        V.tensor_scalar(sc[:, 20:23], sc[:, 28:31], 0.5, None, ALU.mult)
        V.tensor_scalar(sc[:, 24:27], sc[:, 28:31], -0.5, None, ALU.mult)
        V.drain()
        k.bar()
        for mi, (wd, rows, cols, off) in enumerate(mats):
            nrc = rows // 128
            ncolc = cols // 128
            half_n = (nrc // 2) * cols
            for hf in range(2):
                k.dh(wbuf.ap().rearrange("p (c d) -> p c d", d=cols)[:, 0:nrc // 2, :],
                     wd.ap().rearrange("(c p) d -> p c d", p=128)[:, hf * (nrc // 2):(hf + 1) * (nrc // 2), :])
                k.wait_dma(V)
                V.tensor_scalar(wqtstage[:, hf * half_n:(hf + 1) * half_n], wbuf[:, 0:half_n],
                                sc[:, 20 + mi:21 + mi], None, ALU.is_ge)
                V.drain()
                V.scalar_tensor_tensor(wqtstage[:, hf * half_n:(hf + 1) * half_n], wbuf[:, 0:half_n],
                                       sc[:, 24 + mi:25 + mi], wqtstage[:, hf * half_n:(hf + 1) * half_n],
                                       ALU.is_le, ALU.subtract)
                k.edge(V, nc.sync)
            k.edge(V, P)
            tiles = [(cc_i, rc) for cc_i in range(ncolc) for rc in range(nrc)]
            for i0 in range(0, len(tiles), 4):
                grp = tiles[i0:i0 + 4]
                for j, (cc_i, rc) in enumerate(grp):
                    P.transpose(PB[j][:, 0:128],
                                wqtstage[:, rc * cols + cc_i * 128: rc * cols + cc_i * 128 + 128],
                                ident_f[:, :])
                k.edge(P, V)
                for j, (cc_i, rc) in enumerate(grp):
                    V.tensor_copy(c1buf[:, cc_i * rows + rc * 128: cc_i * rows + (rc + 1) * 128],
                                  PB[j][:, 0:128])
                k.edge(V, P)
            k.edge(V, nc.sync)
            k.dh(AP(cc_wq_in, off, [[rows, 128], [128 * rows, ncolc], [1, rows]]),
                 c1buf.ap().rearrange("p (a r) -> p a r", r=rows)[:, 0:ncolc, :])
            k.bar()
        k.wait_dma(G)
        G.collective_compute("AllGather", ALU.bypass, replica_groups=[list(range(8))],
                             ins=[cc_wq_in.ap().opt()], outs=[cc_wq_out.ap().opt()]
                             ).then_inc(k.sem["cc"], 1)
        k.cnt["cc"] += 1
        sb.pop()
        tap("wstat", sc[:, 0:32])

    if upto >= 2:
        sb.push()
        tri_s = sb.alloc("tri_s", [128, CH], F32)
        wnorm_s = sb.alloc("wnorm_s", [128, DV], F32)
        gate_s = sb.alloc("gate_s", [128, 1], F32)
        k.dh(tri_s[:, :], tri_gla[:, :])
        k.dh(wnorm_s[:, :], Wnorm_rep[:, :])
        k.dh(gate_s[:, :], gla_gate[:, :])
        xln_fm = sb.alloc("xln_fm", [128, NDIN * TOK], BF16)
        scratch = sb.alloc("lnscr", [128, 2048], F32)
        qdec = sb.alloc("qdec", [128, HG * NDKC * TOK], BF16)
        o_all = sb.alloc("o_all", [128, NTOKC * VD], BF16)
        g_all = sb.alloc("g_all", [128, NTOKC * VD], BF16)
        kbuf = sb.alloc("kbuf", [128, NDKC * TOK], BF16)
        vbuf = sb.alloc("vbuf", [128, NTOKC * DV], BF16)
        v_odd = sb.alloc("v_odd", [64, NTOKC * DV], BF16)
        Sf = sb.alloc("Sf", [128, NDKC * DV], F32)
        Sb = sb.alloc("Sb", [128, NDKC * DV], BF16)
        t_fm = sb.alloc("t_fm", [16, TOK], BF16)
        wgk1_s = sb.alloc("wgk1_s", [128, NDIN * 16], BF16)
        wgk2_s = sb.alloc("wgk2_s", [16, D], BF16)
        wbuf0 = sb.alloc("whead0", [128, NDIN * 256], BF16)

        kkT = sb.alloc("kkT", [128, DK], BF16)
        abuf = sb.alloc("abuf", [128, CH], BF16)
        nbg = sb.alloc("nbg", [128, HG * NDKC], F32)
        bgf = sb.alloc("bgf", [128, HG * NDKC], F32)

        ln_to_fm(xln_fm, scratch, scratch)
        tap("xln_fm", xln_fm[:, :])

        k.dh(bgf[:, :], bgk2[:, :])
        k.wait_dma(V)
        V.tensor_scalar(nbg[:, :], bgf[:, :], -1.0, None, ALU.mult)
        V.drain()
        k.dg(wgk1_s.ap().rearrange("p (c o) -> p c o", o=16),
             Wgk1.ap().rearrange("(c p) o -> p c o", p=128))
        k.dg(wgk2_s[:, :], Wgk2[:, :])
        k.bar()
        for half in range(2):
            for dc in range(NDIN):
                P.matmul(PB[4][0:16, 0:512], wgk1_s[:, dc * 16:(dc + 1) * 16],
                         xln_fm[:, dc * TOK + half * 512: dc * TOK + (half + 1) * 512],
                         start=(dc == 0), stop=(dc == NDIN - 1))
            k.edge(P, V)
            V.tensor_copy(t_fm[0:16, half * 512:(half + 1) * 512], PB[4][0:16, 0:512])
            k.edge(V, P)
        k.bar()
        tap("t_fm", t_fm[:, :])

        for h in range(HG):
            qh = qdec[:, h * NDKC * TOK:(h + 1) * NDKC * TOK]
            for dst, wsrc, scale_, buf in ((qh, Wq_a, SCALE_GLA, "q"), (None, Wk_a, None, "k")):
                for chf in range(2):
                    k.dg(wbuf0.ap().rearrange("p (c o) -> p c o", o=256),
                         wsrc.ap()[h, chf, :, :].rearrange("(c p) o -> p c o", p=128))
                    k.bar()
                    for dkc in (chf * 2, chf * 2 + 1):
                        dkl = dkc - chf * 2
                        for tb in range(2):
                            pbi = PB[tb]
                            for dc in range(NDIN):
                                P.matmul(pbi[:, :],
                                         wbuf0[:, dc * 256 + dkl * 128: dc * 256 + (dkl + 1) * 128],
                                         xln_fm[:, dc * TOK + tb * 512: dc * TOK + (tb + 1) * 512],
                                         start=(dc == 0), stop=(dc == NDIN - 1))
                        k.edge(P, V)
                        for tb in range(2):
                            if buf == "q":
                                V.tensor_scalar(dst[:, dkc * TOK + tb * 512: dkc * TOK + (tb + 1) * 512],
                                                PB[tb][:, :], scale_, None, ALU.mult)
                            else:
                                V.tensor_copy(kbuf[:, dkc * TOK + tb * 512: dkc * TOK + (tb + 1) * 512],
                                              PB[tb][:, :])
                        k.edge(V, P)
                    k.bar()
            k.dg(wbuf0.ap().rearrange("p (c o) -> p c o", o=DV),
                 Wv_a.ap()[h, :, :].rearrange("(c p) o -> p c o", p=128))
            k.bar()
            for tc in range(NTOKC):
                pbi = PB[tc % 2]
                for dc in range(NDIN):
                    P.matmul(pbi[:, 0:DV],
                             xln_fm[:, dc * TOK + tc * 128: dc * TOK + (tc + 1) * 128],
                             wbuf0[:, dc * DV:(dc + 1) * DV],
                             start=(dc == 0), stop=(dc == NDIN - 1))
                k.edge(P, V)
                V.tensor_copy(vbuf[:, tc * DV:(tc + 1) * DV], pbi[:, 0:DV])
                k.edge(V, P)
            k.bar()
            k.dh(v_odd[0:64, :], vbuf[64:128, :])
            k.bar()
            if h == 0:
                tap("q0", qdec[:, 0:NDKC * TOK])
                tap("k0", kbuf[:, :])
                tap("v0", vbuf[:, :])
            lsq = scratch[:, 1536:1536 + NDKC * CH]
            bfm = scratch[:, 1792:1792 + NDKC * CH]
            scr_bf = scratch.ap().bitcast(BF16)
            qet = scr_bf[:, 512:512 + NDKC * CH]
            kdt = scr_bf[:, 768:768 + NDKC * CH]
            ebt = scratch[:, 768:768 + NDKC * CH]
            enbt = scratch[:, 1024:1024 + NDKC * CH]
            kkf = scratch[:, 1280:1280 + NDKC * CH]
            V.memset(Sf[:, :], 0.0)
            V.memset(Sb[:, :], 0.0)
            V.memset(sc[:, 32:36], 0.0)
            V.drain()
            k.bar()
            for c in range(NCH):
                od = (c % 2) * 64
                tc = c // 2
                for dkc in range(NDKC):
                    P.matmul(PB[2][:, dkc * CH:(dkc + 1) * CH],
                             wgk2_s[0:16, (h * DK + dkc * 128):(h * DK + dkc * 128) + 128],
                             t_fm[0:16, c * CH:(c + 1) * CH], start=True, stop=True)
                k.edge(P, S)
                zabs = scratch[:, 0:NDKC * CH]
                rmin = scratch[:, 512:512 + NDKC * CH]
                for dkc in range(NDKC):
                    S.activation(zabs[:, dkc * CH:(dkc + 1) * CH],
                                 PB[2][:, dkc * CH:(dkc + 1) * CH],
                                 ACT.Abs, bias=bgf[:, h * NDKC + dkc: h * NDKC + dkc + 1], scale=1.0)
                    S.activation(rmin[:, dkc * CH:(dkc + 1) * CH],
                                 PB[2][:, dkc * CH:(dkc + 1) * CH],
                                 ACT.Relu, bias=nbg[:, h * NDKC + dkc: h * NDKC + dkc + 1], scale=-1.0)
                S.drain()
                S.activation(zabs, zabs, ACT.Exp, bias=0.0, scale=-1.0)
                S.drain()
                S.activation(zabs, zabs, ACT.Ln, bias=sc[:, 61:62], scale=1.0)
                k.edge(S, V, P)
                V.scalar_tensor_tensor(lsq, rmin, -1.0, zabs, ALU.mult, ALU.subtract)
                V.drain()
                for dkc in range(NDKC):
                    V.tensor_tensor_scan(bfm[:, dkc * CH:(dkc + 1) * CH],
                                         lsq[:, dkc * CH:(dkc + 1) * CH],
                                         lsq[:, dkc * CH:(dkc + 1) * CH], 0.0, ALU.add, ALU.bypass)
                k.edge(V, S)
                S.activation(ebt, bfm, ACT.Exp, bias=0.0, scale=S16)
                S.activation(enbt, bfm, ACT.Exp, bias=0.0, scale=-S16)
                blast_ap = bfm.rearrange("p (a s) -> p a s", s=CH)[:, :, CH - 1:CH].opt()
                S.activation(sc[:, 36:40], blast_ap, ACT.Exp, bias=0.0, scale=S16)
                S.activation(sc[:, 40:44], sc[:, 32:36], ACT.Exp, bias=0.0, scale=S16)
                k.edge(S, V)
                V.tensor_tensor(sc[:, 32:36], sc[:, 32:36], blast_ap, ALU.add)
                V.drain()
                q_ap = AP(qdec, h * NDKC * TOK + c * CH, [[HG * NDKC * TOK, 128], [TOK, NDKC], [1, CH]])
                k_ap = AP(kbuf, c * CH, [[NDKC * TOK, 128], [TOK, NDKC], [1, CH]])
                qe3 = qet.rearrange("p (a s) -> p a s", s=CH)
                V.tensor_tensor(qe3, q_ap, ebt.rearrange("p (a s) -> p a s", s=CH), ALU.mult)
                V.tensor_tensor(kdt.rearrange("p (a s) -> p a s", s=CH), k_ap,
                                enbt.rearrange("p (a s) -> p a s", s=CH), ALU.mult)
                V.drain()
                for dkc in range(NDKC):
                    V.scalar_tensor_tensor(kkf[:, dkc * CH:(dkc + 1) * CH],
                                           kdt[:, dkc * CH:(dkc + 1) * CH],
                                           sc[:, 36 + dkc:37 + dkc],
                                           kdt[:, dkc * CH:(dkc + 1) * CH],
                                           ALU.mult, ALU.bypass)
                V.drain()
                for dkc in range(NDKC):
                    V.tensor_scalar(AP(qdec, h * NDKC * TOK + dkc * TOK + c * CH,
                                       [[HG * NDKC * TOK, 128], [1, CH]]),
                                    qet[:, dkc * CH:(dkc + 1) * CH],
                                    sc[:, 40 + dkc:41 + dkc], None, ALU.mult)
                k.edge(V, P)
                for dkc in range(NDKC):
                    P.matmul(PB[3][od:od + 64, 0:64], kdt[:, dkc * CH:(dkc + 1) * CH],
                             qet[:, dkc * CH:(dkc + 1) * CH],
                             start=(dkc == 0), stop=(dkc == NDKC - 1), tile_position=(0, od))
                for dkc in range(NDKC):
                    P.transpose(PB[4][0:64, dkc * 128:(dkc + 1) * 128],
                                kkf[:, dkc * CH:(dkc + 1) * CH], ident_f[:, :])
                k.edge(P, V)
                V.tensor_tensor(abuf[od:od + 64, :], PB[3][od:od + 64, 0:64],
                                tri_s[od:od + 64, :], ALU.mult)
                V.tensor_copy(kkT[0:64, :], PB[4][0:64, :])
                k.edge(V, P)
                P.matmul(PB[5][od:od + 64, 0:DV], abuf[od:od + 64, :],
                         vbuf[od:od + 64, tc * DV:(tc + 1) * DV], start=True, stop=False,
                         tile_position=(od, od))
                for dkc in range(NDKC):
                    P.matmul(PB[5][od:od + 64, 0:DV], qet[:, dkc * CH:(dkc + 1) * CH],
                             Sb[:, dkc * DV:(dkc + 1) * DV], start=False,
                             stop=(dkc == NDKC - 1), tile_position=(0, od))
                v_u = vbuf[0:64, tc * DV:(tc + 1) * DV] if od == 0 else \
                    v_odd[0:64, tc * DV:(tc + 1) * DV]
                for dkc in range(NDKC):
                    P.matmul(PB[6 + dkc // 2][:, (dkc % 2) * DV:(dkc % 2 + 1) * DV],
                             kkT[0:64, dkc * 128:(dkc + 1) * 128],
                             v_u, start=True, stop=True, tile_position=(0, 0))
                k.edge(P, V)
                V.tensor_copy(o_all[od:od + 64, tc * VD + h * DV: tc * VD + (h + 1) * DV],
                              PB[5][od:od + 64, 0:DV])
                for dkc in range(NDKC):
                    V.scalar_tensor_tensor(Sf[:, dkc * DV:(dkc + 1) * DV],
                                           Sf[:, dkc * DV:(dkc + 1) * DV],
                                           sc[:, 36 + dkc:37 + dkc],
                                           PB[6 + dkc // 2][:, (dkc % 2) * DV:(dkc % 2 + 1) * DV],
                                           ALU.mult, ALU.add)
                V.drain()
                V.tensor_copy(Sb[:, :], Sf[:, :])
                k.edge(V, P)
            k.bar()
            k.dh(cc_S_in.ap().rearrange("(a p) v -> p a v", p=128)[:, h * NDKC:(h + 1) * NDKC, :],
                 Sf.ap().rearrange("p (a v) -> p a v", v=DV))
            k.bar()
            if h == 0:
                tap("S0f", Sf[:, :])
                tap("o_h0", o_all[:, 0:VD])
        k.wait_dma(G)
        G.collective_compute("AllGather", ALU.bypass,
                             replica_groups=[[0, 1], [2, 3], [4, 5], [6, 7]],
                             ins=[cc_S_in.ap().opt()], outs=[cc_S_out.ap().opt()]
                             ).then_inc(k.sem["cc"], 1)
        k.cnt["cc"] += 1
        for qtr in range(4):
            k.dg(wbuf0.ap().rearrange("p (c o) -> p c o", o=256)[:, 0:NDIN, :],
                 Wg_a.ap()[qtr, :, :].rearrange("(c p) o -> p c o", p=128))
            k.bar()
            for tc in range(NTOKC):
                pbi = PB[tc % 2]
                for dc in range(NDIN):
                    P.matmul(pbi[:, 0:256], xln_fm[:, dc * TOK + tc * 128: dc * TOK + (tc + 1) * 128],
                             wbuf0[:, dc * 256:(dc + 1) * 256],
                             start=(dc == 0), stop=(dc == NDIN - 1))
                k.edge(P, V)
                V.tensor_copy(g_all[:, tc * VD + qtr * 256: tc * VD + (qtr + 1) * 256],
                              pbi[:, 0:256])
                k.edge(V, P)
            k.bar()
        tap("g_all", g_all[:, :])
        G.wait_ge(k.sem["cc"], k.cnt["cc"])
        s0buf = kbuf
        for h in range(HG):
            k.dg(s0buf.ap().rearrange("p (a v) -> p a v", v=DV)[:, 0:NDKC, :],
                 cc_S_out.ap()[0, :, :].rearrange("(a p) v -> p a v", p=128)
                 [:, h * NDKC:(h + 1) * NDKC, :])
            k.bar()
            qh = qdec[:, h * NDKC * TOK:(h + 1) * NDKC * TOK]
            for tc in range(NTOKC):
                pbi = PB[2 + tc % 2]
                for dkc in range(NDKC):
                    P.matmul(pbi[:, 0:DV], qh[:, dkc * TOK + tc * 128: dkc * TOK + (tc + 1) * 128],
                             s0buf[:, dkc * DV:(dkc + 1) * DV],
                             start=(dkc == 0), stop=(dkc == NDKC - 1))
                k.edge(P, V)
                dst = o_all[:, tc * VD + h * DV: tc * VD + (h + 1) * DV]
                V.scalar_tensor_tensor(dst, pbi[:, 0:DV], gate_s[:, 0:1], dst, ALU.mult, ALU.add)
                k.edge(V, P)
            k.bar()
        k.bar()
        tap("o_corr", o_all[:, :])
        z_fm = qdec
        for tc in range(NTOKC):
            for h in range(HG):
                o_sl = o_all[:, tc * VD + h * DV: tc * VD + (h + 1) * DV]
                g_sl = g_all[:, tc * VD + h * DV: tc * VD + (h + 1) * DV]
                V.scalar_tensor_tensor(scratch[:, 0:DV], o_sl, 1.0, o_sl, ALU.bypass, ALU.mult,
                                       accum_out=sc[:, 44:45])
                k.edge(V, S)
                S.activation(sc[:, 45:46], sc[:, 44:45], ACT.Ln, bias=sc[:, 60:61], scale=1.0 / DV)
                S.drain()
                S.activation(sc[:, 46:47], sc[:, 45:46], ACT.Exp, bias=0.0, scale=-0.5)
                S.activation(scratch[:, 1024:1024 + DV], g_sl, ACT.Silu, bias=0.0, scale=1.0)
                k.edge(S, V)
                V.tensor_scalar(scratch[:, 0:DV], o_sl, sc[:, 46:47], None, ALU.mult)
                V.drain()
                V.tensor_tensor(scratch[:, 0:DV], scratch[:, 0:DV], wnorm_s[:, :], ALU.mult)
                V.drain()
                V.tensor_tensor(scratch[:, 256:256 + DV], scratch[:, 0:DV],
                                scratch[:, 1024:1024 + DV], ALU.mult)
                k.edge(V, P)
                for j in range(2):
                    vc = 2 * h + j
                    pbi = PB[j]
                    P.transpose(pbi[:, 0:128], scratch[:, 256 + j * 128:256 + (j + 1) * 128],
                                ident_f[:, :])
                k.edge(P, V)
                for j in range(2):
                    vc = 2 * h + j
                    V.tensor_copy(z_fm[:, vc * TOK + tc * 128: vc * TOK + (tc + 1) * 128],
                                  PB[j][:, 0:128])
                k.edge(V, P)
        k.bar()
        tap("z_fm", z_fm[:, 0:NVDC * TOK])
        for qtr in range(4):
            k.dg(wbuf0.ap().rearrange("p (c o) -> p c o", o=512)[:, 0:NVDC, :],
                 Wo_a.ap()[qtr, :, :].rearrange("(c p) o -> p c o", p=128))
            k.bar()
            for tc in range(NTOKC):
                pbi = PB[tc % 2]
                for vc in range(NVDC):
                    P.matmul(pbi[:, :], z_fm[:, vc * TOK + tc * 128: vc * TOK + (tc + 1) * 128],
                             wbuf0[:, vc * 512:(vc + 1) * 512],
                             start=(vc == 0), stop=(vc == NVDC - 1))
                k.edge(P, V)
                V.tensor_tensor(resid[:, tc * D + qtr * 512: tc * D + qtr * 512 + 512],
                                resid[:, tc * D + qtr * 512: tc * D + qtr * 512 + 512],
                                pbi[:, :], ALU.add)
                k.edge(V, P)
            k.bar()
        k.bar()
        sb.pop()
        tap("x1", resid[:, :])




    def emit_ffn(tag):
        sb.push()
        xq_fm = sb.alloc(f"xqfm{tag}", [128, NDIN * TOK], BF16)
        scratch = sb.alloc(f"fscr{tag}", [128, D], F32)
        qscr = sb.alloc(f"qscr{tag}", [128, D], I32)
        slab_g = sb.alloc(f"slabg{tag}", [128, 2 * NDIN * 512], BF16)
        slab_u = sb.alloc(f"slabu{tag}", [128, 2 * NDIN * 512], BF16)
        sgt = sb.alloc(f"sgt{tag}", [128, 2 * 512], BF16)
        ut = sb.alloc(f"ut{tag}", [128, 2 * 512], F32)
        for tc in range(NTOKC):
            xt = resid[:, tc * D:(tc + 1) * D]
            V.reduce_sum(sc[:, 0:1], xt, axis=AX)
            V.scalar_tensor_tensor(scratch[:, 0:D], xt, 1.0, xt, ALU.bypass, ALU.mult,
                                   accum_out=sc[:, 1:2])
            V.drain()
            V.tensor_scalar(sc[:, 2:3], sc[:, 0:1], 1.0 / D, None, ALU.mult)
            V.drain()
            V.tensor_tensor(sc[:, 3:4], sc[:, 2:3], sc[:, 2:3], ALU.mult)
            V.tensor_scalar(sc[:, 4:5], sc[:, 1:2], 1.0 / D, None, ALU.mult)
            V.drain()
            V.tensor_tensor(sc[:, 5:6], sc[:, 4:5], sc[:, 3:4], ALU.subtract)
            k.edge(V, S)
            S.activation(sc[:, 6:7], sc[:, 5:6], ACT.Ln, bias=sc[:, 60:61], scale=1.0)
            S.drain()
            S.activation(sc[:, 7:8], sc[:, 6:7], ACT.Exp, bias=0.0, scale=-0.5)
            k.edge(S, V)
            V.tensor_scalar(scratch[:, 0:D], xt, sc[:, 2:3], sc[:, 7:8],
                            ALU.subtract, ALU.mult)
            V.drain()
            V.reduce_max(sc[:, 8:9], scratch[:, 0:D], axis=AX, apply_absolute_value=True)
            V.drain()
            V.tensor_scalar(sc[:, 16 + tc:17 + tc], sc[:, 8:9], 1.0 / 127.0, EPS,
                            ALU.mult, ALU.add)
            V.drain()
            V.reciprocal(sc[:, 9:10], sc[:, 16 + tc:17 + tc])
            V.drain()
            V.tensor_scalar(scratch[:, 0:D], scratch[:, 0:D], sc[:, 9:10], None, ALU.mult)
            V.drain()
            V.tensor_copy(qscr[:, :], scratch[:, 0:D])
            V.drain()
            V.tensor_copy(scratch[:, 0:D], qscr[:, :])
            k.edge(V, P)
            for dc in range(0, NDIN, 2):
                P.transpose(PB[6][:, 0:128], scratch[:, dc * 128:(dc + 1) * 128], ident_f[:, :])
                P.transpose(PB[7][:, 0:128], scratch[:, (dc + 1) * 128:(dc + 2) * 128], ident_f[:, :])
                k.edge(P, V)
                V.tensor_copy(xq_fm[:, dc * TOK + tc * 128: dc * TOK + (tc + 1) * 128],
                              PB[6][:, 0:128])
                V.tensor_copy(xq_fm[:, (dc + 1) * TOK + tc * 128: (dc + 1) * TOK + (tc + 1) * 128],
                              PB[7][:, 0:128])
                k.edge(V, P)
        V.tensor_scalar(sc[:, 40:48], sc[:, 16:24], sc[:, 28:29], -1.0, ALU.mult, ALU.mult)
        V.tensor_scalar(sc[:, 48:56], sc[:, 16:24], sc[:, 29:30], -1.0, ALU.mult, ALU.mult)
        V.memset(sc[:, 32:40], 0.0)
        V.drain()
        k.bar()
        tap(f"xqfm_{tag}", xq_fm[:, :])
        G.wait_ge(k.sem["cc"], k.cnt["cc"])
        def slab_ap(mat_off, IB, par):
            r = IB // 2
            w0 = (IB % 2) * 512
            return AP(cc_wq_out, r * WQ_SL + mat_off + w0,
                      [[1024, 128], [128 * 1024, NDIN], [1, 512]])
        pe0 = k.cnt["pe"]; ve0 = k.cnt["ve"]; h0 = k.cnt["h"]; gcA = []
        k.dg(slab_g.ap().rearrange("p (s c o) -> p s c o", s=2, o=512)[:, 0, :, :], slab_ap(0, 0, 0))
        k.dg(slab_u.ap().rearrange("p (s c o) -> p s c o", s=2, o=512)[:, 0, :, :], slab_ap(WQ_G, 0, 0))
        gcA.append(k.cnt["g"])
        for IB in range(16):
            if IB + 1 < 16:
                G.wait_ge(k.sem["pe"], pe0 + IB * 8)
                par2 = (IB + 1) % 2
                k.dg(slab_g.ap().rearrange("p (s c o) -> p s c o", s=2, o=512)[:, par2, :, :],
                     slab_ap(0, IB + 1, par2))
                k.dg(slab_u.ap().rearrange("p (s c o) -> p s c o", s=2, o=512)[:, par2, :, :],
                     slab_ap(WQ_G, IB + 1, par2))
                gcA.append(k.cnt["g"])
            sp = IB % 2
            for tc in range(NTOKC):
                u = IB * 8 + tc
                par = u % 2
                pg, pu = PB[par * 2], PB[par * 2 + 1]
                sg_t = sgt[:, par * 512:(par + 1) * 512]
                u_t = ut[:, par * 512:(par + 1) * 512]
                if u >= 2:
                    P.wait_ge(k.sem["ve"], ve0 + (u - 1) * 2)
                if tc == 0:
                    P.wait_ge(k.sem["g"], gcA[IB])
                for dc in range(NDIN):
                    P.matmul(pg, xq_fm[:, dc * TOK + tc * 128: dc * TOK + (tc + 1) * 128],
                             slab_g[:, sp * NDIN * 512 + dc * 512: sp * NDIN * 512 + (dc + 1) * 512],
                             start=(dc == 0), stop=(dc == NDIN - 1))
                for dc in range(NDIN):
                    P.matmul(pu, xq_fm[:, dc * TOK + tc * 128: dc * TOK + (tc + 1) * 128],
                             slab_u[:, sp * NDIN * 512 + dc * 512: sp * NDIN * 512 + (dc + 1) * 512],
                             start=(dc == 0), stop=(dc == NDIN - 1))
                P.drain().then_inc(k.sem["pe"], 1); k.cnt["pe"] += 1
                V.wait_ge(k.sem["pe"], pe0 + u + 1)
                if u >= 2:
                    V.wait_ge(k.sem["h"], h0 + (u - 1) * 16)
                scr_t = scratch[:, par * 512:(par + 1) * 512]
                V.tensor_scalar(scr_t, pg, sc[:, 40 + tc:41 + tc], None, ALU.mult)
                k.edge(V, S)
                S.activation(sg_t, scr_t, ACT.Silu, bias=0.0, scale=1.0)
                k.edge(S, V)
                V.scalar_tensor_tensor(u_t, pu, sc[:, 48 + tc:49 + tc], sg_t,
                                       ALU.mult, ALU.mult)
                V.drain()
                V.reduce_max(sc[:, 10 + par:11 + par], u_t, axis=AX, apply_absolute_value=True)
                V.drain()
                V.tensor_tensor(sc[:, 32 + tc:33 + tc], sc[:, 32 + tc:33 + tc],
                                sc[:, 10 + par:11 + par], ALU.max)
                V.drain().then_inc(k.sem["ve"], 1); k.cnt["ve"] += 1
                nc.sync.wait_ge(k.sem["ve"], k.cnt["ve"])
                k.dh(AP(u_spill, (tc * 128) * FI + IB * 512, [[FI, 128], [1, 512]]), u_t)
        k.bar()
        V.tensor_scalar(sc[:, 32:40], sc[:, 32:40], 1.0 / 127.0, EPS, ALU.mult, ALU.add)
        V.drain()
        V.tensor_scalar(sc[:, 48:56], sc[:, 32:40], sc[:, 30:31], -1.0, ALU.mult, ALU.mult)
        for tc in range(NTOKC):
            V.reciprocal(sc[:, 0 + tc:1 + tc], sc[:, 32 + tc:33 + tc])
        V.drain()
        k.bar()
        sb.pop()
        sb.push()
        xqu_tm = sb.alloc(f"xqutm{tag}", [128, NTOKC * FI], I8)
        urow = sb.alloc(f"urow{tag}", [128, 2 * (FI // 16)], F32)
        scratch = sb.alloc(f"fscr2{tag}", [128, 512], F32)
        qscr = sb.alloc(f"qscr2{tag}", [128, 512], I32)
        units_c = [(tc, hf) for tc in range(NTOKC) for hf in range(16)]
        ve0 = k.cnt["ve"]; h0 = k.cnt["h"]
        def urow_ap(tc, hf):
            return AP(u_spill, (tc * 128) * FI + hf * 512, [[FI, 128], [1, 512]])
        k.dh(urow[:, 0:512], urow_ap(*units_c[0]))
        for u, (tc, hf) in enumerate(units_c):
            if u + 1 < len(units_c):
                nc.sync.wait_ge(k.sem["ve"], ve0 + u)
                k.dh(urow.ap().rearrange("p (s n) -> p s n", s=2)[:, (u + 1) % 2, :],
                     urow_ap(*units_c[u + 1]))
            V.wait_ge(k.sem["h"], h0 + (u + 1) * 16)
            ur = urow[:, (u % 2) * 512:(u % 2 + 1) * 512]
            V.tensor_scalar(scratch[:, 0:512], ur, sc[:, 0 + tc:1 + tc], None, ALU.mult)
            V.drain()
            V.tensor_copy(qscr[:, :], scratch[:, 0:512])
            V.drain()
            off = tc * FI + hf * 512
            V.tensor_copy(xqu_tm[:, off:off + 512], qscr[:, :])
            V.drain().then_inc(k.sem["ve"], 1); k.cnt["ve"] += 1
        k.bar()
        if f"usp_{tag}" in taps:
            dtmp = nc.dram_tensor(f"dbg_usp_{tag}", [TOK, FI], BF16, kind="ExternalOutput")
            k.dg(dtmp.ap(), u_spill.ap())
            k.bar()
            dbg[f"usp_{tag}"] = 1
        tap(f"xqu_{tag}", xqu_tm[:, :])
        tap(f"scs_{tag}", sc[:, 0:64])
        xqu_fm = sb.alloc(f"xqufm{tag}", [128, 64 * TOK], I8)
        units_d = [(ic, tc) for ic in range(64) for tc in range(0, NTOKC, 2)]
        pe0 = k.cnt["pe"]; ve0 = k.cnt["ve"]
        for u, (ic, tc) in enumerate(units_d):
            par = u % 2
            so = par * 256
            if u >= 2:
                V.wait_ge(k.sem["pe"], pe0 + u - 1)
            V.tensor_copy(scratch[:, so:so + 128],
                          xqu_tm[:, tc * FI + ic * 128: tc * FI + (ic + 1) * 128])
            V.tensor_copy(scratch[:, so + 128:so + 256],
                          xqu_tm[:, (tc + 1) * FI + ic * 128: (tc + 1) * FI + (ic + 1) * 128])
            V.drain().then_inc(k.sem["ve"], 1); k.cnt["ve"] += 1
            P.wait_ge(k.sem["ve"], ve0 + 2 * u + 1)
            if u >= 2:
                P.wait_ge(k.sem["ve"], ve0 + 2 * (u - 2) + 2)
            P.transpose(PB[4 + par * 2][:, 0:128], scratch[:, so:so + 128], ident_f[:, :])
            P.transpose(PB[5 + par * 2][:, 0:128], scratch[:, so + 128:so + 256], ident_f[:, :])
            P.drain().then_inc(k.sem["pe"], 1); k.cnt["pe"] += 1
            V.wait_ge(k.sem["pe"], pe0 + u + 1)
            V.tensor_copy(xqu_fm[:, ic * TOK + tc * 128: ic * TOK + (tc + 1) * 128],
                          PB[4 + par * 2][:, 0:128])
            V.tensor_copy(xqu_fm[:, ic * TOK + (tc + 1) * 128: ic * TOK + (tc + 2) * 128],
                          PB[5 + par * 2][:, 0:128])
            V.drain().then_inc(k.sem["ve"], 1); k.cnt["ve"] += 1
        P.wait_ge(k.sem["ve"], k.cnt["ve"])
        k.bar()
        wdb = sb.alloc(f"wdb{tag}", [128, 2 * 1024], BF16)
        xqb = sb.alloc(f"xqb{tag}", [128, 2 * 512], BF16)
        def wdb_ap(oh, ic):
            return AP(cc_wq_out, (4 * oh) * WQ_SL + 2 * WQ_G + ic * 128 * 256,
                      [[256, 128], [WQ_SL, 4], [1, 256]])
        for oh in range(2):
            for th in range(2):
                pe0 = k.cnt["pe"]; ve0 = k.cnt["ve"]; g0 = k.cnt["g"]
                k.dg(wdb.ap().rearrange("p (s n) -> p s n", s=2)[:, 0, :], wdb_ap(oh, 0))
                for ic in range(64):
                    par = ic % 2
                    if ic + 1 < 64:
                        G.wait_ge(k.sem["pe"], pe0 + ic)
                        k.dg(wdb.ap().rearrange("p (s n) -> p s n", s=2)[:, (ic + 1) % 2, :],
                             wdb_ap(oh, ic + 1))
                    if ic >= 2:
                        V.wait_ge(k.sem["pe"], pe0 + ic - 1)
                    V.wait_ge(k.sem["g"], g0 + (ic + 1) * 16)
                    V.tensor_copy(xqb[:, par * 512:(par + 1) * 512],
                                  xqu_fm[:, ic * TOK + th * 512: ic * TOK + (th + 1) * 512])
                    V.drain().then_inc(k.sem["ve"], 1); k.cnt["ve"] += 1
                    P.wait_ge(k.sem["ve"], ve0 + ic + 1)
                    for tcl in range(4):
                        for ob in range(2):
                            P.matmul(PB[tcl * 2 + ob], xqb[:, par * 512 + tcl * 128: par * 512 + (tcl + 1) * 128],
                                     wdb[:, par * 1024 + ob * 512: par * 1024 + (ob + 1) * 512],
                                     start=(ic == 0), stop=(ic == 63))
                    P.drain().then_inc(k.sem["pe"], 1); k.cnt["pe"] += 1
                k.edge(P, V)
                for tcl in range(4):
                    tc = th * 4 + tcl
                    for ob in range(2):
                        col = oh * 1024 + ob * 512
                        V.scalar_tensor_tensor(resid[:, tc * D + col: tc * D + col + 512],
                                               PB[tcl * 2 + ob], sc[:, 48 + tc:49 + tc],
                                               resid[:, tc * D + col: tc * D + col + 512],
                                               ALU.mult, ALU.add)
                k.edge(V, P)
                k.bar()
        k.bar()
        sb.pop()

    if upto >= 3:
        emit_ffn("f1")
        tap("x2", resid[:, :])

    if upto >= 4:
        sb.push()
        q_fm = sb.alloc("bqfm", [128, HB * TOK], BF16)
        k_fm = sb.alloc("bkfm", [128, KVH * 2 * TOK], BF16)
        v_tm = sb.alloc("bvtm", [128, 16 * KVH * DH], BF16)
        sb.push()
        xln_fm = sb.alloc("bxln", [128, NDIN * TOK], BF16)
        sb.push()
        lnscr = sb.alloc("blnscr", [128, D], F32)
        ln_to_fm(xln_fm, lnscr, lnscr)
        sb.pop()
        slab = sb.alloc("bslab", [128, NDIN * 512], BF16)
        scratch = sb.alloc("bscr", [128, 1536], F32)
        cos_s = sb.alloc("bcos", [128, NTOKC * DH], BF16)
        sin_s = sb.alloc("bsin", [128, NTOKC * DH], BF16)

        k.dg(cos_s.ap().rearrange("p (c d) -> p c d", d=DH),
             cos_t.ap().rearrange("(c p) d -> p c d", p=128))
        k.dg(sin_s.ap().rearrange("p (c d) -> p c d", d=DH),
             sin_t.ap().rearrange("(c p) d -> p c d", p=128))
        k.bar()

        def rope_tile(tc, nh):
            cbc = AP(cos_s, tc * DH, [[NTOKC * DH, 128], [0, nh], [1, DH]])
            sA1 = AP(sin_s, tc * DH, [[NTOKC * DH, 128], [0, nh], [1, 64]])
            sA2 = AP(sin_s, tc * DH + 64, [[NTOKC * DH, 128], [0, nh], [1, 64]])
            src3 = AP(scratch, 0, [[1536, 128], [DH, nh], [1, DH]])
            src_lo = AP(scratch, 0, [[1536, 128], [DH, nh], [1, 64]])
            src_hi = AP(scratch, 64, [[1536, 128], [DH, nh], [1, 64]])
            dst3 = AP(scratch, 512, [[1536, 128], [DH, nh], [1, DH]])
            dst_lo = AP(scratch, 512, [[1536, 128], [DH, nh], [1, 64]])
            dst_hi = AP(scratch, 512 + 64, [[1536, 128], [DH, nh], [1, 64]])
            tmp = AP(scratch, 1024, [[1536, 128], [64, nh], [1, 64]])
            V.tensor_tensor(dst3, src3, cbc, ALU.mult)
            V.drain()
            V.tensor_tensor(tmp, src_hi, sA1, ALU.mult)
            V.drain()
            V.tensor_tensor(dst_lo, dst_lo, tmp, ALU.add)
            V.drain()
            V.tensor_tensor(tmp, src_lo, sA2, ALU.mult)
            V.drain()
            V.tensor_tensor(dst_hi, dst_hi, tmp, ALU.add)
            V.drain()

        for wsrc, kind in ((Wk_b, "k"), (Wv_b, "v")):
            k.dg(slab.ap().rearrange("p (c o) -> p c o", o=512)[:, 0:NDIN, :],
                 wsrc.ap().rearrange("(c p) o -> p c o", p=128))
            k.bar()
            for tc in range(NTOKC):
                pbi = PB[tc % 2]
                for dc in range(NDIN):
                    P.matmul(pbi, xln_fm[:, dc * TOK + tc * 128: dc * TOK + (tc + 1) * 128],
                             slab[:, dc * 512:(dc + 1) * 512],
                             start=(dc == 0), stop=(dc == NDIN - 1))
                k.edge(P, V)
                k.wait_dma(V)
                if kind == "k":
                    V.tensor_copy(scratch[:, 0:512], pbi)
                    V.drain()
                    rope_tile(tc, KVH)
                    V.tensor_copy(scratch.ap().bitcast(BF16)[:, 2048:2048 + 512],
                                  scratch[:, 512:1024])
                    V.drain()
                    col0 = 0
                else:
                    V.tensor_copy(scratch.ap().bitcast(BF16)[:, 2048:2048 + 512], pbi)
                    V.drain()
                    col0 = KVH * DH
                k.edge(V, nc.sync, P)
                k.dh(AP(cc_kv_in, (tc * 128) * (2 * KVH * DH) + col0,
                        [[2 * KVH * DH, 128], [1, 512]]),
                     scratch.ap().bitcast(BF16)[:, 2048:2048 + 512])
            k.bar()
        k.wait_dma(G)
        G.collective_compute("AllGather", ALU.bypass,
                             replica_groups=[[0, 1], [2, 3], [4, 5], [6, 7]],
                             ins=[cc_kv_in.ap().opt()], outs=[cc_kv_out.ap().opt()]
                             ).then_inc(k.sem["cc"], 1)
        k.cnt["cc"] += 1
        for qtr in range(4):
            k.dg(slab.ap().rearrange("p (c o) -> p c o", o=512)[:, 0:NDIN, :],
                 Wq_b.ap().rearrange("(c p) (b o) -> p c b o", p=128, o=512)[:, :, qtr, :])
            k.bar()
            for tc in range(NTOKC):
                pbi = PB[tc % 2]
                for dc in range(NDIN):
                    P.matmul(pbi, xln_fm[:, dc * TOK + tc * 128: dc * TOK + (tc + 1) * 128],
                             slab[:, dc * 512:(dc + 1) * 512],
                             start=(dc == 0), stop=(dc == NDIN - 1))
                k.edge(P, V)
                V.tensor_copy(scratch[:, 0:512], pbi)
                V.drain()
                rope_tile(tc, 4)
                k.edge(V, P)
                for j in range(4):
                    hh_ = qtr * 4 + j
                    P.transpose(PB[4 + j % 2][:, 0:128], scratch[:, 512 + j * 128:512 + (j + 1) * 128],
                                ident_f[:, :])
                    k.edge(P, V)
                    V.tensor_copy(q_fm[:, hh_ * TOK + tc * 128: hh_ * TOK + (tc + 1) * 128],
                                  PB[4 + j % 2][:, 0:128])
                    k.edge(V, P)
            k.bar()
        sb.pop()
        sb.push()
        o_fm = sb.alloc("bofm", [128, HB * TOK], BF16)
        scratch = sb.alloc("bscr2", [128, 1536], F32)
        pbuf = sb.alloc("bp", [128, 2048], BF16)
        pT = sb.alloc("bpT", [128, 2048], BF16)
        trim_s = sb.alloc("trims", [128, 128], F32)
        trir_s = sb.alloc("trirs", [128, 128], F32)
        bcols = sb.alloc("bcolsb", [128, 8], F32)
        ident_b2b = sb.alloc("identb2b", [128, 128], BF16)
        k.dg(ident_b2b[:, :], ident_c[:, :])
        wob_slab = sb.alloc("bwo", [128, NDIN * 512], BF16)
        k.dh(trim_s[:, :], diag_mask[:, :])
        k.dh(trir_s[:, :], tri_r[:, :])
        k.dh(bcols[:, 0:4], attnb_cols[:, :])
        V.memset(bcols[:, 6:7], NEG)
        V.drain()
        k.bar()
        for kt in range(16):
            sl_ = kt // 8
            r0 = (kt % 8) * 128
            k.dg(scratch[:, 0:512],
                 AP(cc_kv_out, (sl_ * TOK + r0) * (2 * KVH * DH), [[2 * KVH * DH, 128], [1, 512]]))
            k.bar()
            for j in range(KVH):
                P.transpose(PB[4][:, 0:128], scratch[:, j * 128:(j + 1) * 128], ident_f[:, :])
                k.edge(P, V)
                V.tensor_copy(k_fm[:, j * 2048 + kt * 128: j * 2048 + (kt + 1) * 128],
                              PB[4][:, 0:128])
                k.edge(V, P)
            k.bar()
        k.dg(v_tm.ap().rearrange("p (c o) -> p c o", o=512),
             AP(cc_kv_out, KVH * DH, [[2 * KVH * DH, 128], [128 * 2 * KVH * DH, 16], [1, 512]]))
        k.bar()
        pbT4 = pb_hi[0].ap().bitcast(BF16)
        pbT5 = pb_hi[1].ap().bitcast(BF16)
        units_a = [(hh_, qt) for hh_ in range(HB) for qt in range(NTOKC)]
        se_e = {}; ve_n = {}; ve_m = {}; pe_s = {}; pe_t = {}; ve_o = {}

        def emit_smms(u):
            hh_, qt = units_a[u]
            kv = hh_ // 4
            if u > 0:
                P.wait_ge(k.sem["se"], se_e[u - 1])
            for kb in range(4):
                P.matmul(PB[kb], q_fm[:, hh_ * TOK + qt * 128: hh_ * TOK + (qt + 1) * 128],
                         k_fm[:, kv * 2048 + kb * 512: kv * 2048 + (kb + 1) * 512],
                         start=True, stop=True)
            P.drain().then_inc(k.sem["pe"], 1); k.cnt["pe"] += 1
            pe_s[u] = k.cnt["pe"]

        def emit_soft(u):
            hh_, qt = units_a[u]
            V.wait_ge(k.sem["pe"], pe_s[u])
            if u >= 1:
                V.wait_ge(k.sem["pe"], pe_t.get(u - 1, 0))
            V.tensor_tensor(psS[:, qt * 128:(qt + 1) * 128],
                            psS[:, qt * 128:(qt + 1) * 128], trim_s[:, :], ALU.add)
            V.tensor_tensor(psS[:, 1024 + qt * 128: 1024 + (qt + 1) * 128],
                            psS[:, 1024 + qt * 128: 1024 + (qt + 1) * 128],
                            trir_s[:, :], ALU.add)
            V.drain()
            if qt < 7:
                nsfx = (7 - qt) * 128
                V.tensor_scalar(psS[:, (qt + 1) * 128:(qt + 1) * 128 + nsfx],
                                psS[:, (qt + 1) * 128:(qt + 1) * 128 + nsfx],
                                bcols[:, 0:1], None, ALU.add)
                V.tensor_scalar(psS[:, 1024 + (qt + 1) * 128: 1024 + (qt + 1) * 128 + nsfx],
                                psS[:, 1024 + (qt + 1) * 128: 1024 + (qt + 1) * 128 + nsfx],
                                bcols[:, 6:7], None, ALU.add)
            if qt > 0:
                V.tensor_scalar(psS[:, 1024:1024 + qt * 128],
                                psS[:, 1024:1024 + qt * 128],
                                bcols[:, 1:2], None, ALU.add)
            V.drain()
            V.reduce_max(sc[:, 0:1], psS[:, :], axis=AX)
            V.drain()
            V.tensor_scalar(sc[:, 5:6], sc[:, 0:1], -SCALE_B, None, ALU.mult)
            V.drain().then_inc(k.sem["ve"], 1); k.cnt["ve"] += 1
            ve_m[u] = k.cnt["ve"]
            S.wait_ge(k.sem["ve"], ve_m[u])
            S.activation(pbuf[:, :], psS[:, :], ACT.Exp,
                         bias=sc[:, 5:6], scale=SCALE_B, accum_out=sc[:, 12:13])
            S.drain().then_inc(k.sem["se"], 1); k.cnt["se"] += 1
            se_e[u] = k.cnt["se"]
            V.wait_ge(k.sem["se"], se_e[u])
            V.reciprocal(sc[:, 13:14], sc[:, 12:13])
            V.drain()
            V.tensor_scalar(pbuf[:, :], pbuf[:, :], sc[:, 13:14], None, ALU.mult)
            V.drain().then_inc(k.sem["ve"], 1); k.cnt["ve"] += 1
            ve_n[u] = k.cnt["ve"]

        def emit_pv(u):
            hh_, qt = units_a[u]
            kv = hh_ // 4
            P.wait_ge(k.sem["ve"], ve_n[u])
            if u >= 1:
                P.wait_ge(k.sem["ve"], ve_o.get(u - 1, 0))
            for j in range(8):
                P.transpose(pbT4[:, j * 128:(j + 1) * 128],
                            pbuf[:, j * 128:(j + 1) * 128], ident_b2b[:, :])
            for j in range(8):
                P.transpose(pbT5[:, j * 128:(j + 1) * 128],
                            pbuf[:, 1024 + j * 128: 1024 + (j + 1) * 128], ident_b2b[:, :])
            P.drain().then_inc(k.sem["pe"], 1); k.cnt["pe"] += 1
            pe_t[u] = k.cnt["pe"]
            V.wait_ge(k.sem["pe"], pe_t[u])
            V.tensor_copy(pT[:, 0:1024], pbT4[:, :])
            V.tensor_copy(pT[:, 1024:2048], pbT5[:, :])
            V.drain().then_inc(k.sem["ve"], 1); k.cnt["ve"] += 1
            P.wait_ge(k.sem["ve"], k.cnt["ve"])
            for kc in range(16):
                P.matmul(PB[6][:, 0:128], v_tm[:, kc * 512 + kv * 128: kc * 512 + (kv + 1) * 128],
                         pT[:, kc * 128:(kc + 1) * 128],
                         start=(kc == 0), stop=(kc == 15))
            P.drain().then_inc(k.sem["pe"], 1); k.cnt["pe"] += 1
            V.wait_ge(k.sem["pe"], k.cnt["pe"])
            V.tensor_copy(o_fm[:, hh_ * TOK + qt * 128: hh_ * TOK + (qt + 1) * 128],
                          PB[6][:, 0:128])
            V.drain().then_inc(k.sem["ve"], 1); k.cnt["ve"] += 1
            ve_o[u] = k.cnt["ve"]

        emit_smms(0)
        for u in range(len(units_a)):
            emit_soft(u)
            if u + 1 < len(units_a):
                emit_smms(u + 1)
            emit_pv(u)
        k.bar()
        tap("o_attnb", o_fm[:, :])
        for qtr in range(4):
            k.dg(wob_slab.ap().rearrange("p (c o) -> p c o", o=512)[:, 0:NDIN, :],
                 Wo_b.ap().rearrange("(c p) (b o) -> p c b o", p=128, o=512)[:, :, qtr, :])
            k.bar()
            for tc in range(NTOKC):
                pbi = PB[tc % 2]
                for hc in range(HB):
                    P.matmul(pbi, o_fm[:, hc * TOK + tc * 128: hc * TOK + (tc + 1) * 128],
                             wob_slab[:, hc * 512:(hc + 1) * 512],
                             start=(hc == 0), stop=(hc == HB - 1))
                k.edge(P, V)
                V.tensor_tensor(resid[:, tc * D + qtr * 512: tc * D + qtr * 512 + 512],
                                resid[:, tc * D + qtr * 512: tc * D + qtr * 512 + 512],
                                pbi, ALU.add)
                k.edge(V, P)
            k.bar()
        sb.pop()
        sb.pop()
        tap("x3", resid[:, :])

    if upto >= 5:
        emit_ffn("f2")

    k.bar()
    k.dh(out_d.ap().rearrange("(c p) d -> p c d", p=128),
         resid.ap().rearrange("p (c d) -> p c d", d=D))
    k.wait_dma(nc.sync)
    return nc, dbg


def stage_inputs(inputs):
    f = lambda a: np.ascontiguousarray(a, dtype=np.float32)
    x = np.asarray(inputs["L0_attn_a_x"])
    Wq_a = f(np.asarray(inputs["L0_attn_a_Wq"]).reshape(D, HG, 2, DK // 2).transpose(1, 2, 0, 3))
    Wk_a = f(np.asarray(inputs["L0_attn_a_Wk"]).reshape(D, HG, 2, DK // 2).transpose(1, 2, 0, 3))
    Wv_a = f(np.asarray(inputs["L0_attn_a_Wv"]).reshape(D, HG, DV).transpose(1, 0, 2))
    Wnorm_rep = f(np.broadcast_to(np.asarray(inputs["L0_attn_a_Wnorm"])[None, :], (128, DV)))
    tri = (np.arange(CH)[:, None] <= np.arange(CH)[None, :]).astype(np.float32)
    tri2 = f(np.concatenate([tri, tri], axis=0))
    tri_causal = np.where(np.arange(128)[None, :] <= np.arange(128)[:, None], 0.0, NEG).astype(np.float32)
    zeros128 = np.zeros((128, 128), np.float32)
    negs128 = np.full((128, 128), NEG, np.float32)
    ident = np.eye(128, dtype=np.float32)
    inv = 1.0 / 10000.0 ** (np.arange(0, DH, 2, dtype=np.float64) / DH)
    posf = np.arange(T, dtype=np.float64)[:, None] * inv[None, :]
    cos_full = np.concatenate([np.cos(posf)] * 2, axis=-1).astype(np.float32)
    sin_full = np.concatenate([-np.sin(posf), np.sin(posf)], axis=-1).astype(np.float32)
    wg = np.asarray(inputs["L0_ffn_w_gate"]); wu = np.asarray(inputs["L0_ffn_w_up"])
    wdn = np.asarray(inputs["L0_ffn_w_down"])

    in_maps = []
    for core in range(8):
        b, hh = core // 2, core % 2
        m = {
            "x_own": f(x[b, hh * TOK:(hh + 1) * TOK, :]),
            "Wq_a": Wq_a, "Wk_a": Wk_a, "Wv_a": Wv_a,
            "Wg_a": f(np.asarray(inputs["L0_attn_a_Wg"]).reshape(D, 4, VD // 4).transpose(1, 0, 2)),
            "Wgk1": f(inputs["L0_attn_a_Wgk1"]), "Wgk2": f(inputs["L0_attn_a_Wgk2"]),
            "bgk2": f(np.asarray(inputs["L0_attn_a_bgk2"]).reshape(HG * NDKC, 128).T),
            "Wnorm_rep": Wnorm_rep,
            "Wo_a": f(np.asarray(inputs["L0_attn_a_Wo"]).reshape(VD, 4, D // 4).transpose(1, 0, 2)),
            "Wq_b": f(inputs["L0_attn_b_Wq"]), "Wk_b": f(inputs["L0_attn_b_Wk"]),
            "Wv_b": f(inputs["L0_attn_b_Wv"]), "Wo_b": f(inputs["L0_attn_b_Wo"]),
            "wg_sl": f(wg[core * (FI // 8):(core + 1) * (FI // 8), :]),
            "wu_sl": f(wu[core * (FI // 8):(core + 1) * (FI // 8), :]),
            "wd_sl": f(wdn[core * (D // 8):(core + 1) * (D // 8), :]),
            "cos_t": f(cos_full[hh * TOK:(hh + 1) * TOK]),
            "sin_t": f(sin_full[hh * TOK:(hh + 1) * TOK]),
            "gla_gate": np.full((128, 1), float(hh), np.float32),
            "attnb_bias": np.full((128, 1), 0.0 if hh else NEG, np.float32),
            "tri_gla": tri2, "ident_c": ident,
            "diag_mask": tri_causal if hh == 0 else zeros128,
            "tri_r": negs128 if hh == 0 else tri_causal,
            "attnb_cols": (np.array([[NEG, NEG, 0.0, 0.0]], np.float32) if hh == 0
                           else np.array([[0.0, 0.0, 0.0, 0.0]], np.float32)
                           ).repeat(128, axis=0),
        }
        in_maps.append(m)
    return in_maps


def kernel(**inputs):
    nc, _ = build()
    in_maps = stage_inputs(inputs)
    res = run_bass_kernel_spmd(nc, in_maps, core_ids=list(range(8)))
    outs = np.empty((B, T, D), np.float32)
    for core in range(8):
        b, hh = core // 2, core % 2
        outs[b, hh * TOK:(hh + 1) * TOK, :] = res.results[core]["out"]
    return outs


if __name__ == "__main__":
    nc, dbg = build()
    print("built", nc.n_instructions())


# version 07 (final)
# speedup vs baseline: 1.5248x; 1.0437x over previous
import numpy as np
import ml_dtypes
import concourse.bass as bass
import concourse.mybir as mybir
from concourse.ap import AP
from concourse.bass_utils import run_bass_kernel_spmd

F32 = mybir.dt.float32
BF16 = mybir.dt.bfloat16
I32 = mybir.dt.int32
I8 = mybir.dt.int8
ALU = mybir.AluOpType
ACT = mybir.ActivationFunctionType
AX = mybir.AxisListType.X

B, T, D = 4, 2048, 2048
TOK = 1024
NTOKC = TOK // 128
NDIN = D // 128
HG, DK, DV, CH = 4, 512, 256, 64
NCH = TOK // CH
NDKC = DK // 128
VD = HG * DV
NVDC = VD // 128
SCALE_GLA = float(DK) ** -0.5
S16 = 1.0 / 16.0
EPS = 1e-5
FI = 8192
HB, KVH, DH = 16, 4, 128
SCALE_B = float(1.0 / np.sqrt(np.float64(DH)))
NEG = -1e30
MM = 4096.0

DT_BYTES = {F32: 4, BF16: 2, I32: 4, I8: 1}


class SBuf:
    def __init__(self, nc):
        self.nc = nc
        self.base = (nc.sbuf_base + 31) & ~31
        self.limit = nc.sbuf_top
        self.off = self.base
        self.marks = []
        self.n = 0

    def alloc(self, name, shape, dtype):
        nbytes = int(np.prod(shape[1:])) * DT_BYTES[dtype]
        nbytes = (nbytes + 31) & ~31
        assert self.off + nbytes <= self.limit, (
            f"SBUF overflow at {name}: {self.off}+{nbytes} > {self.limit}")
        t = self.nc.alloc_sbuf_tensor_at(f"{name}_{self.n}", list(shape), dtype, offset=self.off)
        self.n += 1
        self.off += nbytes
        return t

    def push(self):
        self.marks.append(self.off)

    def pop(self):
        self.off = self.marks.pop()


class K:
    def __init__(self, nc):
        self.nc = nc
        self.sb = SBuf(nc)
        self.cnt = {}
        self.sem = {}
        for name in ("h", "g", "cc", "pe", "ve", "se", "ge"):
            self.sem[name] = nc.alloc_semaphore(f"sem_{name}")
            self.cnt[name] = 0
        self.ename = {}

    def _e(self, eng):
        n = {"PE": "pe", "DVE": "ve", "Activation": "se", "Pool": "ge"}
        return n[str(eng.engine.value) if hasattr(eng, "engine") else "?"]

    def edge(self, prod, *cons):
        import concourse.mybir as mb
        emap = {mb.EngineType.PE: "pe", mb.EngineType.DVE: "ve",
                mb.EngineType.Activation: "se", mb.EngineType.Pool: "ge"}
        name = emap[prod.engine]
        self.cnt[name] += 1
        prod.drain().then_inc(self.sem[name], 1)
        for c in cons:
            c.wait_ge(self.sem[name], self.cnt[name])

    def dh(self, out, in_):
        self.nc.sync.dma_start(out=out, in_=in_).then_inc(self.sem["h"], 16)
        self.cnt["h"] += 16

    def dg(self, out, in_):
        self.nc.gpsimd.dma_start(out=out, in_=in_).then_inc(self.sem["g"], 16)
        self.cnt["g"] += 16

    def wait_dma(self, *engines):
        for e in engines:
            e.wait_ge(self.sem["h"], self.cnt["h"])
            e.wait_ge(self.sem["g"], self.cnt["g"])

    def bar(self):
        nc = self.nc
        self.wait_dma(nc.sync, nc.gpsimd)
        nc.all_engine_barrier()

    def pipe2(self, units, mm_fn, evac_fn):
        nc = self.nc
        pe0, ve0 = self.cnt["pe"], self.cnt["ve"]
        for i, unit in enumerate(units):
            if i >= 2:
                nc.tensor.wait_ge(self.sem["ve"], ve0 + i - 1)
            mm_fn(unit, i % 2)
            nc.tensor.drain().then_inc(self.sem["pe"], 1)
            self.cnt["pe"] += 1
            nc.vector.wait_ge(self.sem["pe"], pe0 + i + 1)
            evac_fn(unit, i % 2)
            nc.vector.drain().then_inc(self.sem["ve"], 1)
            self.cnt["ve"] += 1
        nc.tensor.wait_ge(self.sem["ve"], self.cnt["ve"])


def build(taps=(), upto=99):
    nc = bass.Bass("TRN2", target_bir_lowering=False)
    k = K(nc)
    sb = k.sb
    V, S, P, G = nc.vector, nc.scalar, nc.tensor, nc.gpsimd

    di = lambda n, shp, dt=F32: nc.dram_tensor(n, shp, dt, kind="ExternalInput")
    x_own = di("x_own", [TOK, D])
    out_d = nc.dram_tensor("out", [TOK, D], F32, kind="ExternalOutput")
    Wq_a = di("Wq_a", [HG, 2, D, DK // 2]); Wk_a = di("Wk_a", [HG, 2, D, DK // 2])
    Wv_a = di("Wv_a", [HG, D, DV]); Wg_a = di("Wg_a", [4, D, VD // 4])
    Wgk1 = di("Wgk1", [D, 16]); Wgk2 = di("Wgk2", [16, D]); bgk2 = di("bgk2", [128, HG * NDKC])
    Wnorm_rep = di("Wnorm_rep", [128, DV]); Wo_a = di("Wo_a", [4, VD, D // 4])
    Wq_b = di("Wq_b", [D, D]); Wk_b = di("Wk_b", [D, KVH * DH])
    Wv_b = di("Wv_b", [D, KVH * DH]); Wo_b = di("Wo_b", [D, D])
    wg_sl = di("wg_sl", [FI // 8, D]); wu_sl = di("wu_sl", [FI // 8, D])
    wd_sl = di("wd_sl", [D // 8, FI])
    cos_t = di("cos_t", [TOK, DH]); sin_t = di("sin_t", [TOK, DH])
    gla_gate = di("gla_gate", [128, 1])
    attnb_bias = di("attnb_bias", [128, 1])
    tri_gla = di("tri_gla", [128, CH])
    diag_mask = di("diag_mask", [128, 128])
    tri_r = di("tri_r", [128, 128])
    attnb_cols = di("attnb_cols", [128, 4])
    ident_c = di("ident_c", [128, 128])

    cc_ws_in = nc.dram_tensor("cc_ws_in", [128, 4], F32)
    cc_ws_out = nc.dram_tensor("cc_ws_out", [128, 4], F32)
    WQ_G = D * (FI // 8)
    WQ_SL = 2 * WQ_G + FI * (D // 8)
    cc_wq_in = nc.dram_tensor("cc_wq_in", [WQ_SL], BF16)
    cc_wq_out = nc.dram_tensor("cc_wq_out", [8, WQ_SL], BF16)
    cc_S_in = nc.dram_tensor("cc_S_in", [HG * NDKC * 128, DV], F32)
    cc_S_out = nc.dram_tensor("cc_S_out", [2, HG * NDKC * 128, DV], F32)
    u_spill = nc.dram_tensor("u_spill", [TOK, FI], F32)
    cc_kv_in = nc.dram_tensor("cc_kv_in", [TOK, 2 * KVH * DH], BF16)
    cc_kv_out = nc.dram_tensor("cc_kv_out", [2, TOK, 2 * KVH * DH], BF16)

    dbg = {}

    def tap(name, a):
        if name not in taps:
            return
        d = nc.dram_tensor(f"dbg_{name}", list(a.shape), a.dtype, kind="ExternalOutput")
        k.bar()
        k.dg(d.ap(), a)
        k.bar()
        dbg[name] = f"dbg_{name}"

    resid = sb.alloc("resid", [128, NTOKC * D], F32)
    ident_f = sb.alloc("ident_f", [128, 128], F32)
    sc = sb.alloc("smallcols", [128, 64], F32)

    psS_t = nc.alloc_psum_tensor("psS", [128, 2048], F32)
    pb_hi = [nc.alloc_psum_tensor(f"pbk{i}", [128, 512], F32) for i in range(4, 8)]
    PB = [psS_t.ap()[:, i * 512:(i + 1) * 512] for i in range(4)] + [t.ap() for t in pb_hi]
    psS = psS_t.ap()

    k.dh(resid.ap().rearrange("p (c d) -> p c d", d=D),
         x_own.ap().rearrange("(c p) d -> p c d", p=128))
    k.dh(ident_f[:, :], ident_c[:, :])
    V.memset(sc[:, 60:61], EPS)
    V.memset(sc[:, 61:62], 1.0)
    V.drain()
    k.bar()

    def ln_to_fm(xln_fm, xf_tile, scratch):
        for tc in range(NTOKC):
            xt = resid[:, tc * D:(tc + 1) * D]
            V.reduce_sum(sc[:, 0:1], xt, axis=AX)
            V.scalar_tensor_tensor(scratch[:, 0:D], xt, 1.0, xt, ALU.bypass, ALU.mult,
                                   accum_out=sc[:, 1:2])
            V.drain()
            V.tensor_scalar(sc[:, 2:3], sc[:, 0:1], 1.0 / D, None, ALU.mult)
            V.drain()
            V.tensor_tensor(sc[:, 3:4], sc[:, 2:3], sc[:, 2:3], ALU.mult)
            V.tensor_scalar(sc[:, 4:5], sc[:, 1:2], 1.0 / D, None, ALU.mult)
            V.drain()
            V.tensor_tensor(sc[:, 5:6], sc[:, 4:5], sc[:, 3:4], ALU.subtract)
            k.edge(V, S)
            S.activation(sc[:, 6:7], sc[:, 5:6], ACT.Ln, bias=sc[:, 60:61], scale=1.0)
            S.drain()
            S.activation(sc[:, 7:8], sc[:, 6:7], ACT.Exp, bias=0.0, scale=-0.5)
            k.edge(S, V)
            V.tensor_scalar(xf_tile[:, 0:D], xt, sc[:, 2:3], sc[:, 7:8],
                            ALU.subtract, ALU.mult)
            k.edge(V, P)
            for r0 in range(0, NDIN, 8):
                for j in range(8):
                    dc = r0 + j
                    P.transpose(PB[j % 4 + 4][:, (j // 4) * 128:(j // 4) * 128 + 128],
                                xf_tile[:, dc * 128:(dc + 1) * 128], ident_f[:, :])
                k.edge(P, V)
                for j in range(8):
                    dc = r0 + j
                    V.tensor_copy(xln_fm[:, dc * TOK + tc * 128: dc * TOK + (tc + 1) * 128],
                                  PB[j % 4 + 4][:, (j // 4) * 128:(j // 4) * 128 + 128])
                k.edge(V, P)
        k.bar()

    mats = [(wg_sl, FI // 8, D, 0), (wu_sl, FI // 8, D, WQ_G), (wd_sl, D // 8, FI, 2 * WQ_G)]
    if upto >= 1:
        sb.push()
        ones_f = sb.alloc("ones_f", [128, 128], F32)
        V.memset(ones_f[:, :], 1.0)
        V.drain()
        wbuf = sb.alloc("wsw", [128, 8192], F32)
        wqtstage = sb.alloc("wqtstage", [128, 16384], F32)
        c1buf = sb.alloc("c1buf", [128, 16384], BF16)
        for mi, (wd, rows, cols, _off) in enumerate(mats):
            nrc = rows // 128
            for hf in range(2):
                k.dh(wbuf.ap().rearrange("p (c d) -> p c d", d=cols)[:, 0:nrc // 2, :],
                     wd.ap().rearrange("(c p) d -> p c d", p=128)[:, hf * (nrc // 2):(hf + 1) * (nrc // 2), :])
                k.wait_dma(V)
                V.reduce_sum(sc[:, 12 + hf:13 + hf], wbuf[:, 0:(nrc // 2) * cols], axis=AX,
                             apply_absolute_value=True)
                k.edge(V, nc.sync)
            V.tensor_tensor(sc[:, 8 + mi:9 + mi], sc[:, 12:13], sc[:, 13:14], ALU.add)
            V.drain()
        V.memset(sc[:, 11:12], 0.0)
        k.edge(V, nc.sync)
        k.dh(cc_ws_in.ap(), sc[:, 8:12])
        k.wait_dma(G)
        G.collective_compute("AllReduce", ALU.add, replica_groups=[list(range(8))],
                             ins=[cc_ws_in.ap().opt()], outs=[cc_ws_out.ap().opt()]
                             ).then_inc(k.sem["cc"], 1)
        k.cnt["cc"] += 1
        G.wait_ge(k.sem["cc"], k.cnt["cc"])
        k.dg(sc[:, 16:20], cc_ws_out.ap())
        k.bar()
        P.matmul(PB[0][:, 0:4], ones_f[:, :], sc[:, 16:20], start=True, stop=True)
        k.edge(P, V)
        for mi, (wd, rows, cols, _off) in enumerate(mats):
            n_el = rows * 8 * cols
            V.tensor_scalar(sc[:, 24 + mi:25 + mi], PB[0][:, mi:mi + 1], 1.0 / n_el, None,
                            ALU.mult)
        V.drain()
        k.bar()
        for mi, (wd, rows, cols, _off) in enumerate(mats):
            nrc = rows // 128
            for hf in range(2):
                k.dh(wbuf.ap().rearrange("p (c d) -> p c d", d=cols)[:, 0:nrc // 2, :],
                     wd.ap().rearrange("(c p) d -> p c d", p=128)[:, hf * (nrc // 2):(hf + 1) * (nrc // 2), :])
                k.wait_dma(S, V)
                half_n = (nrc // 2) * cols
                S.activation(wqtstage[:, 0:half_n], wbuf[:, 0:half_n], ACT.Abs,
                             bias=0.0, scale=1.0)
                k.edge(S, V)
                V.tensor_scalar(wqtstage[:, 0:half_n], wqtstage[:, 0:half_n],
                                sc[:, 24 + mi:25 + mi], None, ALU.subtract)
                V.drain()
                V.reduce_sum(sc[:, 12 + hf:13 + hf], wqtstage[:, 0:half_n], axis=AX)
                k.edge(V, nc.sync)
            V.tensor_tensor(sc[:, 8 + mi:9 + mi], sc[:, 12:13], sc[:, 13:14], ALU.add)
            V.drain()
        V.memset(sc[:, 11:12], 0.0)
        k.edge(V, nc.sync)
        k.dh(cc_ws_in.ap(), sc[:, 8:12])
        k.wait_dma(G)
        G.collective_compute("AllReduce", ALU.add, replica_groups=[list(range(8))],
                             ins=[cc_ws_in.ap().opt()], outs=[cc_ws_out.ap().opt()]
                             ).then_inc(k.sem["cc"], 1)
        k.cnt["cc"] += 1
        G.wait_ge(k.sem["cc"], k.cnt["cc"])
        k.dg(sc[:, 16:20], cc_ws_out.ap())
        k.bar()
        P.matmul(PB[0][:, 0:4], ones_f[:, :], sc[:, 16:20], start=True, stop=True)
        k.edge(P, V)
        for mi, (wd, rows, cols, _off) in enumerate(mats):
            n_el = rows * 8 * cols
            V.tensor_scalar(sc[:, 28 + mi:29 + mi], PB[0][:, mi:mi + 1], 1.0 / n_el, None,
                            ALU.mult)
        V.drain()
        V.tensor_tensor(sc[:, 28:31], sc[:, 28:31], sc[:, 24:27], ALU.add)
        V.drain()
        V.tensor_scalar(sc[:, 28:31], sc[:, 28:31], 1.0, EPS, ALU.mult, ALU.add)
        V.drain()
        V.tensor_scalar(sc[:, 20:23], sc[:, 28:31], 0.5, None, ALU.mult)
        V.tensor_scalar(sc[:, 24:27], sc[:, 28:31], -0.5, None, ALU.mult)
        V.drain()
        k.bar()
        for mi, (wd, rows, cols, off) in enumerate(mats):
            nrc = rows // 128
            ncolc = cols // 128
            half_n = (nrc // 2) * cols
            for hf in range(2):
                k.dh(wbuf.ap().rearrange("p (c d) -> p c d", d=cols)[:, 0:nrc // 2, :],
                     wd.ap().rearrange("(c p) d -> p c d", p=128)[:, hf * (nrc // 2):(hf + 1) * (nrc // 2), :])
                k.wait_dma(V)
                V.tensor_scalar(wqtstage[:, hf * half_n:(hf + 1) * half_n], wbuf[:, 0:half_n],
                                sc[:, 20 + mi:21 + mi], None, ALU.is_ge)
                V.drain()
                V.scalar_tensor_tensor(wqtstage[:, hf * half_n:(hf + 1) * half_n], wbuf[:, 0:half_n],
                                       sc[:, 24 + mi:25 + mi], wqtstage[:, hf * half_n:(hf + 1) * half_n],
                                       ALU.is_le, ALU.subtract)
                k.edge(V, nc.sync)
            k.edge(V, P)
            tiles = [(cc_i, rc) for cc_i in range(ncolc) for rc in range(nrc)]
            for i0 in range(0, len(tiles), 4):
                grp = tiles[i0:i0 + 4]
                for j, (cc_i, rc) in enumerate(grp):
                    P.transpose(PB[j][:, 0:128],
                                wqtstage[:, rc * cols + cc_i * 128: rc * cols + cc_i * 128 + 128],
                                ident_f[:, :])
                k.edge(P, V)
                for j, (cc_i, rc) in enumerate(grp):
                    V.tensor_copy(c1buf[:, cc_i * rows + rc * 128: cc_i * rows + (rc + 1) * 128],
                                  PB[j][:, 0:128])
                k.edge(V, P)
            k.edge(V, nc.sync)
            k.dh(AP(cc_wq_in, off, [[rows, 128], [128 * rows, ncolc], [1, rows]]),
                 c1buf.ap().rearrange("p (a r) -> p a r", r=rows)[:, 0:ncolc, :])
            k.bar()
        k.wait_dma(G)
        G.collective_compute("AllGather", ALU.bypass, replica_groups=[list(range(8))],
                             ins=[cc_wq_in.ap().opt()], outs=[cc_wq_out.ap().opt()]
                             ).then_inc(k.sem["cc"], 1)
        k.cnt["cc"] += 1
        sb.pop()
        tap("wstat", sc[:, 0:32])

    if upto >= 2:
        sb.push()
        tri_s = sb.alloc("tri_s", [128, CH], F32)
        wnorm_s = sb.alloc("wnorm_s", [128, DV], F32)
        gate_s = sb.alloc("gate_s", [128, 1], F32)
        k.dh(tri_s[:, :], tri_gla[:, :])
        k.dh(wnorm_s[:, :], Wnorm_rep[:, :])
        k.dh(gate_s[:, :], gla_gate[:, :])
        xln_fm = sb.alloc("xln_fm", [128, NDIN * TOK], BF16)
        scratch = sb.alloc("lnscr", [128, 2048], F32)
        qdec = sb.alloc("qdec", [128, HG * NDKC * TOK], BF16)
        o_all = sb.alloc("o_all", [128, NTOKC * VD], BF16)
        g_all = sb.alloc("g_all", [128, NTOKC * VD], BF16)
        kbuf = sb.alloc("kbuf", [128, NDKC * TOK], BF16)
        vbuf = sb.alloc("vbuf", [128, NTOKC * DV], BF16)
        v_odd = sb.alloc("v_odd", [64, NTOKC * DV], BF16)
        Sf = sb.alloc("Sf", [128, NDKC * DV], F32)
        Sb = sb.alloc("Sb", [128, NDKC * DV], BF16)
        t_fm = sb.alloc("t_fm", [16, TOK], BF16)
        wgk1_s = sb.alloc("wgk1_s", [128, NDIN * 16], BF16)
        wgk2_s = sb.alloc("wgk2_s", [16, D], BF16)
        wbuf0 = sb.alloc("whead0", [128, NDIN * 256], BF16)

        kkT = sb.alloc("kkT", [128, DK], BF16)
        abuf = sb.alloc("abuf", [128, CH], BF16)
        nbg = sb.alloc("nbg", [128, HG * NDKC], F32)
        bgf = sb.alloc("bgf", [128, HG * NDKC], F32)

        ln_to_fm(xln_fm, scratch, scratch)
        tap("xln_fm", xln_fm[:, :])

        k.dh(bgf[:, :], bgk2[:, :])
        k.wait_dma(V)
        V.tensor_scalar(nbg[:, :], bgf[:, :], -1.0, None, ALU.mult)
        V.drain()
        k.dg(wgk1_s.ap().rearrange("p (c o) -> p c o", o=16),
             Wgk1.ap().rearrange("(c p) o -> p c o", p=128))
        k.dg(wgk2_s[:, :], Wgk2[:, :])
        k.bar()
        for half in range(2):
            for dc in range(NDIN):
                P.matmul(PB[4][0:16, 0:512], wgk1_s[:, dc * 16:(dc + 1) * 16],
                         xln_fm[:, dc * TOK + half * 512: dc * TOK + (half + 1) * 512],
                         start=(dc == 0), stop=(dc == NDIN - 1))
            k.edge(P, V)
            V.tensor_copy(t_fm[0:16, half * 512:(half + 1) * 512], PB[4][0:16, 0:512])
            k.edge(V, P)
        k.bar()
        tap("t_fm", t_fm[:, :])

        for h in range(HG):
            qh = qdec[:, h * NDKC * TOK:(h + 1) * NDKC * TOK]
            for dst, wsrc, scale_, buf in ((qh, Wq_a, SCALE_GLA, "q"), (None, Wk_a, None, "k")):
                for chf in range(2):
                    k.dg(wbuf0.ap().rearrange("p (c o) -> p c o", o=256),
                         wsrc.ap()[h, chf, :, :].rearrange("(c p) o -> p c o", p=128))
                    k.bar()
                    for dkc in (chf * 2, chf * 2 + 1):
                        dkl = dkc - chf * 2
                        for tb in range(2):
                            pbi = PB[tb]
                            for dc in range(NDIN):
                                P.matmul(pbi[:, :],
                                         wbuf0[:, dc * 256 + dkl * 128: dc * 256 + (dkl + 1) * 128],
                                         xln_fm[:, dc * TOK + tb * 512: dc * TOK + (tb + 1) * 512],
                                         start=(dc == 0), stop=(dc == NDIN - 1))
                        k.edge(P, V)
                        for tb in range(2):
                            if buf == "q":
                                V.tensor_scalar(dst[:, dkc * TOK + tb * 512: dkc * TOK + (tb + 1) * 512],
                                                PB[tb][:, :], scale_, None, ALU.mult)
                            else:
                                V.tensor_copy(kbuf[:, dkc * TOK + tb * 512: dkc * TOK + (tb + 1) * 512],
                                              PB[tb][:, :])
                        k.edge(V, P)
                    k.bar()
            k.dg(wbuf0.ap().rearrange("p (c o) -> p c o", o=DV),
                 Wv_a.ap()[h, :, :].rearrange("(c p) o -> p c o", p=128))
            k.bar()
            def _vmm(tc, par):
                for dc in range(NDIN):
                    P.matmul(PB[par][:, 0:DV],
                             xln_fm[:, dc * TOK + tc * 128: dc * TOK + (tc + 1) * 128],
                             wbuf0[:, dc * DV:(dc + 1) * DV],
                             start=(dc == 0), stop=(dc == NDIN - 1))
            def _vev(tc, par):
                V.tensor_copy(vbuf[:, tc * DV:(tc + 1) * DV], PB[par][:, 0:DV])
            k.pipe2(list(range(NTOKC)), _vmm, _vev)
            k.bar()
            k.dh(v_odd[0:64, :], vbuf[64:128, :])
            k.bar()
            if h == 0:
                tap("q0", qdec[:, 0:NDKC * TOK])
                tap("k0", kbuf[:, :])
                tap("v0", vbuf[:, :])
            lsq = scratch[:, 1536:1536 + NDKC * CH]
            bfm = scratch[:, 1792:1792 + NDKC * CH]
            scr_bf = scratch.ap().bitcast(BF16)
            qet = scr_bf[:, 512:512 + NDKC * CH]
            kdt = scr_bf[:, 768:768 + NDKC * CH]
            ebt = scratch[:, 768:768 + NDKC * CH]
            enbt = scratch[:, 1024:1024 + NDKC * CH]
            kkf = scratch[:, 1280:1280 + NDKC * CH]
            V.memset(Sf[:, :], 0.0)
            V.memset(Sb[:, :], 0.0)
            V.memset(sc[:, 32:36], 0.0)
            V.drain()
            k.bar()
            for c in range(NCH):
                od = (c % 2) * 64
                tc = c // 2
                for dkc in range(NDKC):
                    P.matmul(PB[2][:, dkc * CH:(dkc + 1) * CH],
                             wgk2_s[0:16, (h * DK + dkc * 128):(h * DK + dkc * 128) + 128],
                             t_fm[0:16, c * CH:(c + 1) * CH], start=True, stop=True)
                k.edge(P, S)
                zabs = scratch[:, 0:NDKC * CH]
                rmin = scratch[:, 512:512 + NDKC * CH]
                for dkc in range(NDKC):
                    S.activation(zabs[:, dkc * CH:(dkc + 1) * CH],
                                 PB[2][:, dkc * CH:(dkc + 1) * CH],
                                 ACT.Abs, bias=bgf[:, h * NDKC + dkc: h * NDKC + dkc + 1], scale=1.0)
                    S.activation(rmin[:, dkc * CH:(dkc + 1) * CH],
                                 PB[2][:, dkc * CH:(dkc + 1) * CH],
                                 ACT.Relu, bias=nbg[:, h * NDKC + dkc: h * NDKC + dkc + 1], scale=-1.0)
                S.drain()
                S.activation(zabs, zabs, ACT.Exp, bias=0.0, scale=-1.0)
                S.drain()
                S.activation(zabs, zabs, ACT.Ln, bias=sc[:, 61:62], scale=1.0)
                k.edge(S, V, P)
                V.scalar_tensor_tensor(lsq, rmin, -1.0, zabs, ALU.mult, ALU.subtract)
                V.drain()
                for dkc in range(NDKC):
                    V.tensor_tensor_scan(bfm[:, dkc * CH:(dkc + 1) * CH],
                                         lsq[:, dkc * CH:(dkc + 1) * CH],
                                         lsq[:, dkc * CH:(dkc + 1) * CH], 0.0, ALU.add, ALU.bypass)
                k.edge(V, S)
                S.activation(ebt, bfm, ACT.Exp, bias=0.0, scale=S16)
                S.activation(enbt, bfm, ACT.Exp, bias=0.0, scale=-S16)
                blast_ap = bfm.rearrange("p (a s) -> p a s", s=CH)[:, :, CH - 1:CH].opt()
                S.activation(sc[:, 36:40], blast_ap, ACT.Exp, bias=0.0, scale=S16)
                S.activation(sc[:, 40:44], sc[:, 32:36], ACT.Exp, bias=0.0, scale=S16)
                k.edge(S, V)
                V.tensor_tensor(sc[:, 32:36], sc[:, 32:36], blast_ap, ALU.add)
                V.drain()
                q_ap = AP(qdec, h * NDKC * TOK + c * CH, [[HG * NDKC * TOK, 128], [TOK, NDKC], [1, CH]])
                k_ap = AP(kbuf, c * CH, [[NDKC * TOK, 128], [TOK, NDKC], [1, CH]])
                qe3 = qet.rearrange("p (a s) -> p a s", s=CH)
                V.tensor_tensor(qe3, q_ap, ebt.rearrange("p (a s) -> p a s", s=CH), ALU.mult)
                V.tensor_tensor(kdt.rearrange("p (a s) -> p a s", s=CH), k_ap,
                                enbt.rearrange("p (a s) -> p a s", s=CH), ALU.mult)
                V.drain()
                for dkc in range(NDKC):
                    V.scalar_tensor_tensor(kkf[:, dkc * CH:(dkc + 1) * CH],
                                           kdt[:, dkc * CH:(dkc + 1) * CH],
                                           sc[:, 36 + dkc:37 + dkc],
                                           kdt[:, dkc * CH:(dkc + 1) * CH],
                                           ALU.mult, ALU.bypass)
                V.drain()
                for dkc in range(NDKC):
                    V.tensor_scalar(AP(qdec, h * NDKC * TOK + dkc * TOK + c * CH,
                                       [[HG * NDKC * TOK, 128], [1, CH]]),
                                    qet[:, dkc * CH:(dkc + 1) * CH],
                                    sc[:, 40 + dkc:41 + dkc], None, ALU.mult)
                k.edge(V, P)
                for dkc in range(NDKC):
                    P.matmul(PB[3][od:od + 64, 0:64], kdt[:, dkc * CH:(dkc + 1) * CH],
                             qet[:, dkc * CH:(dkc + 1) * CH],
                             start=(dkc == 0), stop=(dkc == NDKC - 1), tile_position=(0, od))
                for dkc in range(NDKC):
                    P.transpose(PB[4][0:64, dkc * 128:(dkc + 1) * 128],
                                kkf[:, dkc * CH:(dkc + 1) * CH], ident_f[:, :])
                k.edge(P, V)
                V.tensor_tensor(abuf[od:od + 64, :], PB[3][od:od + 64, 0:64],
                                tri_s[od:od + 64, :], ALU.mult)
                V.tensor_copy(kkT[0:64, :], PB[4][0:64, :])
                k.edge(V, P)
                P.matmul(PB[5][od:od + 64, 0:DV], abuf[od:od + 64, :],
                         vbuf[od:od + 64, tc * DV:(tc + 1) * DV], start=True, stop=False,
                         tile_position=(od, od))
                for dkc in range(NDKC):
                    P.matmul(PB[5][od:od + 64, 0:DV], qet[:, dkc * CH:(dkc + 1) * CH],
                             Sb[:, dkc * DV:(dkc + 1) * DV], start=False,
                             stop=(dkc == NDKC - 1), tile_position=(0, od))
                v_u = vbuf[0:64, tc * DV:(tc + 1) * DV] if od == 0 else \
                    v_odd[0:64, tc * DV:(tc + 1) * DV]
                for dkc in range(NDKC):
                    P.matmul(PB[6 + dkc // 2][:, (dkc % 2) * DV:(dkc % 2 + 1) * DV],
                             kkT[0:64, dkc * 128:(dkc + 1) * 128],
                             v_u, start=True, stop=True, tile_position=(0, 0))
                k.edge(P, V)
                V.tensor_copy(o_all[od:od + 64, tc * VD + h * DV: tc * VD + (h + 1) * DV],
                              PB[5][od:od + 64, 0:DV])
                for dkc in range(NDKC):
                    V.scalar_tensor_tensor(Sf[:, dkc * DV:(dkc + 1) * DV],
                                           Sf[:, dkc * DV:(dkc + 1) * DV],
                                           sc[:, 36 + dkc:37 + dkc],
                                           PB[6 + dkc // 2][:, (dkc % 2) * DV:(dkc % 2 + 1) * DV],
                                           ALU.mult, ALU.add)
                V.drain()
                V.tensor_copy(Sb[:, :], Sf[:, :])
                k.edge(V, P)
            k.bar()
            k.dh(cc_S_in.ap().rearrange("(a p) v -> p a v", p=128)[:, h * NDKC:(h + 1) * NDKC, :],
                 Sf.ap().rearrange("p (a v) -> p a v", v=DV))
            k.bar()
            if h == 0:
                tap("S0f", Sf[:, :])
                tap("o_h0", o_all[:, 0:VD])
        k.wait_dma(G)
        G.collective_compute("AllGather", ALU.bypass,
                             replica_groups=[[0, 1], [2, 3], [4, 5], [6, 7]],
                             ins=[cc_S_in.ap().opt()], outs=[cc_S_out.ap().opt()]
                             ).then_inc(k.sem["cc"], 1)
        k.cnt["cc"] += 1
        for qtr in range(4):
            k.dg(wbuf0.ap().rearrange("p (c o) -> p c o", o=256)[:, 0:NDIN, :],
                 Wg_a.ap()[qtr, :, :].rearrange("(c p) o -> p c o", p=128))
            k.bar()
            def _gmm(tc, par):
                for dc in range(NDIN):
                    P.matmul(PB[par][:, 0:256],
                             xln_fm[:, dc * TOK + tc * 128: dc * TOK + (tc + 1) * 128],
                             wbuf0[:, dc * 256:(dc + 1) * 256],
                             start=(dc == 0), stop=(dc == NDIN - 1))
            def _gev(tc, par):
                V.tensor_copy(g_all[:, tc * VD + qtr * 256: tc * VD + (qtr + 1) * 256],
                              PB[par][:, 0:256])
            k.pipe2(list(range(NTOKC)), _gmm, _gev)
            k.bar()
        tap("g_all", g_all[:, :])
        G.wait_ge(k.sem["cc"], k.cnt["cc"])
        s0buf = kbuf
        for h in range(HG):
            k.dg(s0buf.ap().rearrange("p (a v) -> p a v", v=DV)[:, 0:NDKC, :],
                 cc_S_out.ap()[0, :, :].rearrange("(a p) v -> p a v", p=128)
                 [:, h * NDKC:(h + 1) * NDKC, :])
            k.bar()
            qh = qdec[:, h * NDKC * TOK:(h + 1) * NDKC * TOK]
            for tc in range(NTOKC):
                pbi = PB[2 + tc % 2]
                for dkc in range(NDKC):
                    P.matmul(pbi[:, 0:DV], qh[:, dkc * TOK + tc * 128: dkc * TOK + (tc + 1) * 128],
                             s0buf[:, dkc * DV:(dkc + 1) * DV],
                             start=(dkc == 0), stop=(dkc == NDKC - 1))
                k.edge(P, V)
                dst = o_all[:, tc * VD + h * DV: tc * VD + (h + 1) * DV]
                V.scalar_tensor_tensor(dst, pbi[:, 0:DV], gate_s[:, 0:1], dst, ALU.mult, ALU.add)
                k.edge(V, P)
            k.bar()
        k.bar()
        tap("o_corr", o_all[:, :])
        z_fm = qdec
        for tc in range(NTOKC):
            for h in range(HG):
                o_sl = o_all[:, tc * VD + h * DV: tc * VD + (h + 1) * DV]
                g_sl = g_all[:, tc * VD + h * DV: tc * VD + (h + 1) * DV]
                V.scalar_tensor_tensor(scratch[:, 0:DV], o_sl, 1.0, o_sl, ALU.bypass, ALU.mult,
                                       accum_out=sc[:, 44:45])
                k.edge(V, S)
                S.activation(sc[:, 45:46], sc[:, 44:45], ACT.Ln, bias=sc[:, 60:61], scale=1.0 / DV)
                S.drain()
                S.activation(sc[:, 46:47], sc[:, 45:46], ACT.Exp, bias=0.0, scale=-0.5)
                S.activation(scratch[:, 1024:1024 + DV], g_sl, ACT.Silu, bias=0.0, scale=1.0)
                k.edge(S, V)
                V.tensor_scalar(scratch[:, 0:DV], o_sl, sc[:, 46:47], None, ALU.mult)
                V.drain()
                V.tensor_tensor(scratch[:, 0:DV], scratch[:, 0:DV], wnorm_s[:, :], ALU.mult)
                V.drain()
                V.tensor_tensor(scratch[:, 256:256 + DV], scratch[:, 0:DV],
                                scratch[:, 1024:1024 + DV], ALU.mult)
                k.edge(V, P)
                for j in range(2):
                    vc = 2 * h + j
                    pbi = PB[j]
                    P.transpose(pbi[:, 0:128], scratch[:, 256 + j * 128:256 + (j + 1) * 128],
                                ident_f[:, :])
                k.edge(P, V)
                for j in range(2):
                    vc = 2 * h + j
                    V.tensor_copy(z_fm[:, vc * TOK + tc * 128: vc * TOK + (tc + 1) * 128],
                                  PB[j][:, 0:128])
                k.edge(V, P)
        k.bar()
        tap("z_fm", z_fm[:, 0:NVDC * TOK])
        for qtr in range(4):
            k.dg(wbuf0.ap().rearrange("p (c o) -> p c o", o=512)[:, 0:NVDC, :],
                 Wo_a.ap()[qtr, :, :].rearrange("(c p) o -> p c o", p=128))
            k.bar()
            for tc in range(NTOKC):
                pbi = PB[tc % 2]
                for vc in range(NVDC):
                    P.matmul(pbi[:, :], z_fm[:, vc * TOK + tc * 128: vc * TOK + (tc + 1) * 128],
                             wbuf0[:, vc * 512:(vc + 1) * 512],
                             start=(vc == 0), stop=(vc == NVDC - 1))
                k.edge(P, V)
                V.tensor_tensor(resid[:, tc * D + qtr * 512: tc * D + qtr * 512 + 512],
                                resid[:, tc * D + qtr * 512: tc * D + qtr * 512 + 512],
                                pbi[:, :], ALU.add)
                k.edge(V, P)
            k.bar()
        k.bar()
        sb.pop()
        tap("x1", resid[:, :])




    def emit_ffn(tag):
        sb.push()
        xq_fm = sb.alloc(f"xqfm{tag}", [128, NDIN * TOK], BF16)
        scratch = sb.alloc(f"fscr{tag}", [128, D], F32)
        qscr = sb.alloc(f"qscr{tag}", [128, D], I32)
        slab_g = sb.alloc(f"slabg{tag}", [128, 2 * NDIN * 512], BF16)
        slab_u = sb.alloc(f"slabu{tag}", [128, 2 * NDIN * 512], BF16)
        sgt = sb.alloc(f"sgt{tag}", [128, 2 * 512], BF16)
        ut = sb.alloc(f"ut{tag}", [128, 2 * 512], F32)
        for tc in range(NTOKC):
            xt = resid[:, tc * D:(tc + 1) * D]
            V.reduce_sum(sc[:, 0:1], xt, axis=AX)
            V.scalar_tensor_tensor(scratch[:, 0:D], xt, 1.0, xt, ALU.bypass, ALU.mult,
                                   accum_out=sc[:, 1:2])
            V.drain()
            V.tensor_scalar(sc[:, 2:3], sc[:, 0:1], 1.0 / D, None, ALU.mult)
            V.drain()
            V.tensor_tensor(sc[:, 3:4], sc[:, 2:3], sc[:, 2:3], ALU.mult)
            V.tensor_scalar(sc[:, 4:5], sc[:, 1:2], 1.0 / D, None, ALU.mult)
            V.drain()
            V.tensor_tensor(sc[:, 5:6], sc[:, 4:5], sc[:, 3:4], ALU.subtract)
            k.edge(V, S)
            S.activation(sc[:, 6:7], sc[:, 5:6], ACT.Ln, bias=sc[:, 60:61], scale=1.0)
            S.drain()
            S.activation(sc[:, 7:8], sc[:, 6:7], ACT.Exp, bias=0.0, scale=-0.5)
            k.edge(S, V)
            V.tensor_scalar(scratch[:, 0:D], xt, sc[:, 2:3], sc[:, 7:8],
                            ALU.subtract, ALU.mult)
            V.drain()
            V.reduce_max(sc[:, 8:9], scratch[:, 0:D], axis=AX, apply_absolute_value=True)
            V.drain()
            V.tensor_scalar(sc[:, 16 + tc:17 + tc], sc[:, 8:9], 1.0 / 127.0, EPS,
                            ALU.mult, ALU.add)
            V.drain()
            V.reciprocal(sc[:, 9:10], sc[:, 16 + tc:17 + tc])
            V.drain()
            V.tensor_scalar(scratch[:, 0:D], scratch[:, 0:D], sc[:, 9:10], None, ALU.mult)
            V.drain()
            V.tensor_copy(qscr[:, :], scratch[:, 0:D])
            V.drain()
            V.tensor_copy(scratch[:, 0:D], qscr[:, :])
            k.edge(V, P)
            for r0 in range(0, NDIN, 8):
                for j in range(8):
                    dc = r0 + j
                    P.transpose(PB[j % 4 + 4][:, (j // 4) * 128:(j // 4) * 128 + 128],
                                scratch[:, dc * 128:(dc + 1) * 128], ident_f[:, :])
                k.edge(P, V)
                for j in range(8):
                    dc = r0 + j
                    V.tensor_copy(xq_fm[:, dc * TOK + tc * 128: dc * TOK + (tc + 1) * 128],
                                  PB[j % 4 + 4][:, (j // 4) * 128:(j // 4) * 128 + 128])
                k.edge(V, P)
        V.tensor_scalar(sc[:, 40:48], sc[:, 16:24], sc[:, 28:29], -1.0, ALU.mult, ALU.mult)
        V.tensor_scalar(sc[:, 48:56], sc[:, 16:24], sc[:, 29:30], -1.0, ALU.mult, ALU.mult)
        V.memset(sc[:, 32:40], 0.0)
        V.drain()
        k.bar()
        tap(f"xqfm_{tag}", xq_fm[:, :])
        G.wait_ge(k.sem["cc"], k.cnt["cc"])
        def slab_ap(mat_off, IB, par):
            r = IB // 2
            w0 = (IB % 2) * 512
            return AP(cc_wq_out, r * WQ_SL + mat_off + w0,
                      [[1024, 128], [128 * 1024, NDIN], [1, 512]])
        pe0 = k.cnt["pe"]; ve0 = k.cnt["ve"]; h0 = k.cnt["h"]; gcA = []
        k.dg(slab_g.ap().rearrange("p (s c o) -> p s c o", s=2, o=512)[:, 0, :, :], slab_ap(0, 0, 0))
        k.dg(slab_u.ap().rearrange("p (s c o) -> p s c o", s=2, o=512)[:, 0, :, :], slab_ap(WQ_G, 0, 0))
        gcA.append(k.cnt["g"])
        for IB in range(16):
            if IB + 1 < 16:
                G.wait_ge(k.sem["pe"], pe0 + IB * 8)
                par2 = (IB + 1) % 2
                k.dg(slab_g.ap().rearrange("p (s c o) -> p s c o", s=2, o=512)[:, par2, :, :],
                     slab_ap(0, IB + 1, par2))
                k.dg(slab_u.ap().rearrange("p (s c o) -> p s c o", s=2, o=512)[:, par2, :, :],
                     slab_ap(WQ_G, IB + 1, par2))
                gcA.append(k.cnt["g"])
            sp = IB % 2
            for tc in range(NTOKC):
                u = IB * 8 + tc
                par = u % 2
                pg, pu = PB[par * 2], PB[par * 2 + 1]
                sg_t = sgt[:, par * 512:(par + 1) * 512]
                u_t = ut[:, par * 512:(par + 1) * 512]
                if u >= 2:
                    P.wait_ge(k.sem["ve"], ve0 + (u - 1) * 2)
                if tc == 0:
                    P.wait_ge(k.sem["g"], gcA[IB])
                for dc in range(NDIN):
                    P.matmul(pg, xq_fm[:, dc * TOK + tc * 128: dc * TOK + (tc + 1) * 128],
                             slab_g[:, sp * NDIN * 512 + dc * 512: sp * NDIN * 512 + (dc + 1) * 512],
                             start=(dc == 0), stop=(dc == NDIN - 1))
                for dc in range(NDIN):
                    P.matmul(pu, xq_fm[:, dc * TOK + tc * 128: dc * TOK + (tc + 1) * 128],
                             slab_u[:, sp * NDIN * 512 + dc * 512: sp * NDIN * 512 + (dc + 1) * 512],
                             start=(dc == 0), stop=(dc == NDIN - 1))
                P.drain().then_inc(k.sem["pe"], 1); k.cnt["pe"] += 1
                V.wait_ge(k.sem["pe"], pe0 + u + 1)
                if u >= 2:
                    V.wait_ge(k.sem["h"], h0 + (u - 1) * 16)
                scr_t = scratch[:, par * 512:(par + 1) * 512]
                V.tensor_scalar(scr_t, pg, sc[:, 40 + tc:41 + tc], None, ALU.mult)
                k.edge(V, S)
                S.activation(sg_t, scr_t, ACT.Silu, bias=0.0, scale=1.0)
                k.edge(S, V)
                V.scalar_tensor_tensor(u_t, pu, sc[:, 48 + tc:49 + tc], sg_t,
                                       ALU.mult, ALU.mult)
                V.drain()
                V.reduce_max(sc[:, 10 + par:11 + par], u_t, axis=AX, apply_absolute_value=True)
                V.drain()
                V.tensor_tensor(sc[:, 32 + tc:33 + tc], sc[:, 32 + tc:33 + tc],
                                sc[:, 10 + par:11 + par], ALU.max)
                V.drain().then_inc(k.sem["ve"], 1); k.cnt["ve"] += 1
                nc.sync.wait_ge(k.sem["ve"], k.cnt["ve"])
                k.dh(AP(u_spill, (tc * 128) * FI + IB * 512, [[FI, 128], [1, 512]]), u_t)
        k.bar()
        V.tensor_scalar(sc[:, 32:40], sc[:, 32:40], 1.0 / 127.0, EPS, ALU.mult, ALU.add)
        V.drain()
        V.tensor_scalar(sc[:, 48:56], sc[:, 32:40], sc[:, 30:31], -1.0, ALU.mult, ALU.mult)
        for tc in range(NTOKC):
            V.reciprocal(sc[:, 0 + tc:1 + tc], sc[:, 32 + tc:33 + tc])
        V.drain()
        k.bar()
        sb.pop()
        sb.push()
        xqu_tm = sb.alloc(f"xqutm{tag}", [128, NTOKC * FI], I8)
        urow = sb.alloc(f"urow{tag}", [128, 2 * (FI // 16)], F32)
        scratch = sb.alloc(f"fscr2{tag}", [128, 512], F32)
        qscr = sb.alloc(f"qscr2{tag}", [128, 512], I32)
        units_c = [(tc, hf) for tc in range(NTOKC) for hf in range(16)]
        ve0 = k.cnt["ve"]; h0 = k.cnt["h"]
        def urow_ap(tc, hf):
            return AP(u_spill, (tc * 128) * FI + hf * 512, [[FI, 128], [1, 512]])
        k.dh(urow[:, 0:512], urow_ap(*units_c[0]))
        for u, (tc, hf) in enumerate(units_c):
            if u + 1 < len(units_c):
                nc.sync.wait_ge(k.sem["ve"], ve0 + u)
                k.dh(urow.ap().rearrange("p (s n) -> p s n", s=2)[:, (u + 1) % 2, :],
                     urow_ap(*units_c[u + 1]))
            V.wait_ge(k.sem["h"], h0 + (u + 1) * 16)
            ur = urow[:, (u % 2) * 512:(u % 2 + 1) * 512]
            V.tensor_scalar(scratch[:, 0:512], ur, sc[:, 0 + tc:1 + tc], None, ALU.mult)
            V.drain()
            V.tensor_copy(qscr[:, :], scratch[:, 0:512])
            V.drain()
            off = tc * FI + hf * 512
            V.tensor_copy(xqu_tm[:, off:off + 512], qscr[:, :])
            V.drain().then_inc(k.sem["ve"], 1); k.cnt["ve"] += 1
        k.bar()
        if f"usp_{tag}" in taps:
            dtmp = nc.dram_tensor(f"dbg_usp_{tag}", [TOK, FI], BF16, kind="ExternalOutput")
            k.dg(dtmp.ap(), u_spill.ap())
            k.bar()
            dbg[f"usp_{tag}"] = 1
        tap(f"xqu_{tag}", xqu_tm[:, :])
        tap(f"scs_{tag}", sc[:, 0:64])
        xqu_fm = sb.alloc(f"xqufm{tag}", [128, 64 * TOK], I8)
        units_d = [(ic, tc) for ic in range(64) for tc in range(0, NTOKC, 2)]
        pe0 = k.cnt["pe"]; ve0 = k.cnt["ve"]
        for u, (ic, tc) in enumerate(units_d):
            par = u % 2
            so = par * 256
            if u >= 2:
                V.wait_ge(k.sem["pe"], pe0 + u - 1)
            V.tensor_copy(scratch[:, so:so + 128],
                          xqu_tm[:, tc * FI + ic * 128: tc * FI + (ic + 1) * 128])
            V.tensor_copy(scratch[:, so + 128:so + 256],
                          xqu_tm[:, (tc + 1) * FI + ic * 128: (tc + 1) * FI + (ic + 1) * 128])
            V.drain().then_inc(k.sem["ve"], 1); k.cnt["ve"] += 1
            P.wait_ge(k.sem["ve"], ve0 + 2 * u + 1)
            if u >= 2:
                P.wait_ge(k.sem["ve"], ve0 + 2 * (u - 2) + 2)
            P.transpose(PB[4 + par * 2][:, 0:128], scratch[:, so:so + 128], ident_f[:, :])
            P.transpose(PB[5 + par * 2][:, 0:128], scratch[:, so + 128:so + 256], ident_f[:, :])
            P.drain().then_inc(k.sem["pe"], 1); k.cnt["pe"] += 1
            V.wait_ge(k.sem["pe"], pe0 + u + 1)
            V.tensor_copy(xqu_fm[:, ic * TOK + tc * 128: ic * TOK + (tc + 1) * 128],
                          PB[4 + par * 2][:, 0:128])
            V.tensor_copy(xqu_fm[:, ic * TOK + (tc + 1) * 128: ic * TOK + (tc + 2) * 128],
                          PB[5 + par * 2][:, 0:128])
            V.drain().then_inc(k.sem["ve"], 1); k.cnt["ve"] += 1
        P.wait_ge(k.sem["ve"], k.cnt["ve"])
        k.bar()
        wdb = sb.alloc(f"wdb{tag}", [128, 2 * 1024], BF16)
        xqb = sb.alloc(f"xqb{tag}", [128, 2 * 512], BF16)
        def wdb_ap(oh, ic):
            return AP(cc_wq_out, (4 * oh) * WQ_SL + 2 * WQ_G + ic * 128 * 256,
                      [[256, 128], [WQ_SL, 4], [1, 256]])
        for oh in range(2):
            for th in range(2):
                pe0 = k.cnt["pe"]; ve0 = k.cnt["ve"]; g0 = k.cnt["g"]
                k.dg(wdb.ap().rearrange("p (s n) -> p s n", s=2)[:, 0, :], wdb_ap(oh, 0))
                for ic in range(64):
                    par = ic % 2
                    if ic + 1 < 64:
                        G.wait_ge(k.sem["pe"], pe0 + ic)
                        k.dg(wdb.ap().rearrange("p (s n) -> p s n", s=2)[:, (ic + 1) % 2, :],
                             wdb_ap(oh, ic + 1))
                    if ic >= 2:
                        V.wait_ge(k.sem["pe"], pe0 + ic - 1)
                    V.wait_ge(k.sem["g"], g0 + (ic + 1) * 16)
                    V.tensor_copy(xqb[:, par * 512:(par + 1) * 512],
                                  xqu_fm[:, ic * TOK + th * 512: ic * TOK + (th + 1) * 512])
                    V.drain().then_inc(k.sem["ve"], 1); k.cnt["ve"] += 1
                    P.wait_ge(k.sem["ve"], ve0 + ic + 1)
                    for tcl in range(4):
                        for ob in range(2):
                            P.matmul(PB[tcl * 2 + ob], xqb[:, par * 512 + tcl * 128: par * 512 + (tcl + 1) * 128],
                                     wdb[:, par * 1024 + ob * 512: par * 1024 + (ob + 1) * 512],
                                     start=(ic == 0), stop=(ic == 63))
                    P.drain().then_inc(k.sem["pe"], 1); k.cnt["pe"] += 1
                k.edge(P, V)
                for tcl in range(4):
                    tc = th * 4 + tcl
                    for ob in range(2):
                        col = oh * 1024 + ob * 512
                        V.scalar_tensor_tensor(resid[:, tc * D + col: tc * D + col + 512],
                                               PB[tcl * 2 + ob], sc[:, 48 + tc:49 + tc],
                                               resid[:, tc * D + col: tc * D + col + 512],
                                               ALU.mult, ALU.add)
                k.edge(V, P)
                k.bar()
        k.bar()
        sb.pop()

    if upto >= 3:
        emit_ffn("f1")
        tap("x2", resid[:, :])

    if upto >= 4:
        sb.push()
        q_fm = sb.alloc("bqfm", [128, HB * TOK], BF16)
        k_fm = sb.alloc("bkfm", [128, KVH * 2 * TOK], BF16)
        v_tm = sb.alloc("bvtm", [128, 16 * KVH * DH], BF16)
        sb.push()
        xln_fm = sb.alloc("bxln", [128, NDIN * TOK], BF16)
        sb.push()
        lnscr = sb.alloc("blnscr", [128, D], F32)
        ln_to_fm(xln_fm, lnscr, lnscr)
        sb.pop()
        slab = sb.alloc("bslab", [128, NDIN * 512], BF16)
        scratch = sb.alloc("bscr", [128, 1536], F32)
        cos_s = sb.alloc("bcos", [128, NTOKC * DH], BF16)
        sin_s = sb.alloc("bsin", [128, NTOKC * DH], BF16)

        k.dg(cos_s.ap().rearrange("p (c d) -> p c d", d=DH),
             cos_t.ap().rearrange("(c p) d -> p c d", p=128))
        k.dg(sin_s.ap().rearrange("p (c d) -> p c d", d=DH),
             sin_t.ap().rearrange("(c p) d -> p c d", p=128))
        k.bar()

        def rope_tile(tc, nh):
            cbc = AP(cos_s, tc * DH, [[NTOKC * DH, 128], [0, nh], [1, DH]])
            sA1 = AP(sin_s, tc * DH, [[NTOKC * DH, 128], [0, nh], [1, 64]])
            sA2 = AP(sin_s, tc * DH + 64, [[NTOKC * DH, 128], [0, nh], [1, 64]])
            src3 = AP(scratch, 0, [[1536, 128], [DH, nh], [1, DH]])
            src_lo = AP(scratch, 0, [[1536, 128], [DH, nh], [1, 64]])
            src_hi = AP(scratch, 64, [[1536, 128], [DH, nh], [1, 64]])
            dst3 = AP(scratch, 512, [[1536, 128], [DH, nh], [1, DH]])
            dst_lo = AP(scratch, 512, [[1536, 128], [DH, nh], [1, 64]])
            dst_hi = AP(scratch, 512 + 64, [[1536, 128], [DH, nh], [1, 64]])
            tmp = AP(scratch, 1024, [[1536, 128], [64, nh], [1, 64]])
            V.tensor_tensor(dst3, src3, cbc, ALU.mult)
            V.drain()
            V.tensor_tensor(tmp, src_hi, sA1, ALU.mult)
            V.drain()
            V.tensor_tensor(dst_lo, dst_lo, tmp, ALU.add)
            V.drain()
            V.tensor_tensor(tmp, src_lo, sA2, ALU.mult)
            V.drain()
            V.tensor_tensor(dst_hi, dst_hi, tmp, ALU.add)
            V.drain()

        for wsrc, kind in ((Wk_b, "k"), (Wv_b, "v")):
            k.dg(slab.ap().rearrange("p (c o) -> p c o", o=512)[:, 0:NDIN, :],
                 wsrc.ap().rearrange("(c p) o -> p c o", p=128))
            k.bar()
            pe0 = k.cnt["pe"]; ve0 = k.cnt["ve"]; h0 = k.cnt["h"]
            for tc in range(NTOKC):
                par = tc % 2
                if tc >= 2:
                    P.wait_ge(k.sem["ve"], ve0 + tc - 1)
                for dc in range(NDIN):
                    P.matmul(PB[par], xln_fm[:, dc * TOK + tc * 128: dc * TOK + (tc + 1) * 128],
                             slab[:, dc * 512:(dc + 1) * 512],
                             start=(dc == 0), stop=(dc == NDIN - 1))
                P.drain().then_inc(k.sem["pe"], 1); k.cnt["pe"] += 1
                V.wait_ge(k.sem["pe"], pe0 + tc + 1)
                if tc >= 2:
                    V.wait_ge(k.sem["h"], h0 + (tc - 1) * 16)
                stg = scratch.ap().bitcast(BF16)[:, 2048 + par * 512:2048 + par * 512 + 512]
                if kind == "k":
                    V.tensor_copy(scratch[:, 0:512], PB[par])
                    V.drain()
                    rope_tile(tc, KVH)
                    V.tensor_copy(stg, scratch[:, 512:1024])
                    col0 = 0
                else:
                    V.tensor_copy(stg, PB[par])
                    col0 = KVH * DH
                V.drain().then_inc(k.sem["ve"], 1); k.cnt["ve"] += 1
                nc.sync.wait_ge(k.sem["ve"], k.cnt["ve"])
                k.dh(AP(cc_kv_in, (tc * 128) * (2 * KVH * DH) + col0,
                        [[2 * KVH * DH, 128], [1, 512]]), stg)
            k.bar()
        k.wait_dma(G)
        G.collective_compute("AllGather", ALU.bypass,
                             replica_groups=[[0, 1], [2, 3], [4, 5], [6, 7]],
                             ins=[cc_kv_in.ap().opt()], outs=[cc_kv_out.ap().opt()]
                             ).then_inc(k.sem["cc"], 1)
        k.cnt["cc"] += 1
        for qtr in range(4):
            k.dg(slab.ap().rearrange("p (c o) -> p c o", o=512)[:, 0:NDIN, :],
                 Wq_b.ap().rearrange("(c p) (b o) -> p c b o", p=128, o=512)[:, :, qtr, :])
            k.bar()
            pe0 = k.cnt["pe"]; ve0 = k.cnt["ve"]
            for tc in range(NTOKC):
                par = tc % 2
                if tc >= 2:
                    P.wait_ge(k.sem["ve"], ve0 + (tc - 2) * 2 + 2)
                for dc in range(NDIN):
                    P.matmul(PB[par], xln_fm[:, dc * TOK + tc * 128: dc * TOK + (tc + 1) * 128],
                             slab[:, dc * 512:(dc + 1) * 512],
                             start=(dc == 0), stop=(dc == NDIN - 1))
                P.drain().then_inc(k.sem["pe"], 1); k.cnt["pe"] += 1
                V.wait_ge(k.sem["pe"], pe0 + tc * 2 + 1)
                V.tensor_copy(scratch[:, 0:512], PB[par])
                V.drain()
                rope_tile(tc, 4)
                V.drain().then_inc(k.sem["ve"], 1); k.cnt["ve"] += 1
                P.wait_ge(k.sem["ve"], ve0 + tc * 2 + 1)
                for j in range(4):
                    P.transpose(PB[4 + (j % 2) * 2][:, (j // 2) * 128:(j // 2) * 128 + 128],
                                scratch[:, 512 + j * 128:512 + (j + 1) * 128], ident_f[:, :])
                P.drain().then_inc(k.sem["pe"], 1); k.cnt["pe"] += 1
                V.wait_ge(k.sem["pe"], pe0 + tc * 2 + 2)
                for j in range(4):
                    hh_ = qtr * 4 + j
                    V.tensor_copy(q_fm[:, hh_ * TOK + tc * 128: hh_ * TOK + (tc + 1) * 128],
                                  PB[4 + (j % 2) * 2][:, (j // 2) * 128:(j // 2) * 128 + 128])
                V.drain().then_inc(k.sem["ve"], 1); k.cnt["ve"] += 1
            P.wait_ge(k.sem["ve"], k.cnt["ve"])
            k.bar()
        sb.pop()
        sb.push()
        o_fm = sb.alloc("bofm", [128, HB * TOK], BF16)
        scratch = sb.alloc("bscr2", [128, 1536], F32)
        pbuf = sb.alloc("bp", [128, 2048], BF16)
        pT = sb.alloc("bpT", [128, 2048], BF16)
        trim_s = sb.alloc("trims", [128, 128], F32)
        trir_s = sb.alloc("trirs", [128, 128], F32)
        bcols = sb.alloc("bcolsb", [128, 8], F32)
        ident_b2b = sb.alloc("identb2b", [128, 128], BF16)
        k.dg(ident_b2b[:, :], ident_c[:, :])
        wob_slab = sb.alloc("bwo", [128, NDIN * 512], BF16)
        k.dh(trim_s[:, :], diag_mask[:, :])
        k.dh(trir_s[:, :], tri_r[:, :])
        k.dh(bcols[:, 0:4], attnb_cols[:, :])
        V.memset(bcols[:, 6:7], NEG)
        V.drain()
        k.bar()
        for kt in range(16):
            sl_ = kt // 8
            r0 = (kt % 8) * 128
            k.dg(scratch[:, 0:512],
                 AP(cc_kv_out, (sl_ * TOK + r0) * (2 * KVH * DH), [[2 * KVH * DH, 128], [1, 512]]))
            k.bar()
            for j in range(KVH):
                P.transpose(PB[4 + j % 4][:, 0:128], scratch[:, j * 128:(j + 1) * 128], ident_f[:, :])
            k.edge(P, V)
            for j in range(KVH):
                V.tensor_copy(k_fm[:, j * 2048 + kt * 128: j * 2048 + (kt + 1) * 128],
                              PB[4 + j % 4][:, 0:128])
            k.edge(V, P)
            k.bar()
        k.dg(v_tm.ap().rearrange("p (c o) -> p c o", o=512),
             AP(cc_kv_out, KVH * DH, [[2 * KVH * DH, 128], [128 * 2 * KVH * DH, 16], [1, 512]]))
        k.bar()
        pbT4 = pb_hi[0].ap().bitcast(BF16)
        pbT5 = pb_hi[1].ap().bitcast(BF16)
        units_a = [(hh_, qt) for hh_ in range(HB) for qt in range(NTOKC)]
        se_e = {}; ve_n = {}; ve_m = {}; pe_s = {}; pe_t = {}; ve_o = {}

        def emit_smms(u):
            hh_, qt = units_a[u]
            kv = hh_ // 4
            if u > 0:
                P.wait_ge(k.sem["se"], se_e[u - 1])
            for kb in range(4):
                P.matmul(PB[kb], q_fm[:, hh_ * TOK + qt * 128: hh_ * TOK + (qt + 1) * 128],
                         k_fm[:, kv * 2048 + kb * 512: kv * 2048 + (kb + 1) * 512],
                         start=True, stop=True)
            P.drain().then_inc(k.sem["pe"], 1); k.cnt["pe"] += 1
            pe_s[u] = k.cnt["pe"]

        def emit_soft(u):
            hh_, qt = units_a[u]
            V.wait_ge(k.sem["pe"], pe_s[u])
            if u >= 1:
                V.wait_ge(k.sem["pe"], pe_t.get(u - 1, 0))
            V.tensor_tensor(psS[:, qt * 128:(qt + 1) * 128],
                            psS[:, qt * 128:(qt + 1) * 128], trim_s[:, :], ALU.add)
            V.tensor_tensor(psS[:, 1024 + qt * 128: 1024 + (qt + 1) * 128],
                            psS[:, 1024 + qt * 128: 1024 + (qt + 1) * 128],
                            trir_s[:, :], ALU.add)
            V.drain()
            if qt < 7:
                nsfx = (7 - qt) * 128
                V.tensor_scalar(psS[:, (qt + 1) * 128:(qt + 1) * 128 + nsfx],
                                psS[:, (qt + 1) * 128:(qt + 1) * 128 + nsfx],
                                bcols[:, 0:1], None, ALU.add)
                V.tensor_scalar(psS[:, 1024 + (qt + 1) * 128: 1024 + (qt + 1) * 128 + nsfx],
                                psS[:, 1024 + (qt + 1) * 128: 1024 + (qt + 1) * 128 + nsfx],
                                bcols[:, 6:7], None, ALU.add)
            if qt > 0:
                V.tensor_scalar(psS[:, 1024:1024 + qt * 128],
                                psS[:, 1024:1024 + qt * 128],
                                bcols[:, 1:2], None, ALU.add)
            V.drain()
            V.reduce_max(sc[:, 0:1], psS[:, :], axis=AX)
            V.drain()
            V.tensor_scalar(sc[:, 5:6], sc[:, 0:1], -SCALE_B, None, ALU.mult)
            V.drain().then_inc(k.sem["ve"], 1); k.cnt["ve"] += 1
            ve_m[u] = k.cnt["ve"]
            S.wait_ge(k.sem["ve"], ve_m[u])
            S.activation(pbuf[:, :], psS[:, :], ACT.Exp,
                         bias=sc[:, 5:6], scale=SCALE_B, accum_out=sc[:, 12:13])
            S.drain().then_inc(k.sem["se"], 1); k.cnt["se"] += 1
            se_e[u] = k.cnt["se"]
            V.wait_ge(k.sem["se"], se_e[u])
            V.reciprocal(sc[:, 13:14], sc[:, 12:13])
            V.drain()
            V.tensor_scalar(pbuf[:, :], pbuf[:, :], sc[:, 13:14], None, ALU.mult)
            V.drain().then_inc(k.sem["ve"], 1); k.cnt["ve"] += 1
            ve_n[u] = k.cnt["ve"]

        def emit_pv(u):
            hh_, qt = units_a[u]
            kv = hh_ // 4
            P.wait_ge(k.sem["ve"], ve_n[u])
            if u >= 1:
                P.wait_ge(k.sem["ve"], ve_o.get(u - 1, 0))
            for j in range(8):
                P.transpose(pbT4[:, j * 128:(j + 1) * 128],
                            pbuf[:, j * 128:(j + 1) * 128], ident_b2b[:, :])
            for j in range(8):
                P.transpose(pbT5[:, j * 128:(j + 1) * 128],
                            pbuf[:, 1024 + j * 128: 1024 + (j + 1) * 128], ident_b2b[:, :])
            P.drain().then_inc(k.sem["pe"], 1); k.cnt["pe"] += 1
            pe_t[u] = k.cnt["pe"]
            V.wait_ge(k.sem["pe"], pe_t[u])
            V.tensor_copy(pT[:, 0:1024], pbT4[:, :])
            V.tensor_copy(pT[:, 1024:2048], pbT5[:, :])
            V.drain().then_inc(k.sem["ve"], 1); k.cnt["ve"] += 1
            P.wait_ge(k.sem["ve"], k.cnt["ve"])
            for kc in range(16):
                P.matmul(PB[6][:, 0:128], v_tm[:, kc * 512 + kv * 128: kc * 512 + (kv + 1) * 128],
                         pT[:, kc * 128:(kc + 1) * 128],
                         start=(kc == 0), stop=(kc == 15))
            P.drain().then_inc(k.sem["pe"], 1); k.cnt["pe"] += 1
            V.wait_ge(k.sem["pe"], k.cnt["pe"])
            V.tensor_copy(o_fm[:, hh_ * TOK + qt * 128: hh_ * TOK + (qt + 1) * 128],
                          PB[6][:, 0:128])
            V.drain().then_inc(k.sem["ve"], 1); k.cnt["ve"] += 1
            ve_o[u] = k.cnt["ve"]

        emit_smms(0)
        for u in range(len(units_a)):
            emit_soft(u)
            if u + 1 < len(units_a):
                emit_smms(u + 1)
            emit_pv(u)
        k.bar()
        tap("o_attnb", o_fm[:, :])
        for qtr in range(4):
            k.dg(wob_slab.ap().rearrange("p (c o) -> p c o", o=512)[:, 0:NDIN, :],
                 Wo_b.ap().rearrange("(c p) (b o) -> p c b o", p=128, o=512)[:, :, qtr, :])
            k.bar()
            def _wbmm(tc, par):
                for hc in range(HB):
                    P.matmul(PB[par], o_fm[:, hc * TOK + tc * 128: hc * TOK + (tc + 1) * 128],
                             wob_slab[:, hc * 512:(hc + 1) * 512],
                             start=(hc == 0), stop=(hc == HB - 1))
            def _wbev(tc, par):
                V.tensor_tensor(resid[:, tc * D + qtr * 512: tc * D + qtr * 512 + 512],
                                resid[:, tc * D + qtr * 512: tc * D + qtr * 512 + 512],
                                PB[par], ALU.add)
            k.pipe2(list(range(NTOKC)), _wbmm, _wbev)
            k.bar()
        sb.pop()
        sb.pop()
        tap("x3", resid[:, :])

    if upto >= 5:
        emit_ffn("f2")

    k.bar()
    k.dh(out_d.ap().rearrange("(c p) d -> p c d", p=128),
         resid.ap().rearrange("p (c d) -> p c d", d=D))
    k.wait_dma(nc.sync)
    return nc, dbg


def stage_inputs(inputs):
    f = lambda a: np.ascontiguousarray(a, dtype=np.float32)
    x = np.asarray(inputs["L0_attn_a_x"])
    Wq_a = f(np.asarray(inputs["L0_attn_a_Wq"]).reshape(D, HG, 2, DK // 2).transpose(1, 2, 0, 3))
    Wk_a = f(np.asarray(inputs["L0_attn_a_Wk"]).reshape(D, HG, 2, DK // 2).transpose(1, 2, 0, 3))
    Wv_a = f(np.asarray(inputs["L0_attn_a_Wv"]).reshape(D, HG, DV).transpose(1, 0, 2))
    Wnorm_rep = f(np.broadcast_to(np.asarray(inputs["L0_attn_a_Wnorm"])[None, :], (128, DV)))
    tri = (np.arange(CH)[:, None] <= np.arange(CH)[None, :]).astype(np.float32)
    tri2 = f(np.concatenate([tri, tri], axis=0))
    tri_causal = np.where(np.arange(128)[None, :] <= np.arange(128)[:, None], 0.0, NEG).astype(np.float32)
    zeros128 = np.zeros((128, 128), np.float32)
    negs128 = np.full((128, 128), NEG, np.float32)
    ident = np.eye(128, dtype=np.float32)
    inv = 1.0 / 10000.0 ** (np.arange(0, DH, 2, dtype=np.float64) / DH)
    posf = np.arange(T, dtype=np.float64)[:, None] * inv[None, :]
    cos_full = np.concatenate([np.cos(posf)] * 2, axis=-1).astype(np.float32)
    sin_full = np.concatenate([-np.sin(posf), np.sin(posf)], axis=-1).astype(np.float32)
    wg = np.asarray(inputs["L0_ffn_w_gate"]); wu = np.asarray(inputs["L0_ffn_w_up"])
    wdn = np.asarray(inputs["L0_ffn_w_down"])

    in_maps = []
    for core in range(8):
        b, hh = core // 2, core % 2
        m = {
            "x_own": f(x[b, hh * TOK:(hh + 1) * TOK, :]),
            "Wq_a": Wq_a, "Wk_a": Wk_a, "Wv_a": Wv_a,
            "Wg_a": f(np.asarray(inputs["L0_attn_a_Wg"]).reshape(D, 4, VD // 4).transpose(1, 0, 2)),
            "Wgk1": f(inputs["L0_attn_a_Wgk1"]), "Wgk2": f(inputs["L0_attn_a_Wgk2"]),
            "bgk2": f(np.asarray(inputs["L0_attn_a_bgk2"]).reshape(HG * NDKC, 128).T),
            "Wnorm_rep": Wnorm_rep,
            "Wo_a": f(np.asarray(inputs["L0_attn_a_Wo"]).reshape(VD, 4, D // 4).transpose(1, 0, 2)),
            "Wq_b": f(inputs["L0_attn_b_Wq"]), "Wk_b": f(inputs["L0_attn_b_Wk"]),
            "Wv_b": f(inputs["L0_attn_b_Wv"]), "Wo_b": f(inputs["L0_attn_b_Wo"]),
            "wg_sl": f(wg[core * (FI // 8):(core + 1) * (FI // 8), :]),
            "wu_sl": f(wu[core * (FI // 8):(core + 1) * (FI // 8), :]),
            "wd_sl": f(wdn[core * (D // 8):(core + 1) * (D // 8), :]),
            "cos_t": f(cos_full[hh * TOK:(hh + 1) * TOK]),
            "sin_t": f(sin_full[hh * TOK:(hh + 1) * TOK]),
            "gla_gate": np.full((128, 1), float(hh), np.float32),
            "attnb_bias": np.full((128, 1), 0.0 if hh else NEG, np.float32),
            "tri_gla": tri2, "ident_c": ident,
            "diag_mask": tri_causal if hh == 0 else zeros128,
            "tri_r": negs128 if hh == 0 else tri_causal,
            "attnb_cols": (np.array([[NEG, NEG, 0.0, 0.0]], np.float32) if hh == 0
                           else np.array([[0.0, 0.0, 0.0, 0.0]], np.float32)
                           ).repeat(128, axis=0),
        }
        in_maps.append(m)
    return in_maps


def kernel(**inputs):
    nc, _ = build()
    in_maps = stage_inputs(inputs)
    res = run_bass_kernel_spmd(nc, in_maps, core_ids=list(range(8)))
    outs = np.empty((B, T, D), np.float32)
    for core in range(8):
        b, hh = core // 2, core % 2
        outs[b, hh * TOK:(hh + 1) * TOK, :] = res.results[core]["out"]
    return outs


if __name__ == "__main__":
    nc, dbg = build()
    print("built", nc.n_instructions())
